# version 15
# baseline (speedup 1.0000x reference)
import math
import numpy as np
import concourse.bass as bass
import concourse.mybir as mybir
from concourse.bass_utils import run_bass_kernel_spmd
from concourse.alu_op_type import AluOpType as ALU

F32 = mybir.dt.float32
BF16 = mybir.dt.bfloat16
I32 = mybir.dt.int32
AF = mybir.ActivationFunctionType
AX = mybir.AxisListType

T = 2048
D = 1024
NI = 2816
NEG = -1.0e30

_STAGE = 6
_ATT1_LIMIT = 99


ENGINES = ("sync", "tensor", "vector", "scalar", "gpsimd")


class Buf:
    __slots__ = ("name", "writer", "readers")

    def __init__(self, name):
        self.name = name
        self.writer = None
        self.readers = []


class Sched:

    def __init__(self):
        self.ops = {e: [] for e in ENGINES}
        self.edges = []
        self.gseq = []

    def op(self, eng, fn, reads=(), writes=(), dma=False, sem_only=False):
        idx = len(self.ops[eng])
        for b in reads:
            w = b.writer
            if w is not None and w != (eng, idx):
                self.edges.append((w[0], w[1], eng, idx))
            b.readers.append((eng, idx))
        for b in writes:
            w = b.writer
            if w is not None and w != (eng, idx):
                self.edges.append((w[0], w[1], eng, idx))
            for (re, ri) in b.readers:
                if (re, ri) != (eng, idx):
                    self.edges.append((re, ri, eng, idx))
            b.writer = (eng, idx)
            b.readers = []
        self.ops[eng].append([fn, dma, False, sem_only])
        self.gseq.append((eng, idx))
        return (eng, idx)

    def finalize_and_emit(self, nc):
        ops = self.ops
        incoming = {}
        for (se, si, de, di) in self.edges:
            incoming.setdefault((de, di), set()).add((se, si))
        for (de, di), srcs in incoming.items():
            for (se, si) in srcs:
                ops[se][si][2] = True
        for e in ENGINES:
            for o in ops[e]:
                if o[1]:
                    o[2] = True
        val_at = {}
        for e in ENGINES:
            c = 0
            va = []
            for o in ops[e]:
                if o[2]:
                    c += 16 if o[1] else 1
                va.append(c)
            val_at[e] = va
        K = {e: {e2: 0 for e2 in ENGINES} for e in ENGINES}
        val_issued = {e: 0 for e in ENGINES}
        self_done = {e: -1 for e in ENGINES}
        done_know = {}
        waits = {}
        drains = set()
        for (e, i) in self.gseq:
            srcs = incoming.get((e, i), ())
            sem_w = {}
            drain_srcs = []
            for (se, si) in srcs:
                v = val_at[se][si]
                if sem_w.get(se, 0) < v:
                    sem_w[se] = v
            ke = K[e]
            for se in sorted(sem_w):
                v = sem_w[se]
                if ke[se] >= v:
                    continue
                waits.setdefault((e, i), {})[se] = v
                dk = done_know.get((se, v))
                if dk:
                    for e2, v2 in dk.items():
                        if ke[e2] < v2:
                            ke[e2] = v2
                if ke[se] < v:
                    ke[se] = v
            if drain_srcs:
                need = False
                for j in drain_srcs:
                    if self_done[e] >= j:
                        continue
                    oj = ops[e][j]
                    if oj[2] and ke[e] >= val_at[e][j]:
                        continue
                    need = True
                    break
                if need:
                    drains.add((e, i))
                    self_done[e] = i - 1
                    if ke[e] < val_issued[e]:
                        ke[e] = val_issued[e]
            o = ops[e][i]
            if o[2]:
                val_issued[e] += 16 if o[1] else 1
                snap = dict(ke)
                snap[e] = val_issued[e]
                done_know[(e, val_issued[e])] = snap
        sems = {e: nc.alloc_semaphore(f"sem_{e}") for e in ENGINES}
        nops = sum(len(v) for v in ops.values())
        nincs = sum(1 for e in ENGINES for o in ops[e] if o[2])
        nwaits = sum(len(d) for d in waits.values())
        print(f"[kernel] ops={nops} incs={nincs} waits={nwaits} "
              f"drains={len(drains)} edges={len(self.edges)} "
              + " ".join(f"{e}:{len(ops[e])}" for e in ENGINES))
        sched = self

        def emit(eng_name, handle):
            for i, o in enumerate(sched.ops[eng_name]):
                w = waits.get((eng_name, i))
                if w:
                    for se in sorted(w):
                        handle.wait_ge(sems[se], w[se])
                if (eng_name, i) in drains:
                    handle.drain()
                inst = o[0](handle)
                if o[2]:
                    assert inst is not None, f"op {eng_name}:{i} needs inc"
                    inst.then_inc(sems[eng_name], 16 if o[1] else 1)

        with nc.Block() as block:
            @block.sync
            def _(h):
                emit("sync", h)

            @block.tensor
            def _(h):
                emit("tensor", h)

            @block.vector
            def _(h):
                emit("vector", h)

            @block.scalar
            def _(h):
                emit("scalar", h)

            @block.gpsimd
            def _(h):
                emit("gpsimd", h)


def _alibi_slopes(n):
    def pow2(m):
        start = 2 ** (-(2 ** (-(math.log2(m) - 3))))
        return [start * start ** i for i in range(m)]
    if math.log2(n).is_integer():
        return np.array(pow2(n), dtype=np.float32)
    c = 2 ** int(math.floor(math.log2(n)))
    s = pow2(c) + _alibi_slopes(2 * c).tolist()[: n - c]
    return np.array(s, dtype=np.float32)


def _rel_bucket(relpos):
    num_buckets = 16
    n = -relpos
    ret = (n < 0).astype(np.int64) * num_buckets
    n = np.abs(n)
    max_exact = num_buckets // 2
    is_small = n < max_exact
    val_if_large = max_exact + (
        np.log(np.maximum(n, 1).astype(np.float32) / max_exact)
        / math.log(128 / max_exact) * (num_buckets - max_exact)
    ).astype(np.int64)
    val_if_large = np.minimum(val_if_large, num_buckets - 1)
    return ret + np.where(is_small, n, val_if_large)


def host_constants():
    c = {}
    c["c_ident"] = np.eye(128, dtype=np.float32)
    c["c_ones1"] = np.ones((1, 128), dtype=np.float32)
    c["c_allones"] = np.ones((128, 128), dtype=np.float32)
    c["c_negones"] = -np.ones((128, 128), dtype=np.float32)
    ii = np.arange(128)
    c["c_tri"] = (ii[:, None] <= ii[None, :]).astype(np.float32)
    c["c_maskst"] = np.where(ii[:, None] <= ii[None, :], 0.0, NEG).astype(np.float32)
    c["c_rev"] = np.eye(128, dtype=np.float32)[::-1].copy()
    j = np.arange(4096)
    rel = np.minimum(j, 4094) - 2047
    buck = _rel_bucket(-rel)
    oh = np.zeros((32, 4096), np.float32)
    oh[buck, j] = 8.0
    c["c_onehot"] = oh
    sel15 = np.zeros((32, 128), np.float32)
    sel15[15, :] = 1.0
    sel31 = np.zeros((32, 128), np.float32)
    sel31[31, :] = 1.0
    c["c_sel15"] = sel15
    c["c_sel31"] = sel31
    r = np.exp(-_alibi_slopes(8))
    c["c_rrep"] = np.tile(np.repeat(r, 2)[None, :], (128, 1)).astype(np.float32)
    return c


CONST_SHAPES = {"c_ident": [128, 128], "c_ones1": [1, 128],
                "c_allones": [128, 128], "c_negones": [128, 128],
                "c_tri": [128, 128], "c_maskst": [128, 128], "c_rev": [128, 128],
                "c_sel15": [32, 128], "c_sel31": [32, 128], "c_rrep": [128, 16]}


class K:
    def __init__(self, nc, stage):
        self.nc = nc
        self.s = Sched()
        self.stage = stage
        self.debug_fn = None
        self._dumps = []

    def dump(self, name, src_ap, bufs, shape):
        t = self.nc.dram_tensor("dbg_" + name, list(shape), src_ap.dtype,
                                kind="ExternalOutput")
        b = Buf("dbg_" + name)
        self.dma("sync", t.ap(), src_ap, list(bufs), [b])
        self._dumps.append(b)

    def sbuf(self, name, shape, dtype=F32, nbufs=1):
        h = self.nc.alloc_sbuf_tensor(name, shape, dtype)
        bufs = [Buf(f"{name}.{i}") for i in range(nbufs)]
        return h, (bufs[0] if nbufs == 1 else bufs)

    def dram_scratch(self, name, shape, dtype=F32):
        h = self.nc.dram_tensor(name, shape, dtype)
        return h, Buf(name)

    def aview(self, off, n):
        ap = self.A_h[:, off:off + n]
        lo, hi = off // 512, (off + n - 1) // 512
        return ap, self.A_bufs[lo:hi + 1]

    def dma(self, eng, dst_ap, src_ap, reads, writes):
        self.s.op(eng, lambda h, d=dst_ap, s_=src_ap: h.dma_start(out=d, in_=s_),
                  reads=list(reads), writes=list(writes), dma=True)

    def mm(self, bank, out_ap, pairs, reads, start=True, stop=True):
        pb = self.psum_b[bank]

        def fn(h, pairs=pairs, out=out_ap, start=start, stop=stop):
            last = None
            n = len(pairs)
            for i, (l, r) in enumerate(pairs):
                last = h.matmul(out, l, r, start=(start and i == 0),
                                stop=(stop and i == n - 1),
                                skip_group_check=True)
            return last
        rd = list(reads)
        if not start:
            rd = rd + [pb]
        self.s.op("tensor", fn, reads=rd, writes=[pb])

    def tr(self, bank, out_ap, in_ap, reads):
        pb = self.psum_b[bank]
        self.s.op("tensor",
                  lambda h, o=out_ap, i=in_ap, d=self.ident_ap: h.transpose(o, i, d),
                  reads=list(reads) + [self.ident_buf], writes=[pb])

    def v(self, fn, reads=(), writes=()):
        self.s.op("vector", fn, reads=list(reads), writes=list(writes))

    def _e(self, eng, fn, reads=(), writes=()):
        self.s.op(eng, fn, reads=list(reads), writes=list(writes))

    def heat(self, n, reads, bank=0):
        for _ in range(n):
            self.mm(bank, self.pap(bank, 128, 512),
                    [(self.c_ident[:, :], self.heat_h[:, :])],
                    [self.c_ident_b, self.heat_b] + list(reads))

    def sc(self, fn, reads=(), writes=()):
        self.s.op("scalar", fn, reads=list(reads), writes=list(writes))

    def gp(self, fn, reads=(), writes=(), sem_only=False):
        self.s.op("gpsimd", fn, reads=list(reads), writes=list(writes),
                  sem_only=sem_only)

    def pb(self, bank):
        return self.psum_b[bank]

    def pap(self, bank, p=128, w=512):
        return self.psum_h[bank][0:p, 0:w]

    def build(self, in_desc):
        nc = self.nc
        ins = {}
        self.in_bufs = {}
        for name, shape, dtype in in_desc:
            ins[name] = nc.dram_tensor(name, list(shape), dtype, kind="ExternalInput")
            self.in_bufs[name] = Buf("in_" + name)
        self.ins = ins
        self.out_t = nc.dram_tensor("out", [T, D], F32, kind="ExternalOutput")
        self.out_buf = Buf("out")

        self.psum_h = []
        self.psum_b = []
        for i in range(8):
            h = nc.alloc_psum_tensor(f"pbank{i}", [128, 512], F32)
            self.psum_h.append(h)
            self.psum_b.append(Buf(f"pbank{i}"))

        AW = 19456
        self.A_h, _ = self.sbuf("arena", [128, AW], F32)
        self.A_words = AW
        self.A_bufs = [Buf(f"A.{i}") for i in range(AW // 512)]

        for name, shp in CONST_SHAPES.items():
            h, b = self.sbuf(name + "_s", shp, F32)
            setattr(self, name, h)
            setattr(self, name + "_b", b)
            self.dma("sync", h[:, :], ins[name].ap(), [self.in_bufs[name]], [b])
        self.ident_ap = self.c_ident[:, :]
        self.ident_buf = self.c_ident_b

        self.heat_h, hb = self.sbuf("heat", [128, 512], F32)
        self.heat_b = hb
        self.v(lambda h, a=self.heat_h[:, :]: h.memset(a, 1.0), [], [hb])

        self.xh, self.xb = self.sbuf("x", [128, 16 * 1024], F32, nbufs=16)
        for i in range(16):
            self.dma("sync" if i % 2 == 0 else "gpsimd",
                     self.xh[:, i * 1024:(i + 1) * 1024],
                     ins["x_in"].ap()[i * 128:(i + 1) * 128, :],
                     [self.in_bufs["x_in"]], [self.xb[i]])

        self.xhat_h, self.xhat_b = self.sbuf("xhat", [128, 4 * 1024], F32, nbufs=4)
        self.stat_h, self.stat_b = self.sbuf("stat", [128, 64], F32, nbufs=8)
        self.fsc_h, self.fsc_b = self.sbuf("fsc", [128, 1536], F32, nbufs=2)
        self.wp_h, self.wp_b = self.sbuf("wpool", [128, 8 * 512], F32, nbufs=8)
        self.sm_h, self.sm_b = self.sbuf("small", [128, 16 * 32], F32, nbufs=16)

        self.wgqT_d, self.wgqT_db = self.dram_scratch("wgqT_d", [11, 128, 2048])
        self.wuqT_d, self.wuqT_db = self.dram_scratch("wuqT_d", [11, 128, 2048])
        self.wdqT_d, self.wdqT_db = self.dram_scratch("wdqT_d", [44, 128, 512])
        self.iwgT_d, self.iwgT_db = self.dram_scratch("iwgT_d", [22, 128, 1024], BF16)
        self.iwuT_d, self.iwuT_db = self.dram_scratch("iwuT_d", [22, 128, 1024], BF16)
        self.iwdT_d, self.iwdT_db = self.dram_scratch("iwdT_d", [8, 128, 2816], BF16)
        self.hsp_d, self.hsp_db = self.dram_scratch("hsp_d", [NI, T])
        self.ysp_d, self.ysp_db = self.dram_scratch("ysp_d", [D, T])
        self.xTsp_d, self.xTsp_db = self.dram_scratch("xTsp_d", [D, T])
        self.qsp_d, self.qsp_db = self.dram_scratch("qsp_d", [D, T])
        self.ksp_d, self.ksp_db = self.dram_scratch("ksp_d", [D, T])
        self.vsp_d, self.vsp_db = self.dram_scratch("vsp_d", [T, D])
        self.wr_d, self.wr_db = self.dram_scratch("wr_d", [16, 4096])
        self.cc_in_d, self.cc_in_db = self.dram_scratch("cc_in", [1, 16])
        self.cc_out_d, self.cc_out_db = self.dram_scratch("cc_out", [1, 16])

        self.l0w_done = False
        self.l1w_done = False
        self.att1_prep_done = False

        if self.debug_fn is not None:
            self.debug_fn(self)
        if self.stage >= 1:
            self.ffn0()
        if self.stage >= 2:
            self.att0()
        if self.stage >= 3:
            self.ffn0()
        if self.stage >= 4:
            self.ffn1()
        if self.stage >= 5:
            self.att1()
        if self.stage >= 6:
            self.ffn1()

        for i in range(16):
            self.s.op("sync", (lambda h, i=i: h.dma_start(
                out=self.out_t.ap()[i * 128:(i + 1) * 128, :],
                in_=self.xh[:, i * 1024:(i + 1) * 1024])),
                reads=[self.xb[i]], writes=[self.out_buf], dma=True)
        self.s.op("sync", lambda h: None, reads=[self.out_buf] + self._dumps)
        self.s.finalize_and_emit(nc)

    def ln_tile(self, ti, dst_ap, dst_bufs, slot, sq_ap, sq_bufs):
        x_ap = self.xh[:, ti * 1024:(ti + 1) * 1024]
        xbuf = self.xb[ti]
        self.ln_generic(x_ap, [xbuf], dst_ap, dst_bufs, slot, sq_ap, sq_bufs)

    def ln_generic(self, x_ap, x_bufs, dst_ap, dst_bufs, slot, sq_ap, sq_bufs):
        st = self.stat_h
        stb = self.stat_b[slot % 8]
        o = 8 * (slot % 8)
        self.v(lambda h, x=x_ap, st=st, o=o: h.tensor_reduce(
            st[:, o:o + 1].unsqueeze(-1), x.rearrange("p (a b) -> p a b", b=1024),
            AX.X, ALU.add), list(x_bufs), [stb])
        self.v(lambda h, st=st, o=o: h.memset(st[:, o + 1:o + 2], 0.0),
               [], [stb])
        self.sc(lambda h, x=x_ap, sa=sq_ap, st=st, o=o: h.activation(
            sa, x, AF.Square, accum_out=st[:, o + 1:o + 2]),
            list(x_bufs), list(sq_bufs) + [stb])
        self.v(lambda h, st=st, o=o: h.tensor_scalar(
            st[:, o + 2:o + 3], st[:, o:o + 1], 1.0 / 1024.0, None, ALU.mult),
            [stb], [stb])
        self.v(lambda h, st=st, o=o: h.tensor_tensor(
            st[:, o + 5:o + 6], st[:, o + 2:o + 3], st[:, o + 2:o + 3], ALU.mult),
            [stb], [stb])
        self.v(lambda h, st=st, o=o: h.scalar_tensor_tensor(
            st[:, o + 4:o + 5], st[:, o + 1:o + 2], 1.0 / 1024.0,
            st[:, o + 5:o + 6], ALU.mult, ALU.subtract), [stb], [stb])
        self.v(lambda h, st=st, o=o: h.tensor_scalar(
            st[:, o + 4:o + 5], st[:, o + 4:o + 5], 1e-5, None, ALU.add),
            [stb], [stb])
        self.sc(lambda h, st=st, o=o: h.activation(
            st[:, o + 5:o + 6], st[:, o + 4:o + 5], AF.Sqrt),
            [stb], [stb])
        self.v(lambda h, st=st, o=o: h.reciprocal(
            st[:, o + 3:o + 4], st[:, o + 5:o + 6]), [stb], [stb])
        self.v(lambda h, x=x_ap, d=dst_ap, st=st, o=o: h.tensor_scalar(
            d, x, st[:, o + 2:o + 3], st[:, o + 3:o + 4], ALU.subtract, ALU.mult),
            list(x_bufs) + [stb], list(dst_bufs))

    def fp4(self, src_ap, src_bufs, dst_ap, dst_bufs, W, fslot,
            t1_ap, t1_bufs, t2_ap, t2_bufs, eng="vector"):
        nb = W // 16
        fs = self.fsc_h
        fb = self.fsc_b[fslot % 2]
        fo = 768 * (fslot % 2)
        assert 2 * nb <= 768
        src3 = src_ap.rearrange("p (a b) -> p a b", b=16)
        self.v(lambda h, s3=src3, d=fs[:, fo:fo + nb]: h.tensor_reduce(
            d.unsqueeze(-1), s3, AX.X, ALU.max, apply_absolute_value=True),
            list(src_bufs), [fb])
        self.v(lambda h, d=fs[:, fo:fo + nb]: h.tensor_scalar(
            d, d, 1.0 / 6.0, 1e-12, ALU.mult, ALU.max), [fb], [fb])
        self.v(lambda h, d=fs[:, fo + nb:fo + 2 * nb], s_=fs[:, fo:fo + nb]:
               h.reciprocal(d, s_), [fb], [fb])
        self.v(lambda h, d=fs[:, fo:fo + nb]: h.tensor_scalar(
            d, d, 0.5, None, ALU.mult), [fb], [fb])
        s_sc = fs[:, fo:fo + nb]
        s_rc = fs[:, fo + nb:fo + 2 * nb]
        scl_b = bass.AP(s_sc.tensor, s_sc.offset, [list(s_sc.ap[0]), [1, nb], [0, 16]])
        rec_b = bass.AP(s_rc.tensor, s_rc.offset, [list(s_rc.ap[0]), [1, nb], [0, 16]])
        t13 = t1_ap.rearrange("p (a b) -> p a b", b=16)
        self._e(eng, lambda h, d=t13, s3=src3, r=rec_b: h.tensor_tensor(d, s3, r, ALU.mult),
               list(src_bufs) + [fb], list(t1_bufs))
        self.sc(lambda h, d=t2_ap, s_=t1_ap: h.activation(d, s_, AF.Abs),
                list(t1_bufs), list(t2_bufs))
        self._e(eng, lambda h, d=dst_ap, a=t2_ap: h.tensor_scalar(
            d, a, 5.0, 2.0, ALU.is_gt, ALU.mult), list(t2_bufs), list(dst_bufs))
        for m in (2.5, 3.5):
            self._e(eng, lambda h, d=dst_ap, a=t2_ap, m=m: h.scalar_tensor_tensor(
                d, a, m, dst_ap, ALU.is_gt, ALU.add),
                list(t2_bufs) + list(dst_bufs), list(dst_bufs))
        self._e(eng, lambda h, d=dst_ap: h.tensor_scalar(
            d, d, 2.0, None, ALU.mult), list(dst_bufs), list(dst_bufs))
        for m in (0.25, 0.75, 1.25, 1.75):
            self._e(eng, lambda h, d=dst_ap, a=t2_ap, m=m: h.scalar_tensor_tensor(
                d, a, m, dst_ap, ALU.is_gt, ALU.add),
                list(t2_bufs) + list(dst_bufs), list(dst_bufs))
        self.sc(lambda h, d=t1_ap: h.activation(d, d, AF.Sign),
                list(t1_bufs), list(t1_bufs))
        self._e(eng, lambda h, d=dst_ap, s_=t1_ap: h.tensor_tensor(d, d, s_, ALU.mult),
               list(dst_bufs) + list(t1_bufs), list(dst_bufs))
        d3 = dst_ap.rearrange("p (a b) -> p a b", b=16)
        self._e(eng, lambda h, d=d3, s_=scl_b: h.tensor_tensor(d, d, s_, ALU.mult),
               list(dst_bufs) + [fb], list(dst_bufs))

    def ln_group(self, tis, dsts, sq_ap, sq_bufs):
        g0 = tis[0]
        n = len(tis)
        assert list(tis) == list(range(g0, g0 + n)) and n <= 4
        st = self.stat_h
        stb = self.stat_b[0]
        x4 = self.xh[:, g0 * 1024:(g0 + n) * 1024]
        xbufs = [self.xb[t] for t in tis]
        self.v(lambda h, x=x4, st=st, n=n: h.tensor_reduce(
            st[:, 0:n].unsqueeze(-1), x.rearrange("p (a b) -> p a b", b=1024),
            AX.X, ALU.add), xbufs, [stb])
        self.v(lambda h, st=st, n=n: h.memset(st[:, 4:4 + n], 0.0), [], [stb])
        for i, t in enumerate(tis):
            self.sc(lambda h, x=self.xh[:, t * 1024:(t + 1) * 1024], sa=sq_ap,
                    st=st, i=i: h.activation(sa, x, AF.Square,
                                             accum_out=st[:, 4 + i:5 + i]),
                    [self.xb[t]], list(sq_bufs) + [stb])
        self.v(lambda h, st=st, n=n: h.tensor_scalar(
            st[:, 8:8 + n], st[:, 0:n], 1.0 / 1024.0, None, ALU.mult), [stb], [stb])
        self.v(lambda h, st=st, n=n: h.tensor_tensor(
            st[:, 20:20 + n], st[:, 8:8 + n], st[:, 8:8 + n], ALU.mult), [stb], [stb])
        self.v(lambda h, st=st, n=n: h.scalar_tensor_tensor(
            st[:, 16:16 + n], st[:, 4:4 + n], 1.0 / 1024.0, st[:, 20:20 + n],
            ALU.mult, ALU.subtract), [stb], [stb])
        self.v(lambda h, st=st, n=n: h.tensor_scalar(
            st[:, 16:16 + n], st[:, 16:16 + n], 1e-5, None, ALU.add), [stb], [stb])
        self.sc(lambda h, st=st, n=n: h.activation(
            st[:, 20:20 + n], st[:, 16:16 + n], AF.Sqrt), [stb], [stb])
        self.v(lambda h, st=st, n=n: h.reciprocal(
            st[:, 12:12 + n], st[:, 20:20 + n]), [stb], [stb])
        for i, t in enumerate(tis):
            d_ap, d_bufs = dsts[i]
            self.v(lambda h, x=self.xh[:, t * 1024:(t + 1) * 1024], d=d_ap,
                   st=st, i=i: h.tensor_scalar(
                d, x, st[:, 8 + i:9 + i], st[:, 12 + i:13 + i],
                ALU.subtract, ALU.mult), [self.xb[t], stb], list(d_bufs))

    def ffn0_weights(self):
        A = self.aview
        SRCR = [A(0, 4096), A(4096, 4096)]
        T1, T1b = A(8192, 4096)
        T2, T2b = A(12288, 4096)
        STG, STGb = A(16384, 2048)
        fs_i = 0
        sslot = 0
        for wname, wdst, wdb, nt_rows, wcols, bat in (
                ("wg0_nat", self.wgqT_d, self.wgqT_db, 22, 1024, 4),
                ("wu0_nat", self.wuqT_d, self.wuqT_db, 22, 1024, 4),
                ("wd0_nat", self.wdqT_d, self.wdqT_db, 8, 2816, 1)):
            src_t = self.ins[wname]
            for nt0 in range(0, nt_rows, bat):
                nb_ = min(bat, nt_rows - nt0)
                W = wcols * nb_
                SRC, SRCb = SRCR[sslot]
                sslot ^= 1
                if nb_ == 1:
                    self.dma("sync", SRC[:, 0:W],
                             src_t.ap()[nt0 * 128:(nt0 + 1) * 128, :],
                             [self.in_bufs[wname]], SRCb)
                else:
                    sap = src_t.ap()[nt0 * 128:(nt0 + nb_) * 128, :].rearrange(
                        "(a b) c -> b a c", a=nb_)
                    dap = SRC[:, 0:W].rearrange("p (a c) -> p a c", c=wcols)
                    self.dma("sync", dap, sap, [self.in_bufs[wname]], SRCb)
                self.fp4(SRC[:, 0:W], SRCb, SRC[:, 0:W], SRCb, W, fs_i,
                         T1[:, 0:W], T1b, T2[:, 0:W], T2b,
                         eng="vector")
                fs_i += 1
                for bi in range(nb_):
                    nt = nt0 + bi
                    nck = wcols // 128
                    for ck in range(nck):
                        bank = 2 + (ck % 2)
                        self.tr(bank, self.pap(bank, 128, 128),
                                SRC[:, bi * wcols + ck * 128:
                                    bi * wcols + ck * 128 + 128], SRCb)
                        if wcols == 1024:
                            sg = STG[:, ck * 256 + (nt % 2) * 128:
                                     ck * 256 + (nt % 2) * 128 + 128]
                        else:
                            sg = STG[:, (ck % 4) * 128:(ck % 4) * 128 + 128]
                        self.sc(lambda h, d=sg, p=self.pap(bank, 128, 128):
                               h.copy(d, p), [self.pb(bank)], STGb)
                        if wcols != 1024:
                            row = (nt // 4) * 22 + ck
                            col = (nt % 4) * 128
                            self.dma("gpsimd",
                                     wdst.ap()[row:row + 1, :, :].squeeze(0)
                                     [:, col:col + 128],
                                     sg, STGb, [wdb])
                    if wcols == 1024 and nt % 2 == 1:
                        self.dma("gpsimd",
                                 wdst.ap()[nt // 2:nt // 2 + 1, :, :].squeeze(0),
                                 STG[:, 0:2048], STGb, [wdb])

    def ffn0(self):
        if not self.l0w_done:
            self.ffn0_weights()
            self.l0w_done = True
        A = self.aview
        XQT, XQTb = A(0, 4096)
        HQT, HQTb = A(4096, 11264)
        SG = [A(15360 + i * 512, 512) for i in range(4)]
        SGall, SGallb = A(15360, 2048)
        HS1, HS1b = A(17408, 1024)
        HS2, HS2b = A(18432, 1024)
        T1, T1b = A(0, 4096)
        T2, T2b = A(4096, 4096)
        fs_i = 0

        for tc in range(4):
            tis = list(range(tc * 4, tc * 4 + 4))
            dsts = [(self.xhat_h[:, i * 1024:(i + 1) * 1024], [self.xhat_b[i]])
                    for i in range(4)]
            self.ln_group(tis, dsts, T2[:, 0:1024], T2b)
            xall = self.xhat_h[:, 0:4096]
            xallb = list(self.xhat_b)
            self.fp4(xall, xallb, xall, xallb, 4096, fs_i, T1, T1b, T2, T2b)
            fs_i += 1
            for tt in range(4):
                for hc in range(8):
                    bank = 2 + (hc % 2)
                    self.tr(bank, self.pap(bank, 128, 128),
                            self.xhat_h[:, tt * 1024 + hc * 128:
                                        tt * 1024 + hc * 128 + 128],
                            [self.xhat_b[tt]])
                    dcol = hc * 512 + tt * 128
                    self.sc(lambda h, d=XQT[:, dcol:dcol + 128],
                           p=self.pap(bank, 128, 128): h.copy(d, p),
                           [self.pb(bank)], [XQTb[dcol // 512]])
            lidx = 0
            for nt2 in range(11):
                off = (nt2 % 2) * 256
                for proj, wsrc, wsb in (("g", self.wgqT_d, self.wgqT_db),
                                        ("u", self.wuqT_d, self.wuqT_db)):
                    par = lidx % 2
                    lidx += 1
                    base = par * 2048
                    for s in range(4):
                        sl = par * 4 + s
                        self.dma("sync", self.wp_h[:, sl * 512:(sl + 1) * 512],
                                 wsrc.ap()[nt2:nt2 + 1, :,
                                           s * 512:(s + 1) * 512].squeeze(0),
                                 [wsb], [self.wp_b[sl]])
                    for tt in range(4):
                        pairs = [(XQT[:, hc * 512 + tt * 128:
                                      hc * 512 + tt * 128 + 128],
                                  self.wp_h[:, base + hc * 256:
                                            base + hc * 256 + 256])
                                 for hc in range(8)]
                        self.mm(tt % 2, self.pap(tt % 2, 128, 256), pairs,
                                [XQTb[hc] for hc in range(8)] +
                                [self.wp_b[par * 4 + s] for s in range(4)])
                        sg_ap, sg_b = SG[tt]
                        if proj == "g":
                            self.sc(lambda h, d=sg_ap[:, off:off + 256],
                                    p=self.pap(tt % 2, 128, 256):
                                    h.activation(d, p, AF.Silu),
                                    [self.pb(tt % 2)], sg_b)
                        else:
                            self.v(lambda h, d=sg_ap[:, off:off + 256],
                                   p=self.pap(tt % 2, 128, 256):
                                   h.tensor_tensor(d, d, p, ALU.mult),
                                   list(sg_b) + [self.pb(tt % 2)], sg_b)
                if nt2 % 2 == 1:
                    nb = nt2 // 2
                    for pr in range(2):
                        pap_ = SGall[:, pr * 1024:(pr + 1) * 1024]
                        pbufs = list(SG[pr * 2][1]) + list(SG[pr * 2 + 1][1])
                        self.fp4(pap_, pbufs, pap_, pbufs, 1024, fs_i,
                                 HS1, HS1b, HS2, HS2b,
                                 eng="vector")
                        fs_i += 1
                    for tt in range(4):
                        sg_ap, sg_b = SG[tt]
                        for q in range(4):
                            bank = 2 + (q % 2)
                            self.tr(bank, self.pap(bank, 128, 128),
                                    sg_ap[:, q * 128:(q + 1) * 128], sg_b)
                            dcol = (nb * 4 + q) * 512 + tt * 128
                            self.sc(lambda h, d=HQT[:, dcol:dcol + 128],
                                   p=self.pap(bank, 128, 128): h.copy(d, p),
                                   [self.pb(bank)], [HQTb[dcol // 512]])
                elif nt2 == 10:
                    for tt in range(4):
                        sg_ap, sg_b = SG[tt]
                        self.fp4(sg_ap[:, 0:256], sg_b, sg_ap[:, 0:256], sg_b,
                                 256, fs_i, HS1[:, 0:256], HS1b,
                                 HS2[:, 0:256], HS2b)
                        fs_i += 1
                    for tt in range(4):
                        sg_ap, sg_b = SG[tt]
                        for q in range(2):
                            bank = 2 + (q % 2)
                            self.tr(bank, self.pap(bank, 128, 128),
                                    sg_ap[:, q * 128:(q + 1) * 128], sg_b)
                            dcol = (5 * 4 + q) * 512 + tt * 128
                            self.sc(lambda h, d=HQT[:, dcol:dcol + 128],
                                   p=self.pap(bank, 128, 128): h.copy(d, p),
                                   [self.pb(bank)], [HQTb[dcol // 512]])
            for fh in range(2):
                for ic in range(22):
                    slot = ic % 4
                    self.dma("sync", self.wp_h[:, slot * 512:(slot + 1) * 512],
                             self.wdqT_d.ap()[fh * 22 + ic:fh * 22 + ic + 1,
                                              :, :].squeeze(0),
                             [self.wdqT_db], [self.wp_b[slot]])
                    for tt in range(4):
                        lcol = ic * 512 + tt * 128
                        self.mm(4 + tt, self.pap(4 + tt),
                                [(HQT[:, lcol:lcol + 128],
                                  self.wp_h[:, slot * 512:(slot + 1) * 512])],
                                [HQTb[lcol // 512], self.wp_b[slot]],
                                start=(ic == 0), stop=(ic == 21))
                for tt in range(4):
                    ti = tc * 4 + tt
                    x_ap = self.xh[:, ti * 1024 + fh * 512: ti * 1024 + fh * 512 + 512]
                    self.v(lambda h, x=x_ap, p=self.pap(4 + tt):
                           h.scalar_tensor_tensor(x, p, 0.5, x, ALU.mult, ALU.add),
                           [self.pb(4 + tt), self.xb[ti]], [self.xb[ti]])

    def att0(self):
        A = self.aview
        XHT, XHTb = A(0, 2048)
        QT, QTb = A(2048, 2048)
        KT, KTb = A(4096, 2048)
        Y, Yb = A(6144, 1024)
        YT, YTb = A(7168, 1024)
        BMG, BMGb = A(8192, 512)
        LL, LLb = A(8704, 512)
        BOUT, BOUTb = A(9216, 1024)
        LNG, LNGb = A(10240, 1024)
        LNB, LNBb = A(11264, 1024)
        FS, FSb = A(12288, 1024)
        VO, VOb = A(13312, 1152)
        DN, DNb = A(14464, 1152)
        ST, STb = A(15616, 2064)
        self._att0_regs = (XHT, XHTb, QT, QTb, KT, KTb, Y, Yb, YT, YTb, BMG,
                           BMGb, LL, LLb, BOUT, BOUTb, LNG, LNGb, LNB, LNBb,
                           FS, FSb, VO, VOb, DN, DNb, ST, STb)

        for name, ap_, b_ in (("bout_rep", BOUT, BOUTb), ("lng_rep", LNG, LNGb),
                              ("lnb_rep", LNB, LNBb)):
            self.dma("sync", ap_, self.ins[name].ap(), [self.in_bufs[name]], b_)
        self.dma("sync", self.sm_h[:, 0:40], self.ins["bmg_bias"].ap(),
                 [self.in_bufs["bmg_bias"]], [self.sm_b[0], self.sm_b[1]])
        self.v(lambda h, a=ST: h.memset(a, 0.0), [], STb)
        VO2, VO2b = A(17680, 1032)
        self._att0_vo2 = (VO2, VO2b)
        for vo_, vob_ in ((VO, VOb), (VO2, VO2b)):
            vco = vo_[:, 0:8 * 129].rearrange("p (a b) -> p a b", b=129)[:, :, 128:129]
            self.v(lambda h, a=vco: h.memset(a, 1.0), [], vob_)

        for g in range(8):
            for hc in range(8):
                self.dma("sync", self.wp_h[:, 3584 + hc * 64: 3584 + hc * 64 + 40],
                         self.ins["wbmgT"].ap()[hc * 128:(hc + 1) * 128, :],
                         [self.in_bufs["wbmgT"]], [self.wp_b[7]])
            for cc in range(2):
                ti = g * 2 + cc
                xh_ap = self.xhat_h[:, cc * 1024:(cc + 1) * 1024]
                xh_buf = self.xhat_b[cc]
                self.ln_tile(ti, xh_ap, [xh_buf], cc, FS, FSb)
                for hc in range(8):
                    self.tr(1, self.pap(1, 128, 128),
                            xh_ap[:, hc * 128:(hc + 1) * 128], [xh_buf])
                    dcol = hc * 256 + cc * 128
                    self.sc(lambda h, d=XHT[:, dcol:dcol + 128],
                           p=self.pap(1, 128, 128): h.copy(d, p),
                           [self.pb(1)], [XHTb[dcol // 512]])
            lq = 0
            for wname, REG, REGb in (("wqT0", QT, QTb), ("wkT0", KT, KTb)):
                wt = self.ins[wname]
                wdb_ = self.in_bufs[wname]
                for ft in range(8):
                    par = lq % 2
                    lq += 1
                    base = par * 1024
                    pbufs = [self.wp_b[par * 2], self.wp_b[par * 2 + 1]]
                    for s2 in range(2):
                        sl = par * 2 + s2
                        self.dma("sync", self.wp_h[:, sl * 512:(sl + 1) * 512],
                                 wt.ap()[ft:ft + 1, :,
                                         s2 * 512:(s2 + 1) * 512].squeeze(0),
                                 [wdb_], [self.wp_b[sl]])
                    for hc in range(8):
                        self.mm(0, self.pap(0, 128, 256),
                                [(self.wp_h[:, base + hc * 128:
                                            base + hc * 128 + 128],
                                  XHT[:, hc * 256:hc * 256 + 256])],
                                pbufs + list(XHTb),
                                start=(hc == 0), stop=(hc == 7))
                    self.v(lambda h, d=FS[:, 0:256], p=self.pap(0, 128, 256):
                           h.tensor_scalar(d, p, 0.0, None, ALU.min),
                           [self.pb(0)], FSb)
                    self.sc(lambda h, d=FS[:, 0:256]: h.activation(d, d, AF.Exp),
                            FSb, FSb)
                    self.v(lambda h, d=FS[:, 256:512], p=self.pap(0, 128, 256):
                           h.tensor_scalar(d, p, 0.0, None, ALU.max),
                           [self.pb(0)], FSb)
                    self.v(lambda h, d=REG[:, ft * 256:(ft + 1) * 256],
                           a=FS[:, 0:256], b2=FS[:, 256:512]:
                           h.tensor_tensor(d, a, b2, ALU.add),
                           FSb, [REGb[(ft * 256) // 512]])
            for cc in range(2):
                pairs = [(XHT[:, hc * 256 + cc * 128: hc * 256 + cc * 128 + 128],
                          self.wp_h[:, 3584 + hc * 64: 3584 + hc * 64 + 40])
                         for hc in range(8)]
                self.mm(5, self.pap(5, 128, 40), pairs,
                        list(XHTb) + [self.wp_b[7]])
                bo = cc * 64
                lo = cc * 64
                self.v(lambda h, d=BMG[:, bo:bo + 40], p=self.pap(5, 128, 40),
                       bb=self.sm_h[:, 0:40]: h.tensor_tensor(d, p, bb, ALU.add),
                       [self.pb(5), self.sm_b[0], self.sm_b[1]], BMGb)
                self.sc(lambda h, d=LL[:, lo:lo + 16], s_=BMG[:, bo:bo + 16]:
                        h.activation(d, s_, AF.Sigmoid), BMGb, LLb)
                self.v(lambda h, d=LL[:, lo:lo + 16], r=self.c_rrep[:, 0:16]:
                       h.tensor_tensor(d, d, r, ALU.mult),
                       LLb + [self.c_rrep_b], LLb)
                self.v(lambda h, d=LL[:, lo:lo + 16]:
                       h.tensor_scalar(d, d, 0.9999, None, ALU.min), LLb, LLb)
                self.sc(lambda h, d=LL[:, lo:lo + 16]: h.activation(d, d, AF.Ln),
                        LLb, LLb)
                self.sc(lambda h, d=LL[:, lo + 32:lo + 40],
                        s_=BMG[:, bo + 32:bo + 40]:
                        h.activation(d, s_, AF.Sigmoid), BMGb, LLb)
                me = BMG[:, bo + 16:bo + 32:2]
                mo = BMG[:, bo + 17:bo + 32:2]
                self.v(lambda h, d=LL[:, lo + 56:lo + 64], a=me, b2=mo:
                       h.tensor_tensor(d, a, b2, ALU.subtract), BMGb, LLb)
                self.sc(lambda h, d=LL[:, lo + 40:lo + 55:2],
                        s_=LL[:, lo + 56:lo + 64]:
                        h.activation(d, s_, AF.Sigmoid), LLb, LLb)
                self.sc(lambda h, d=LL[:, lo + 41:lo + 56:2],
                        s_=LL[:, lo + 56:lo + 64]:
                        h.activation(d, s_, AF.Sigmoid, scale=-1.0), LLb, LLb)
            VO2, VO2b = self._att0_vo2
            for fh in range(2):
                for sl in range(8):
                    self.dma("gpsimd", self.wp_h[:, sl * 512:(sl + 1) * 512],
                             self.ins["wvT0"].ap()[fh:fh + 1, :,
                                                   sl * 512:(sl + 1) * 512].squeeze(0),
                             [self.in_bufs["wvT0"]], [self.wp_b[sl]])
                for cc in range(2):
                    bank = 4 + cc
                    vo_, vob_ = (VO, VOb) if cc == 0 else (VO2, VO2b)
                    for hc in range(8):
                        self.mm(bank, self.pap(bank),
                                [(XHT[:, hc * 256 + cc * 128:
                                       hc * 256 + cc * 128 + 128],
                                  self.wp_h[:, hc * 512:(hc + 1) * 512])],
                                list(XHTb) + [self.wp_b[hc]],
                                start=(hc == 0), stop=(hc == 7))
                    for hd in range(4):
                        hh = fh * 4 + hd
                        self.sc(lambda h, d=vo_[:, hh * 129:hh * 129 + 128],
                               p=self.psum_h[bank][0:128, hd * 128:(hd + 1) * 128]:
                               h.copy(d, p), [self.pb(bank)], vob_)
            for cc in range(2):
                self.att0_chunk(g * 2 + cc, cc)

    def att0_chunk(self, c, cc):
        (XHT, XHTb, QT, QTb, KT, KTb, Y, Yb, YT, YTb, BMG, BMGb, LL, LLb,
         BOUT, BOUTb, LNG, LNGb, LNB, LNBb, FS, FSb, VO, VOb, DN, DNb,
         ST, STb) = self._att0_regs
        lo = cc * 64
        llc = LL[:, lo:lo + 16]
        self.mm(5, self.pap(5, 128, 16), [(self.c_tri[:, :], llc)],
                [self.c_tri_b] + LLb)
        self.sc(lambda h, d=DN[:, 1024 + 16 * cc:1040 + 16 * cc],
                p=self.pap(5, 128, 16):
                h.copy(d, p), [self.pb(5)], DNb)
        self.att0_sc_ktr(cc, 0)
        for hd in range(8):
            if hd + 1 < 8:
                self.att0_sc_ktr(cc, hd + 1)
            self.att0_head(c, cc, hd)
        for hc in range(8):
            self.tr(6, self.pap(6, 128, 128), Y[:, hc * 128:(hc + 1) * 128], Yb)
            self.sc(lambda h, d=YT[:, hc * 128:(hc + 1) * 128],
                   p=self.pap(6, 128, 128): h.copy(d, p),
                   [self.pb(6)], [YTb[hc // 4]])
        for fh in range(2):
            for sl in range(8):
                self.dma("gpsimd", self.wp_h[:, sl * 512:(sl + 1) * 512],
                         self.ins["woutT0"].ap()[fh:fh + 1, :,
                                                 sl * 512:(sl + 1) * 512].squeeze(0),
                         [self.in_bufs["woutT0"]], [self.wp_b[sl]])
            for hc in range(8):
                self.mm(7, self.pap(7),
                        [(YT[:, hc * 128:(hc + 1) * 128],
                          self.wp_h[:, hc * 512:(hc + 1) * 512])],
                        list(YTb) + [self.wp_b[hc]],
                        start=(hc == 0), stop=(hc == 7))
            xh_ap = self.xhat_h[:, cc * 1024 + fh * 512: cc * 1024 + fh * 512 + 512]
            self.v(lambda h, d=FS[:, fh * 512:(fh + 1) * 512], p=self.pap(7),
                   b2=BOUT[:, fh * 512:(fh + 1) * 512]:
                   h.tensor_tensor(d, p, b2, ALU.add),
                   [self.pb(7)] + BOUTb, FSb)
            self.v(lambda h, d=FS[:, fh * 512:(fh + 1) * 512], a=xh_ap:
                   h.tensor_tensor(d, d, a, ALU.add),
                   FSb + [self.xhat_b[cc]], FSb)
        self.ln_generic(FS, FSb, FS, FSb, cc, Y, Yb)
        self.v(lambda h, d=FS, g2=LNG: h.tensor_tensor(d, d, g2, ALU.mult),
               FSb + LNGb, FSb)
        self.v(lambda h, d=FS, b2=LNB: h.tensor_tensor(d, d, b2, ALU.add),
               FSb + LNBb, FSb)
        x_ap = self.xh[:, c * 1024:(c + 1) * 1024]
        self.v(lambda h, x=x_ap, r=FS: h.tensor_tensor(x, x, r, ALU.add),
               FSb + [self.xb[c]], [self.xb[c]])

    def att0_sc_ktr(self, cc, hd):
        (XHT, XHTb, QT, QTb, KT, KTb, Y, Yb, YT, YTb, BMG, BMGb, LL, LLb,
         BOUT, BOUTb, LNG, LNGb, LNB, LNBb, FS, FSb, VO, VOb, DN, DNb,
         ST, STb) = self._att0_regs
        kc = KT[:, hd * 256 + cc * 128: hd * 256 + cc * 128 + 128]
        qc = QT[:, hd * 256 + cc * 128: hd * 256 + cc * 128 + 128]
        kb = [KTb[(hd * 256) // 512]]
        qb = [QTb[(hd * 256) // 512]]
        self.mm(hd % 2, self.pap(hd % 2, 128, 128), [(kc, qc)], kb + qb)
        self.tr(2 + hd % 2, self.pap(2 + hd % 2, 128, 128), kc, kb)

    def att0_head(self, c, cc, hd):
        (XHT, XHTb, QT, QTb, KT, KTb, Y, Yb, YT, YTb, BMG, BMGb, LL, LLb,
         BOUT, BOUTb, LNG, LNGb, LNB, LNBb, FS, FSb, VO, VOb, DN, DNb,
         ST, STb) = self._att0_regs
        lo = cc * 64
        qc = QT[:, hd * 256 + cc * 128: hd * 256 + cc * 128 + 128]
        qb = [QTb[(hd * 256) // 512]]
        if cc == 0:
            voh = VO[:, hd * 129:(hd + 1) * 129]
            vob = VOb
        else:
            VO2, VO2b = self._att0_vo2
            voh = VO2[:, hd * 129:(hd + 1) * 129]
            vob = VO2b
        gcol = LL[:, lo + 32 + hd:lo + 33 + hd]
        bSC = hd % 2
        bKT = 2 + hd % 2
        b4 = 4 + hd % 2
        b6 = 6 + hd % 2
        EA2 = DN[:, 0:256]
        MT2 = DN[:, 256:512]
        QP2 = DN[:, 512:768]
        KP2 = DN[:, 768:1024]
        MT = [DN[:, 256:384], DN[:, 384:512]]
        QP = [DN[:, 512:640], DN[:, 640:768]]
        KP = [DN[:, 768:896], DN[:, 896:1024]]
        acol2 = DN[:, 1024 + 16 * cc + 2 * hd:1026 + 16 * cc + 2 * hd]
        cs2 = DN[:, 1056:1058]
        dn2 = DN[:, 1058:1060]
        f2 = DN[:, 1060:1062]

        def j3(ap2, n=128):
            return ap2.rearrange("p (j c) -> p j c", c=n)

        def jbc(ap_, n=128):
            return bass.AP(ap_.tensor, ap_.offset,
                           [list(ap_.ap[0]), [0, 2], [1, n]])

        def ibc(ap_, n=128):
            return bass.AP(ap_.tensor, ap_.offset,
                           [list(ap_.ap[0]), [1, 2], [0, n]])

        llpair = LL[:, lo + 2 * hd:lo + 2 * hd + 2]
        self.v(lambda h, d=j3(KP2), t=jbc(self.c_tri[:, :]), l=ibc(llpair):
               h.tensor_tensor(d, t, l, ALU.mult),
               [self.c_tri_b] + LLb, DNb)
        for j in range(2):
            self.mm(b4, self.psum_h[b4][0:128, j * 128:(j + 1) * 128],
                    [(self.c_allones[:, :], KP[j])],
                    [self.c_allones_b] + DNb)
        for j in range(2):
            self.mm(b6, self.psum_h[b6][0:128, j * 128:(j + 1) * 128],
                    [(self.c_allones[:, :], KP[j]),
                     (KP[j], self.c_negones[:, :])],
                    [self.c_allones_b, self.c_negones_b] + DNb)
        self.sc(lambda h, d=EA2, p=self.psum_h[b4][0:128, 0:256]:
                h.activation(d, p, AF.Exp), [self.pb(b4)], DNb)
        self.v(lambda h, d=j3(MT2), p=j3(self.psum_h[b6][0:128, 0:256]),
               m=jbc(self.c_maskst[:, :]):
               h.tensor_tensor(d, p, m, ALU.add),
               [self.pb(b6), self.c_maskst_b], DNb)
        self.sc(lambda h, d=MT2: h.activation(d, d, AF.Exp), DNb, DNb)
        self.v(lambda h, d=MT2, g2=gcol:
               h.tensor_scalar(d, d, g2, None, ALU.mult), DNb + LLb, DNb)
        self.v(lambda h, d=j3(MT2), p=jbc(self.pap(bSC, 128, 128)):
               h.tensor_tensor(d, d, p, ALU.mult), DNb + [self.pb(bSC)], DNb)
        self.v(lambda h, d=j3(QP2), q=jbc(qc), e=j3(EA2):
               h.tensor_tensor(d, q, e, ALU.mult), qb + DNb, DNb)
        for j in range(2):
            sthj = ST[:, (2 * hd + j) * 129:(2 * hd + j + 1) * 129]
            self.mm(b4, self.psum_h[b4][0:128, j * 129:j * 129 + 129],
                    [(MT[j], voh), (QP[j], sthj)], DNb + vob + STb)
        self.sc(lambda h, d=cs2, a=acol2: h.activation(d, a, AF.Exp, scale=-1.0),
                DNb, DNb)
        EA0 = DN[:, 0:128]
        ea_last = bass.AP(EA0.tensor, EA0.offset + 127,
                          [list(EA0.ap[0]), [128, 2]])
        self.v(lambda h, d=cs2, e=ea_last: h.tensor_tensor(d, d, e, ALU.mult),
               DNb, DNb)
        self.v(lambda h, d=cs2, g2=gcol: h.tensor_scalar(d, d, g2, None, ALU.mult),
               DNb + LLb, DNb)
        self.v(lambda h, d=j3(KP2), p=jbc(self.pap(bKT, 128, 128)), cs=ibc(cs2):
               h.tensor_tensor(d, p, cs, ALU.mult),
               [self.pb(bKT)] + DNb, DNb)
        for j in range(2):
            self.mm(b6, self.psum_h[b6][0:128, j * 129:j * 129 + 129],
                    [(KP[j], voh)], DNb + vob)
        for j in range(2):
            sthj = ST[:, (2 * hd + j) * 129:(2 * hd + j + 1) * 129]
            self.v(lambda h, s_=sthj, el=ea_last[:, j:j + 1],
                   p=self.psum_h[b6][0:128, j * 129:j * 129 + 129]:
                   h.scalar_tensor_tensor(s_, s_, el, p, ALU.mult, ALU.add),
                   STb + DNb + [self.pb(b6)], STb)
        pd = self.psum_h[b4][0:128, 128:129]
        pd3 = bass.AP(pd.tensor, pd.offset, [list(pd.ap[0]), [129, 2], [1, 1]])
        self.v(lambda h, d=dn2.rearrange("p (j c) -> p j c", c=1), p=pd3:
               h.tensor_scalar(d, p, 1e-6, None, ALU.add), [self.pb(b4)], DNb)
        self.v(lambda h, d=f2, s_=dn2: h.reciprocal(d, s_), DNb, DNb)
        self.v(lambda h, d=f2, w=LL[:, lo + 40 + 2 * hd:lo + 42 + 2 * hd]:
               h.tensor_tensor(d, d, w, ALU.mult), DNb + LLb, DNb)
        yh = Y[:, hd * 128:(hd + 1) * 128]
        self.v(lambda h, d=yh, p=self.psum_h[b4][0:128, 0:128], f=f2[:, 0:1]:
               h.tensor_scalar(d, p, f, None, ALU.mult),
               [self.pb(b4)] + DNb, Yb)
        self.v(lambda h, d=yh, p=self.psum_h[b4][0:128, 129:257], f=f2[:, 1:2]:
               h.scalar_tensor_tensor(d, p, f, d, ALU.mult, ALU.add),
               [self.pb(b4)] + DNb + Yb, Yb)

    def ffn1_weights(self):
        A = self.aview
        SRC, SRCb = A(0, 4096)
        T1, T1b = A(4096, 4096)
        DSTI, DSTIb = A(8192, 4096)
        STGS = [A(12288, 1408), A(13696, 1408)]
        for wname, wdst, wdb, nt_rows, wcols, smo, bat in (
                ("wg1_nat", self.iwgT_d, self.iwgT_db, 22, 1024, 64, 4),
                ("wu1_nat", self.iwuT_d, self.iwuT_db, 22, 1024, 96, 4),
                ("wd1_nat", self.iwdT_d, self.iwdT_db, 8, 2816, 128, 1)):
            src_t = self.ins[wname]
            smb = self.sm_b[smo // 32]
            for nt0 in range(0, nt_rows, bat):
                nb_ = min(bat, nt_rows - nt0)
                W = wcols * nb_
                if nb_ == 1:
                    self.dma("sync", SRC[:, 0:W],
                             src_t.ap()[nt0 * 128:(nt0 + 1) * 128, :],
                             [self.in_bufs[wname]], SRCb)
                else:
                    sap = src_t.ap()[nt0 * 128:(nt0 + nb_) * 128, :].rearrange(
                        "(a b) c -> b a c", a=nb_)
                    dap = SRC[:, 0:W].rearrange("p (a c) -> p a c", c=wcols)
                    self.dma("sync", dap, sap, [self.in_bufs[wname]], SRCb)
                sma = self.sm_h[:, smo + nt0:smo + nt0 + nb_]
                src3 = SRC[:, 0:W].rearrange("p (a c) -> p a c", c=wcols)
                self.v(lambda h, d=sma, s3=src3:
                       h.tensor_reduce(d.unsqueeze(-1), s3,
                                       AX.X, ALU.max, apply_absolute_value=True),
                       SRCb, [smb])
                self.v(lambda h, d=sma: h.tensor_scalar(
                    d, d, 1.0 / 127.0, None, ALU.mult), [smb], [smb])
                rc = self.sm_h[:, 160:160 + nb_]
                self.v(lambda h, d=rc, s_=sma: h.reciprocal(d, s_),
                       [smb], [self.sm_b[5]])
                rc_b = bass.AP(rc.tensor, rc.offset,
                               [list(rc.ap[0]), [1, nb_], [0, wcols]])
                t13 = T1[:, 0:W].rearrange("p (a c) -> p a c", c=wcols)
                self.v(lambda h, d=t13, s3=src3, r=rc_b:
                       h.tensor_tensor(d, s3, r, ALU.mult),
                       SRCb + [self.sm_b[5]], T1b)
                self.v(lambda h, d=T1[:, 0:W]:
                       h.tensor_scalar(d, d, -128.0, 127.0, ALU.max, ALU.min),
                       T1b, T1b)
                self.v(lambda h, d=SRC[:, 0:W].bitcast(I32), s_=T1[:, 0:W]:
                       h.tensor_copy(d, s_), T1b, SRCb)
                self.v(lambda h, d=DSTI[:, 0:W], s_=SRC[:, 0:W].bitcast(I32):
                       h.tensor_copy(d, s_), SRCb, DSTIb)
                nck = wcols // 128
                for bi in range(nb_):
                    nt = nt0 + bi
                    st_ap, st_b = STGS[nt % 2]
                    stB = st_ap.bitcast(BF16)
                    for ck in range(nck):
                        bank = 2 + (ck % 2)
                        self.tr(bank, self.pap(bank, 128, 128),
                                DSTI[:, bi * wcols + ck * 128:
                                     bi * wcols + ck * 128 + 128], DSTIb)
                        sg = stB[:, ck * 128:(ck + 1) * 128]
                        self.sc(lambda h, d=sg, p=self.pap(bank, 128, 128):
                               h.copy(d, p), [self.pb(bank)], st_b)
                    self.dma("gpsimd",
                             wdst.ap()[nt:nt + 1, :, :].squeeze(0),
                             stB[:, 0:nck * 128], st_b, [wdb])

    def global_absmax(self, am_ap, am_buf, gco):
        self.tr(5, self.pap(5, 1, 128), am_ap, [am_buf])
        g1 = self.sm_h[0:1, 176:177]
        self.v(lambda h, d=g1, p=self.pap(5, 1, 128):
               h.tensor_reduce(d, p, AX.X, ALU.max), [self.pb(5)], [self.sm_b[5]])
        cst = self.sm_h[0:1, 192:208]
        self.v(lambda h, d=cst: h.memset(d, 0.0), [], [self.sm_b[6]])
        self.v(lambda h, d=self.sm_h[0:1, 192:193], s_=g1:
               h.tensor_copy(d, s_), [self.sm_b[5]], [self.sm_b[6]])
        self.dma("gpsimd", self.cc_in_d.ap(), cst, [self.sm_b[6]], [self.cc_in_db])
        self.gp(lambda h: h.collective_compute(
            "AllReduce", mybir.AluOpType.max, replica_groups=[list(range(8))],
            ins=[self.cc_in_d.ap().opt()], outs=[self.cc_out_d.ap().opt()]),
            [self.cc_in_db], [self.cc_out_db], sem_only=True)
        self.dma("gpsimd", self.sm_h[0:1, 208:224], self.cc_out_d.ap(),
                 [self.cc_out_db], [self.sm_b[6]])
        self.mm(5, self.pap(5, 128, 1),
                [(self.c_ones1[:, :], self.sm_h[0:1, 208:209])],
                [self.c_ones1_b, self.sm_b[6]])
        gc = self.sm_h[:, gco:gco + 1]
        self.sc(lambda h, d=gc, p=self.pap(5, 128, 1): h.copy(d, p),
               [self.pb(5)], [self.sm_b[7]])
        return gc, self.sm_b[7]

    def scale_from_clip(self, clip_name, gx, gxb, sxo, rsxo):
        clip = self.sm_h[0:1, 226:227]
        self.dma("sync", clip, self.ins[clip_name].ap(),
                 [self.in_bufs[clip_name]], [self.sm_b[7]])
        sgc = self.sm_h[0:1, 227:228]
        self.sc(lambda h, d=sgc, s_=clip: h.activation(d, s_, AF.Sigmoid),
                [self.sm_b[7]], [self.sm_b[7]])
        self.mm(5, self.pap(5, 128, 1), [(self.c_ones1[:, :], sgc)],
                [self.c_ones1_b, self.sm_b[7]])
        sx = self.sm_h[:, sxo:sxo + 1]
        rsx = self.sm_h[:, rsxo:rsxo + 1]
        self.v(lambda h, d=sx, p=self.pap(5, 128, 1), g2=gx:
               h.tensor_tensor(d, p, g2, ALU.mult),
               [self.pb(5), gxb], [self.sm_b[7]])
        self.v(lambda h, d=sx: h.tensor_scalar(d, sx, 1.0 / 127.0, None, ALU.mult),
               [self.sm_b[7]], [self.sm_b[7]])
        self.v(lambda h, d=rsx, s_=sx: h.reciprocal(d, s_),
               [self.sm_b[7]], [self.sm_b[7]])
        return sx, rsx, self.sm_b[7]

    def quant_int(self, src_ap, src_bufs, dst_ap, dst_bufs, rs_ap, rs_bufs,
                  t_ap, t_bufs, u_ap, u_bufs):
        self.v(lambda h, d=t_ap, s_=src_ap, r=rs_ap:
               h.tensor_scalar(d, s_, r, None, ALU.mult),
               list(src_bufs) + list(rs_bufs), list(t_bufs))
        self.v(lambda h, d=t_ap:
               h.tensor_scalar(d, t_ap, -128.0, 127.0, ALU.max, ALU.min),
               list(t_bufs), list(t_bufs))
        self.v(lambda h, d=u_ap.bitcast(I32), s_=t_ap:
               h.tensor_copy(d, s_), list(t_bufs), list(u_bufs))
        self.v(lambda h, d=dst_ap, s_=u_ap.bitcast(I32):
               h.tensor_copy(d, s_), list(u_bufs), list(dst_bufs))

    def ffn1(self):
        if not self.l1w_done:
            self.ffn1_weights()
            self.l1w_done = True
        A = self.aview
        XHT32, XHT32b = A(0, 16384)
        IXQT, IXQTb = A(0, 8192)
        IHQT, IHQTb = A(8192, 5632)
        FS0, FS0b = A(16384, 512)
        FS1, FS1b = A(16896, 512)
        FS2, FS2b = A(17408, 512)
        FS3, FS3b = A(17920, 512)
        SQ, SQb = A(18432, 1024)
        AMX = self.sm_h[:, 384:390]
        AMXb = [self.sm_b[12]]
        IXQTB = IXQT.bitcast(BF16)
        IHQTB = IHQT.bitcast(BF16)
        wpB = self.wp_h[:, :].bitcast(BF16)

        self.v(lambda h, d=AMX[:, 0:4]: h.memset(d, 0.0), [], AMXb)
        for g in range(4):
            tis = list(range(g * 4, g * 4 + 4))
            dsts = [(self.xhat_h[:, i * 1024:(i + 1) * 1024], [self.xhat_b[i]])
                    for i in range(4)]
            self.ln_group(tis, dsts, SQ, SQb)
            self.v(lambda h, d=AMX[:, 2:3], s_=self.xhat_h[:, 0:4096]:
                   h.tensor_reduce(
                d.unsqueeze(-1), s_.rearrange("p (a b) -> p a b", b=4096),
                AX.X, ALU.max, apply_absolute_value=True),
                list(self.xhat_b), AMXb)
            self.v(lambda h, d=AMX[:, 0:1], s_=AMX[:, 2:3]:
                   h.tensor_tensor(d, d, s_, ALU.max), AMXb, AMXb)
            for i, ti in enumerate(tis):
                xh_ap = self.xhat_h[:, i * 1024:(i + 1) * 1024]
                xh_buf = self.xhat_b[i]
                for hc in range(8):
                    bank = 6 + (hc % 2)
                    self.tr(bank, self.pap(bank, 128, 128),
                            xh_ap[:, hc * 128:(hc + 1) * 128], [xh_buf])
                    dcol = hc * 2048 + ti * 128
                    self.sc(lambda h, d=XHT32[:, dcol:dcol + 128],
                           p=self.pap(bank, 128, 128): h.copy(d, p),
                           [self.pb(bank)], [XHT32b[dcol // 512]])
        gx, gxb = self.global_absmax(AMX[:, 0:1], AMXb[0], 224)
        sx, rsx, sxb = self.scale_from_clip("clip_x", gx, gxb, 228, 229)
        for smo, dsto in ((64, 256), (96, 288)):
            self.v(lambda h, d=self.sm_h[:, dsto:dsto + 22],
                   s_=self.sm_h[:, smo:smo + 22], c2=sx:
                   h.tensor_scalar(d, s_, c2, None, ALU.mult),
                   [self.sm_b[smo // 32], sxb], [self.sm_b[dsto // 32]])
        for hc in range(8):
            for tq in range(4):
                co = hc * 2048 + tq * 512
                srcap = XHT32[:, co:co + 512]
                srcb = [XHT32b[co // 512]]
                dst = IXQTB[:, co:co + 512]
                dbufs = [IXQTb[(hc * 1024 + tq * 256) // 512]]
                fa, fab = (FS0, FS0b) if (hc * 4 + tq) % 2 == 0 else (FS2, FS2b)
                fb_, fbb = (FS1, FS1b) if (hc * 4 + tq) % 2 == 0 else (FS3, FS3b)
                self.quant_int(srcap, srcb, dst, dbufs, rsx, [sxb],
                               fa[:, 0:512], fab, fb_[:, 0:512], fbb)
        self.v(lambda h, d=AMX[:, 1:2]: h.memset(d, 0.0), [], AMXb)
        for nt in range(22):
            par = nt % 2
            goff = par * 1024
            uoff = 2048 + par * 1024
            for wd, wdb, off, slot in ((self.iwgT_d, self.iwgT_db, goff, par),
                                       (self.iwuT_d, self.iwuT_db, uoff, 2 + par)):
                srca = wd.ap()[nt:nt + 1, :, :].squeeze(0)
                self.dma("sync", wpB[:, off:off + 1024], srca,
                         [wdb], [self.wp_b[slot]])
            sg = self.sm_h[:, 256 + nt:257 + nt]
            su = self.sm_h[:, 288 + nt:289 + nt]
            for tq in range(4):
                fpar = tq % 2
                Fa, Fab = (FS0, FS0b) if fpar == 0 else (FS2, FS2b)
                Fb, Fbb = (FS1, FS1b) if fpar == 0 else (FS3, FS3b)
                gpairs = [(wpB[:, goff + hc * 128:goff + hc * 128 + 128],
                           IXQTB[:, hc * 2048 + tq * 512:hc * 2048 + tq * 512 + 512])
                          for hc in range(8)]
                upairs = [(wpB[:, uoff + hc * 128:uoff + hc * 128 + 128],
                           IXQTB[:, hc * 2048 + tq * 512:hc * 2048 + tq * 512 + 512])
                          for hc in range(8)]
                self.mm(fpar, self.pap(fpar), gpairs,
                        [self.wp_b[par]] + list(IXQTb))
                self.v(lambda h, d=Fa[:, 0:512], p=self.pap(fpar), s_=sg:
                       h.tensor_scalar(d, p, s_, None, ALU.mult),
                       [self.pb(fpar), self.sm_b[8]], Fab)
                self.sc(lambda h, d=Fb[:, 0:512], s_=Fa[:, 0:512]:
                        h.activation(d, s_, AF.Silu), Fab, Fbb)
                self.mm(2 + fpar, self.pap(2 + fpar), upairs,
                        [self.wp_b[2 + par]] + list(IXQTb))
                self.v(lambda h, d=Fa[:, 0:512], p=self.pap(2 + fpar), s_=su:
                       h.tensor_scalar(d, p, s_, None, ALU.mult),
                       [self.pb(2 + fpar), self.sm_b[9]], Fab)
                self.v(lambda h, d=Fa[:, 0:512], s_=Fb[:, 0:512]:
                       h.tensor_tensor(d, d, s_, ALU.mult), Fab + Fbb, Fab)
                self.v(lambda h, d=AMX[:, 2 + fpar:3 + fpar], s_=Fa[:, 0:512]:
                       h.tensor_reduce(d.unsqueeze(-1),
                                       s_.rearrange("p (a b) -> p a b", b=512),
                                       AX.X, ALU.max, apply_absolute_value=True),
                       Fab, AMXb)
                self.v(lambda h, d=AMX[:, 1:2], s_=AMX[:, 2 + fpar:3 + fpar]:
                       h.tensor_tensor(d, d, s_, ALU.max), AMXb, AMXb)
                self.dma("gpsimd",
                         self.hsp_d.ap()[nt * 128:(nt + 1) * 128,
                                         tq * 512:(tq + 1) * 512],
                         Fa[:, 0:512], Fab, [self.hsp_db])
        gh, ghb = self.global_absmax(AMX[:, 1:2], AMXb[0], 225)
        sh, rsh, shb = self.scale_from_clip("clip_d", gh, ghb, 230, 231)
        self.v(lambda h, d=self.sm_h[:, 320:328], s_=self.sm_h[:, 128:136], c2=sh:
               h.tensor_scalar(d, s_, c2, None, ALU.mult),
               [self.sm_b[4], shb], [self.sm_b[10]])
        for tq in range(4):
            for ic in range(22):
                fpar = ic % 2
                Fa, Fab = (FS0, FS0b) if fpar == 0 else (FS2, FS2b)
                Fb, Fbb = (FS1, FS1b) if fpar == 0 else (FS3, FS3b)
                self.dma("sync", Fa[:, 0:512],
                         self.hsp_d.ap()[ic * 128:(ic + 1) * 128,
                                         tq * 512:(tq + 1) * 512],
                         [self.hsp_db], Fab)
                dst = IHQTB[:, ic * 512:(ic + 1) * 512]
                dbufs = [IHQTb[(ic * 256) // 512]]
                self.quant_int(Fa[:, 0:512], Fab, dst, dbufs, rsh, [shb],
                               Fb[:, 0:512], Fbb, SQ[:, 0:512], SQb)
            for ft in range(8):
                par = ft % 2
                doff = par * 4096
                wslots = [self.wp_b[par * 4], self.wp_b[par * 4 + 1],
                          self.wp_b[par * 4 + 2]]
                srca = self.iwdT_d.ap()[ft:ft + 1, :, :].squeeze(0)
                self.dma("sync", wpB[:, doff:doff + 2816], srca,
                         [self.iwdT_db], wslots)
                pairs = [(wpB[:, doff + ic * 128:doff + ic * 128 + 128],
                          IHQTB[:, ic * 512:(ic + 1) * 512])
                         for ic in range(22)]
                self.mm(4 + par, self.pap(4 + par), pairs, wslots + list(IHQTb))
                scc = self.sm_h[:, 320 + ft:321 + ft]
                Fa, Fab = (FS0, FS0b) if par == 0 else (FS2, FS2b)
                self.v(lambda h, d=Fa[:, 0:512], p=self.pap(4 + par), s_=scc:
                       h.tensor_scalar(d, p, s_, None, ALU.mult),
                       [self.pb(4 + par), self.sm_b[10]], Fab)
                for q4 in range(4):
                    bank = 6 + (q4 % 2)
                    self.tr(bank, self.pap(bank, 128, 128),
                            Fa[:, q4 * 128:(q4 + 1) * 128], Fab)
                    ti = tq * 4 + q4
                    x_ap = self.xh[:, ti * 1024 + ft * 128:
                                   ti * 1024 + ft * 128 + 128]
                    self.v(lambda h, x=x_ap, p=self.pap(bank, 128, 128):
                           h.scalar_tensor_tensor(x, p, 0.5, x, ALU.mult, ALU.add),
                           [self.pb(bank), self.xb[ti]], [self.xb[ti]])

    def att1_prep(self):
        A = self.aview
        OH, OHb = A(0, 4096)
        STAGE, STAGEb = A(4096, 4096)
        self.dma("sync", OH[0:32, :], self.ins["c_onehot"].ap(),
                 [self.in_bufs["c_onehot"]], OHb)
        tbl = self.sm_h[0:32, 336:352]
        self.dma("sync", tbl, self.ins["tbl"].ap(),
                 [self.in_bufs["tbl"]], [self.sm_b[10]])
        for jc in range(8):
            self.mm(4, self.pap(4, 16, 512),
                    [(tbl, OH[0:32, jc * 512:(jc + 1) * 512])],
                    [self.sm_b[10]] + OHb)
            self.sc(lambda h, d=STAGE[0:16, jc * 512:(jc + 1) * 512],
                   p=self.pap(4, 16, 512): h.copy(d, p),
                   [self.pb(4)], STAGEb)
        self.dma("gpsimd", self.wr_d.ap(), STAGE[0:16, 0:4096], STAGEb,
                 [self.wr_db])
        self.mm(5, self.pap(5, 128, 16), [(self.c_sel15[:, :], tbl)],
                [self.c_sel15_b, self.sm_b[10]])
        self.sc(lambda h, d=self.sm_h[:, 352:368], p=self.pap(5, 128, 16):
               h.copy(d, p), [self.pb(5)], [self.sm_b[11]])
        self.mm(5, self.pap(5, 128, 16), [(self.c_sel31[:, :], tbl)],
                [self.c_sel31_b, self.sm_b[10]])
        self.sc(lambda h, d=self.sm_h[:, 368:384], p=self.pap(5, 128, 16):
               h.copy(d, p), [self.pb(5)], [self.sm_b[11]])

    def att1(self):
        lim = _ATT1_LIMIT
        if not self.att1_prep_done:
            self.att1_prep()
            self.att1_prep_done = True
        if lim < 1:
            return
        A = self.aview
        XHT32, XHT32b = A(0, 16384)
        SQ, SQb = A(16384, 1024)
        FS0, FS0b = A(17408, 512)
        FS1, FS1b = A(17920, 512)
        FS2, FS2b = A(18432, 512)
        KTP, KTPb = A(0, 2048)
        VP, VPb = A(2048, 2176)
        QZ = [A(4224, 512), A(5760, 512)]
        PT = [A(4736, 512), A(5248, 512)]
        XTo = [A(8320, 512), A(8832, 512)]
        WWC, WWCb = A(9728, 2304)

        for g in range(4):
            tis = list(range(g * 4, g * 4 + 4))
            dsts = [(self.xhat_h[:, i * 1024:(i + 1) * 1024], [self.xhat_b[i]])
                    for i in range(4)]
            self.ln_group(tis, dsts, SQ, SQb)
            for i, ti in enumerate(tis):
                xh_ap = self.xhat_h[:, i * 1024:(i + 1) * 1024]
                xh_buf = self.xhat_b[i]
                for hc in range(8):
                    bank = 6 + (hc % 2)
                    self.tr(bank, self.pap(bank, 128, 128),
                            xh_ap[:, hc * 128:(hc + 1) * 128], [xh_buf])
                    dcol = hc * 2048 + ti * 128
                    self.sc(lambda h, d=XHT32[:, dcol:dcol + 128],
                           p=self.pap(bank, 128, 128): h.copy(d, p),
                           [self.pb(bank)], [XHT32b[dcol // 512]])
        if lim < 2:
            return
        for wname, dst_d, dst_db in (("wqT1", self.qsp_d, self.qsp_db),
                                     ("wkT1", self.ksp_d, self.ksp_db)):
            wt = self.ins[wname]
            wdb_ = self.in_bufs[wname]
            for half in range(2):
                for sl in range(8):
                    self.dma("sync", self.wp_h[:, sl * 512:(sl + 1) * 512],
                             wt.ap()[half:half + 1, :,
                                     sl * 512:(sl + 1) * 512].squeeze(0),
                             [wdb_], [self.wp_b[sl]])
                for qc in range(4):
                    for f4 in range(4):
                        ft = half * 4 + f4
                        pairs = [(self.wp_h[:, (hc * 4 + f4) * 128:
                                            (hc * 4 + f4) * 128 + 128],
                                  XHT32[:, hc * 2048 + qc * 512:
                                        hc * 2048 + qc * 512 + 512])
                                 for hc in range(8)]
                        self.mm(f4, self.pap(f4), pairs,
                                list(self.wp_b) + list(XHT32b))
                        Fa, Fab = (FS0, FS0b) if f4 % 2 == 0 else (FS1, FS1b)
                        self.sc(lambda h, d=Fa[:, 0:512], p=self.pap(f4):
                               h.copy(d, p), [self.pb(f4)], Fab)
                        self.dma("sync",
                                 dst_d.ap()[ft * 128:(ft + 1) * 128,
                                            qc * 512:(qc + 1) * 512],
                                 Fa[:, 0:512], Fab, [dst_db])
        for fh in range(2):
            for sl in range(8):
                self.dma("sync", self.wp_h[:, sl * 512:(sl + 1) * 512],
                         self.ins["wvT1"].ap()[fh:fh + 1, :,
                                               sl * 512:(sl + 1) * 512].squeeze(0),
                         [self.in_bufs["wvT1"]], [self.wp_b[sl]])
            for ti in range(16):
                pairs = [(XHT32[:, hc * 2048 + ti * 128:
                                hc * 2048 + ti * 128 + 128],
                          self.wp_h[:, hc * 512:(hc + 1) * 512])
                         for hc in range(8)]
                bank = fh * 4 + (ti % 4)
                self.mm(bank, self.pap(bank), pairs,
                        list(self.wp_b) + list(XHT32b))
                Fa, Fab = (FS0, FS0b) if ti % 2 == 0 else (FS1, FS1b)
                self.sc(lambda h, d=Fa[:, 0:512], p=self.pap(bank):
                       h.copy(d, p), [self.pb(bank)], Fab)
                self.dma("sync",
                         self.vsp_d.ap()[ti * 128:(ti + 1) * 128,
                                         fh * 512:(fh + 1) * 512],
                         Fa[:, 0:512], Fab, [self.vsp_db])
        if lim < 3:
            return
        self.v(lambda h, a=QZ[0][0][64:128, :]: h.memset(a, 0.0), [], QZ[0][1])
        self.v(lambda h, a=QZ[1][0][0:64, :]: h.memset(a, 0.0), [], QZ[1][1])
        for p in range(8):
            self.dma("gpsimd", KTP, self.ksp_d.ap()[p * 128:(p + 1) * 128, :],
                     [self.ksp_db], KTPb)
            for hh in range(2):
                hglob = 2 * p + hh
                src = bass.AP(self.wr_d.ap().tensor, hglob * 4096 + 1408,
                              [[1, 128], [1, 1152]])
                self.dma("gpsimd", WWC[:, hh * 1152:(hh + 1) * 1152], src,
                         [self.wr_db], WWCb)
            for tk in range(16):
                src = self.vsp_d.ap()[tk * 128:(tk + 1) * 128,
                                      p * 128:(p + 1) * 128]
                src2 = src.rearrange("p (a b) -> p a b", b=64)
                dst = VP[:, tk * 130:tk * 130 + 130]
                dst2 = bass.AP(dst.tensor, dst.offset,
                               [list(dst.ap[0]), [65, 2], [1, 64]])
                self.dma("gpsimd", dst2, src2, [self.vsp_db], VPb)
            vones = bass.AP(VP.tensor, VP.offset + 64,
                            [list(VP.ap[0]), [130, 16], [65, 2]])
            self.v(lambda h, a=vones: h.memset(a, 1.0), VPb, VPb)
            for qc in range(4):
                self.dma("sync", QZ[0][0][0:64, :],
                         self.qsp_d.ap()[p * 128:p * 128 + 64,
                                         qc * 512:(qc + 1) * 512],
                         [self.qsp_db], QZ[0][1])
                self.dma("sync", QZ[1][0][64:128, :],
                         self.qsp_d.ap()[p * 128 + 64:p * 128 + 128,
                                         qc * 512:(qc + 1) * 512],
                         [self.qsp_db], QZ[1][1])
                for hh in range(2):
                    hglob = 2 * p + hh
                    for kt in range(16):
                        band = (4 * qc - 1 <= kt <= 4 * qc + 4)
                        sb_ = kt % 2
                        pairs = [(KTP[:, kt * 128:(kt + 1) * 128],
                                  QZ[hh][0][:, :])]
                        rds = KTPb + QZ[hh][1]
                        if band:
                            boff = (1920 - 128 * kt + 512 * qc) - 1408
                            ww_ap = WWC[:, hh * 1152 + boff:
                                        hh * 1152 + boff + 512]
                            pairs = pairs + [(self.c_rev[:, :], ww_ap)]
                            rds = rds + [self.c_rev_b] + WWCb
                        self.mm(sb_, self.pap(sb_), pairs, rds)
                        pt_ap, pt_b = PT[sb_]
                        if band:
                            self.sc(lambda h, d=pt_ap, ps=self.pap(sb_):
                                    h.activation(d, ps, AF.Exp, scale=0.125),
                                    [self.pb(sb_)], pt_b)
                        else:
                            bco = 352 if kt <= 4 * qc - 2 else 368
                            bias = self.sm_h[:, bco + hglob:bco + hglob + 1]
                            self.sc(lambda h, d=pt_ap, ps=self.pap(sb_), b2=bias:
                                    h.activation(d, ps, AF.Exp, scale=0.125,
                                                 bias=b2),
                                    [self.pb(sb_), self.sm_b[11]], pt_b)
                        self.mm(2, self.pap(2, 65, 512),
                                [(VP[:, kt * 130 + hh * 65:
                                     kt * 130 + hh * 65 + 65], pt_ap)],
                                VPb + pt_b, start=(kt == 0), stop=(kt == 15))
                    self.v(lambda h, d=FS0[0:1, 0:512],
                           p=self.psum_h[2][64:65, 0:512]:
                           h.reciprocal(d, p), [self.pb(2)], FS0b)
                    self.mm(3, self.pap(3, 64, 512),
                            [(self.c_ones1[0:1, 0:64], FS0[0:1, 0:512])],
                            [self.c_ones1_b] + FS0b)
                    self.sc(lambda h, d=FS1[0:64, 0:512], p=self.pap(3, 64, 512):
                           h.copy(d, p), [self.pb(3)], FS1b)
                    self.v(lambda h, d=FS2[0:64, 0:512],
                           p=self.psum_h[2][0:64, 0:512], m=FS1[0:64, 0:512]:
                           h.tensor_tensor(d, p, m, ALU.mult),
                           [self.pb(2)] + FS1b, FS2b)
                    self.dma("sync",
                             self.ysp_d.ap()[hglob * 64:(hglob + 1) * 64,
                                             qc * 512:(qc + 1) * 512],
                             FS2[0:64, 0:512], FS2b, [self.ysp_db])
        if lim < 4:
            return
        for fh in range(2):
            for sl in range(8):
                self.dma("gpsimd", self.wp_h[:, sl * 512:(sl + 1) * 512],
                         self.ins["woT1"].ap()[fh:fh + 1, :,
                                               sl * 512:(sl + 1) * 512].squeeze(0),
                         [self.in_bufs["woT1"]], [self.wp_b[sl]])
            for tg in range(4):
                for hc in range(8):
                    xt_ap, xt_b = XTo[hc % 2]
                    self.dma("sync", xt_ap[:, 0:512],
                             self.ysp_d.ap()[hc * 128:(hc + 1) * 128,
                                             tg * 512:(tg + 1) * 512],
                             [self.ysp_db], xt_b)
                    for q4 in range(4):
                        self.mm(fh * 4 + q4, self.pap(fh * 4 + q4),
                                [(xt_ap[:, q4 * 128:(q4 + 1) * 128],
                                  self.wp_h[:, hc * 512:(hc + 1) * 512])],
                                xt_b + [self.wp_b[hc]],
                                start=(hc == 0), stop=(hc == 7))
                for q4 in range(4):
                    ti = tg * 4 + q4
                    x_ap = self.xh[:, ti * 1024 + fh * 512:
                                   ti * 1024 + fh * 512 + 512]
                    self.v(lambda h, x=x_ap, p=self.pap(fh * 4 + q4):
                           h.tensor_tensor(x, x, p, ALU.add),
                           [self.pb(fh * 4 + q4), self.xb[ti]], [self.xb[ti]])


def kernel(**inputs):
    cons = host_constants()
    f32 = np.float32

    def tr(a):
        return np.ascontiguousarray(np.asarray(a, f32).T)

    def rep(a):
        return np.ascontiguousarray(np.tile(np.asarray(a, f32)[None, :], (128, 1)))

    shared = dict(cons)
    shared["tbl"] = np.asarray(inputs["L1_attn_rel_bias_table"], f32)
    shared["wg0_nat"] = np.asarray(inputs["L0_ffn_w_gate"], f32)
    shared["wu0_nat"] = np.asarray(inputs["L0_ffn_w_up"], f32)
    shared["wd0_nat"] = np.asarray(inputs["L0_ffn_w_down"], f32)
    def ftile(a):
        return np.ascontiguousarray(
            a.reshape(8, 128, 8, 128).transpose(2, 1, 0, 3).reshape(8, 128, 1024))

    def pool2a(a):
        return np.ascontiguousarray(
            a.reshape(8, 128, 2, 512).transpose(2, 1, 0, 3).reshape(2, 128, 4096))

    shared["wqT0"] = ftile(tr(inputs["L0_attn_Wq"]))
    shared["wkT0"] = ftile(tr(inputs["L0_attn_Wk"]))
    shared["wvT0"] = pool2a(tr(inputs["L0_attn_Wv"]))
    shared["woutT0"] = pool2a(tr(inputs["L0_attn_Wout"]))
    shared["wbmgT"] = np.ascontiguousarray(np.concatenate(
        [np.asarray(inputs["L0_attn_Wbeta"], f32).T,
         np.asarray(inputs["L0_attn_Wmix"], f32).T,
         np.asarray(inputs["L0_attn_Wgate"], f32).T], axis=1))
    shared["bmg_bias"] = np.ascontiguousarray(np.tile(np.concatenate(
        [np.asarray(inputs["L0_attn_bbeta"], f32),
         np.asarray(inputs["L0_attn_bmix"], f32),
         np.asarray(inputs["L0_attn_bgate"], f32)])[None, :], (128, 1)))
    shared["bout_rep"] = rep(inputs["L0_attn_bout"])
    shared["lng_rep"] = rep(inputs["L0_attn_ln_g"])
    shared["lnb_rep"] = rep(inputs["L0_attn_ln_b"])
    def pool2(a):
        return np.ascontiguousarray(
            a.reshape(8, 128, 2, 512).transpose(2, 1, 0, 3).reshape(2, 128, 4096))

    shared["wqT1"] = pool2(tr(inputs["L1_attn_w_q"]))
    shared["wkT1"] = pool2(tr(inputs["L1_attn_w_k"]))
    shared["wvT1"] = pool2(tr(inputs["L1_attn_w_v"]))
    shared["woT1"] = pool2(tr(inputs["L1_attn_w_o"]))
    shared["wg1_nat"] = np.asarray(inputs["L1_ffn_w_gate"], f32)
    shared["wu1_nat"] = np.asarray(inputs["L1_ffn_w_up"], f32)
    shared["wd1_nat"] = np.asarray(inputs["L1_ffn_w_down"], f32)
    shared["clip_x"] = np.asarray(inputs["L1_ffn_clip_x"], f32).reshape(1, 1)
    shared["clip_d"] = np.asarray(inputs["L1_ffn_clip_down"], f32).reshape(1, 1)

    x = np.asarray(inputs["L0_attn_x"], f32)
    in_desc = [("x_in", (T, D), F32)]
    for name, arr in shared.items():
        in_desc.append((name, arr.shape, F32))

    nc = bass.Bass("TRN2", target_bir_lowering=False, debug=False)
    k = K(nc, _STAGE)
    k.build(in_desc)

    in_maps = []
    for c in range(8):
        m = dict(shared)
        m["x_in"] = np.ascontiguousarray(x[c])
        in_maps.append(m)
    res = run_bass_kernel_spmd(nc, in_maps, core_ids=list(range(8)))
    out = np.stack([np.asarray(res.results[c]["out"]) for c in range(8)], axis=0)
    return out.astype(np.float32)


# version 16
# speedup vs baseline: 1.0024x; 1.0024x over previous
import math
import numpy as np
import concourse.bass as bass
import concourse.mybir as mybir
from concourse.bass_utils import run_bass_kernel_spmd
from concourse.alu_op_type import AluOpType as ALU

F32 = mybir.dt.float32
BF16 = mybir.dt.bfloat16
I32 = mybir.dt.int32
AF = mybir.ActivationFunctionType
AX = mybir.AxisListType

T = 2048
D = 1024
NI = 2816
NEG = -1.0e30

_STAGE = 6
_ATT1_LIMIT = 99


ENGINES = ("sync", "tensor", "vector", "scalar", "gpsimd")


class Buf:
    __slots__ = ("name", "writer", "readers")

    def __init__(self, name):
        self.name = name
        self.writer = None
        self.readers = []


class Sched:

    def __init__(self):
        self.ops = {e: [] for e in ENGINES}
        self.edges = []
        self.gseq = []

    def op(self, eng, fn, reads=(), writes=(), dma=False, sem_only=False):
        idx = len(self.ops[eng])
        for b in reads:
            w = b.writer
            if w is not None and w != (eng, idx):
                self.edges.append((w[0], w[1], eng, idx))
            b.readers.append((eng, idx))
        for b in writes:
            w = b.writer
            if w is not None and w != (eng, idx):
                self.edges.append((w[0], w[1], eng, idx))
            for (re, ri) in b.readers:
                if (re, ri) != (eng, idx):
                    self.edges.append((re, ri, eng, idx))
            b.writer = (eng, idx)
            b.readers = []
        self.ops[eng].append([fn, dma, False, sem_only])
        self.gseq.append((eng, idx))
        return (eng, idx)

    def finalize_and_emit(self, nc):
        ops = self.ops
        incoming = {}
        for (se, si, de, di) in self.edges:
            incoming.setdefault((de, di), set()).add((se, si))
        for (de, di), srcs in incoming.items():
            for (se, si) in srcs:
                ops[se][si][2] = True
        for e in ENGINES:
            for o in ops[e]:
                if o[1]:
                    o[2] = True
        val_at = {}
        for e in ENGINES:
            c = 0
            va = []
            for o in ops[e]:
                if o[2]:
                    c += 16 if o[1] else 1
                va.append(c)
            val_at[e] = va
        K = {e: {e2: 0 for e2 in ENGINES} for e in ENGINES}
        val_issued = {e: 0 for e in ENGINES}
        self_done = {e: -1 for e in ENGINES}
        done_know = {}
        waits = {}
        drains = set()
        for (e, i) in self.gseq:
            srcs = incoming.get((e, i), ())
            sem_w = {}
            drain_srcs = []
            for (se, si) in srcs:
                v = val_at[se][si]
                if sem_w.get(se, 0) < v:
                    sem_w[se] = v
            ke = K[e]
            for se in sorted(sem_w):
                v = sem_w[se]
                if ke[se] >= v:
                    continue
                waits.setdefault((e, i), {})[se] = v
                dk = done_know.get((se, v))
                if dk:
                    for e2, v2 in dk.items():
                        if ke[e2] < v2:
                            ke[e2] = v2
                if ke[se] < v:
                    ke[se] = v
            if drain_srcs:
                need = False
                for j in drain_srcs:
                    if self_done[e] >= j:
                        continue
                    oj = ops[e][j]
                    if oj[2] and ke[e] >= val_at[e][j]:
                        continue
                    need = True
                    break
                if need:
                    drains.add((e, i))
                    self_done[e] = i - 1
                    if ke[e] < val_issued[e]:
                        ke[e] = val_issued[e]
            o = ops[e][i]
            if o[2]:
                val_issued[e] += 16 if o[1] else 1
                snap = dict(ke)
                snap[e] = val_issued[e]
                done_know[(e, val_issued[e])] = snap
        sems = {e: nc.alloc_semaphore(f"sem_{e}") for e in ENGINES}
        nops = sum(len(v) for v in ops.values())
        nincs = sum(1 for e in ENGINES for o in ops[e] if o[2])
        nwaits = sum(len(d) for d in waits.values())
        print(f"[kernel] ops={nops} incs={nincs} waits={nwaits} "
              f"drains={len(drains)} edges={len(self.edges)} "
              + " ".join(f"{e}:{len(ops[e])}" for e in ENGINES))
        sched = self

        def emit(eng_name, handle):
            for i, o in enumerate(sched.ops[eng_name]):
                w = waits.get((eng_name, i))
                if w:
                    for se in sorted(w):
                        handle.wait_ge(sems[se], w[se])
                if (eng_name, i) in drains:
                    handle.drain()
                inst = o[0](handle)
                if o[2]:
                    assert inst is not None, f"op {eng_name}:{i} needs inc"
                    inst.then_inc(sems[eng_name], 16 if o[1] else 1)

        with nc.Block() as block:
            @block.sync
            def _(h):
                emit("sync", h)

            @block.tensor
            def _(h):
                emit("tensor", h)

            @block.vector
            def _(h):
                emit("vector", h)

            @block.scalar
            def _(h):
                emit("scalar", h)

            @block.gpsimd
            def _(h):
                emit("gpsimd", h)


def _alibi_slopes(n):
    def pow2(m):
        start = 2 ** (-(2 ** (-(math.log2(m) - 3))))
        return [start * start ** i for i in range(m)]
    if math.log2(n).is_integer():
        return np.array(pow2(n), dtype=np.float32)
    c = 2 ** int(math.floor(math.log2(n)))
    s = pow2(c) + _alibi_slopes(2 * c).tolist()[: n - c]
    return np.array(s, dtype=np.float32)


def _rel_bucket(relpos):
    num_buckets = 16
    n = -relpos
    ret = (n < 0).astype(np.int64) * num_buckets
    n = np.abs(n)
    max_exact = num_buckets // 2
    is_small = n < max_exact
    val_if_large = max_exact + (
        np.log(np.maximum(n, 1).astype(np.float32) / max_exact)
        / math.log(128 / max_exact) * (num_buckets - max_exact)
    ).astype(np.int64)
    val_if_large = np.minimum(val_if_large, num_buckets - 1)
    return ret + np.where(is_small, n, val_if_large)


def host_constants():
    c = {}
    c["c_ident"] = np.eye(128, dtype=np.float32)
    c["c_ones1"] = np.ones((1, 128), dtype=np.float32)
    c["c_allones"] = np.ones((128, 128), dtype=np.float32)
    c["c_negones"] = -np.ones((128, 128), dtype=np.float32)
    ii = np.arange(128)
    c["c_tri"] = (ii[:, None] <= ii[None, :]).astype(np.float32)
    c["c_maskst"] = np.where(ii[:, None] <= ii[None, :], 0.0, NEG).astype(np.float32)
    c["c_rev"] = np.eye(128, dtype=np.float32)[::-1].copy()
    j = np.arange(4096)
    rel = np.minimum(j, 4094) - 2047
    buck = _rel_bucket(-rel)
    oh = np.zeros((32, 4096), np.float32)
    oh[buck, j] = 8.0
    c["c_onehot"] = oh
    sel15 = np.zeros((32, 128), np.float32)
    sel15[15, :] = 1.0
    sel31 = np.zeros((32, 128), np.float32)
    sel31[31, :] = 1.0
    c["c_sel15"] = sel15
    c["c_sel31"] = sel31
    r = np.exp(-_alibi_slopes(8))
    c["c_rrep"] = np.tile(np.repeat(r, 2)[None, :], (128, 1)).astype(np.float32)
    return c


CONST_SHAPES = {"c_ident": [128, 128], "c_ones1": [1, 128],
                "c_allones": [128, 128], "c_negones": [128, 128],
                "c_tri": [128, 128], "c_maskst": [128, 128], "c_rev": [128, 128],
                "c_sel15": [32, 128], "c_sel31": [32, 128], "c_rrep": [128, 16]}


class K:
    def __init__(self, nc, stage):
        self.nc = nc
        self.s = Sched()
        self.stage = stage
        self.debug_fn = None
        self._dumps = []

    def dump(self, name, src_ap, bufs, shape):
        t = self.nc.dram_tensor("dbg_" + name, list(shape), src_ap.dtype,
                                kind="ExternalOutput")
        b = Buf("dbg_" + name)
        self.dma("sync", t.ap(), src_ap, list(bufs), [b])
        self._dumps.append(b)

    def sbuf(self, name, shape, dtype=F32, nbufs=1):
        h = self.nc.alloc_sbuf_tensor(name, shape, dtype)
        bufs = [Buf(f"{name}.{i}") for i in range(nbufs)]
        return h, (bufs[0] if nbufs == 1 else bufs)

    def dram_scratch(self, name, shape, dtype=F32):
        h = self.nc.dram_tensor(name, shape, dtype)
        return h, Buf(name)

    def aview(self, off, n):
        ap = self.A_h[:, off:off + n]
        lo, hi = off // 512, (off + n - 1) // 512
        return ap, self.A_bufs[lo:hi + 1]

    def dma(self, eng, dst_ap, src_ap, reads, writes):
        self.s.op(eng, lambda h, d=dst_ap, s_=src_ap: h.dma_start(out=d, in_=s_),
                  reads=list(reads), writes=list(writes), dma=True)

    def mm(self, bank, out_ap, pairs, reads, start=True, stop=True):
        pb = self.psum_b[bank]

        def fn(h, pairs=pairs, out=out_ap, start=start, stop=stop):
            last = None
            n = len(pairs)
            for i, (l, r) in enumerate(pairs):
                last = h.matmul(out, l, r, start=(start and i == 0),
                                stop=(stop and i == n - 1),
                                skip_group_check=True)
            return last
        rd = list(reads)
        if not start:
            rd = rd + [pb]
        self.s.op("tensor", fn, reads=rd, writes=[pb])

    def tr(self, bank, out_ap, in_ap, reads):
        pb = self.psum_b[bank]
        self.s.op("tensor",
                  lambda h, o=out_ap, i=in_ap, d=self.ident_ap: h.transpose(o, i, d),
                  reads=list(reads) + [self.ident_buf], writes=[pb])

    def v(self, fn, reads=(), writes=()):
        self.s.op("vector", fn, reads=list(reads), writes=list(writes))

    def _e(self, eng, fn, reads=(), writes=()):
        self.s.op(eng, fn, reads=list(reads), writes=list(writes))

    def heat(self, n, reads, bank=0):
        for _ in range(n):
            self.mm(bank, self.pap(bank, 128, 512),
                    [(self.c_ident[:, :], self.heat_h[:, :])],
                    [self.c_ident_b, self.heat_b] + list(reads))

    def sc(self, fn, reads=(), writes=()):
        self.s.op("scalar", fn, reads=list(reads), writes=list(writes))

    def gp(self, fn, reads=(), writes=(), sem_only=False):
        self.s.op("gpsimd", fn, reads=list(reads), writes=list(writes),
                  sem_only=sem_only)

    def pb(self, bank):
        return self.psum_b[bank]

    def pap(self, bank, p=128, w=512):
        return self.psum_h[bank][0:p, 0:w]

    def build(self, in_desc):
        nc = self.nc
        ins = {}
        self.in_bufs = {}
        for name, shape, dtype in in_desc:
            ins[name] = nc.dram_tensor(name, list(shape), dtype, kind="ExternalInput")
            self.in_bufs[name] = Buf("in_" + name)
        self.ins = ins
        self.out_t = nc.dram_tensor("out", [T, D], F32, kind="ExternalOutput")
        self.out_buf = Buf("out")

        self.psum_h = []
        self.psum_b = []
        for i in range(8):
            h = nc.alloc_psum_tensor(f"pbank{i}", [128, 512], F32)
            self.psum_h.append(h)
            self.psum_b.append(Buf(f"pbank{i}"))

        AW = 19456
        self.A_h, _ = self.sbuf("arena", [128, AW], F32)
        self.A_words = AW
        self.A_bufs = [Buf(f"A.{i}") for i in range(AW // 512)]

        for name, shp in CONST_SHAPES.items():
            h, b = self.sbuf(name + "_s", shp, F32)
            setattr(self, name, h)
            setattr(self, name + "_b", b)
            self.dma("sync", h[:, :], ins[name].ap(), [self.in_bufs[name]], [b])
        self.ident_ap = self.c_ident[:, :]
        self.ident_buf = self.c_ident_b

        self.heat_h, hb = self.sbuf("heat", [128, 512], F32)
        self.heat_b = hb
        self.v(lambda h, a=self.heat_h[:, :]: h.memset(a, 1.0), [], [hb])

        self.xh, self.xb = self.sbuf("x", [128, 16 * 1024], F32, nbufs=16)
        for i in range(16):
            self.dma("sync" if i % 2 == 0 else "gpsimd",
                     self.xh[:, i * 1024:(i + 1) * 1024],
                     ins["x_in"].ap()[i * 128:(i + 1) * 128, :],
                     [self.in_bufs["x_in"]], [self.xb[i]])

        self.xhat_h, self.xhat_b = self.sbuf("xhat", [128, 4 * 1024], F32, nbufs=4)
        self.stat_h, self.stat_b = self.sbuf("stat", [128, 64], F32, nbufs=8)
        self.fsc_h, self.fsc_b = self.sbuf("fsc", [128, 1536], F32, nbufs=2)
        self.wp_h, self.wp_b = self.sbuf("wpool", [128, 8 * 512], F32, nbufs=8)
        self.sm_h, self.sm_b = self.sbuf("small", [128, 16 * 32], F32, nbufs=16)

        self.wgqT_d, self.wgqT_db = self.dram_scratch("wgqT_d", [11, 128, 2048])
        self.wuqT_d, self.wuqT_db = self.dram_scratch("wuqT_d", [11, 128, 2048])
        self.wdqT_d, self.wdqT_db = self.dram_scratch("wdqT_d", [44, 128, 512])
        self.iwgT_d, self.iwgT_db = self.dram_scratch("iwgT_d", [22, 128, 1024], BF16)
        self.iwuT_d, self.iwuT_db = self.dram_scratch("iwuT_d", [22, 128, 1024], BF16)
        self.iwdT_d, self.iwdT_db = self.dram_scratch("iwdT_d", [8, 128, 2816], BF16)
        self.hsp_d, self.hsp_db = self.dram_scratch("hsp_d", [NI, T])
        self.ysp_d, self.ysp_db = self.dram_scratch("ysp_d", [D, T])
        self.xTsp_d, self.xTsp_db = self.dram_scratch("xTsp_d", [D, T])
        self.qsp_d, self.qsp_db = self.dram_scratch("qsp_d", [D, T])
        self.ksp_d, self.ksp_db = self.dram_scratch("ksp_d", [D, T])
        self.vsp_d, self.vsp_db = self.dram_scratch("vsp_d", [T, D])
        self.wr_d, self.wr_db = self.dram_scratch("wr_d", [16, 4096])
        self.cc_in_d, self.cc_in_db = self.dram_scratch("cc_in", [1, 16])
        self.cc_out_d, self.cc_out_db = self.dram_scratch("cc_out", [1, 16])

        self.l0w_done = False
        self.l1w_done = False
        self.att1_prep_done = False

        if self.debug_fn is not None:
            self.debug_fn(self)
        if self.stage >= 1:
            self.ffn0()
        if self.stage >= 2:
            self.att0()
        if self.stage >= 3:
            self.ffn0()
        if self.stage >= 4:
            self.ffn1()
        if self.stage >= 5:
            self.att1()
        if self.stage >= 6:
            self.ffn1()

        for i in range(16):
            self.s.op("sync", (lambda h, i=i: h.dma_start(
                out=self.out_t.ap()[i * 128:(i + 1) * 128, :],
                in_=self.xh[:, i * 1024:(i + 1) * 1024])),
                reads=[self.xb[i]], writes=[self.out_buf], dma=True)
        self.s.op("sync", lambda h: None, reads=[self.out_buf] + self._dumps)
        self.s.finalize_and_emit(nc)

    def ln_tile(self, ti, dst_ap, dst_bufs, slot, sq_ap, sq_bufs):
        x_ap = self.xh[:, ti * 1024:(ti + 1) * 1024]
        xbuf = self.xb[ti]
        self.ln_generic(x_ap, [xbuf], dst_ap, dst_bufs, slot, sq_ap, sq_bufs)

    def ln_generic(self, x_ap, x_bufs, dst_ap, dst_bufs, slot, sq_ap, sq_bufs):
        st = self.stat_h
        stb = self.stat_b[slot % 8]
        o = 8 * (slot % 8)
        self.v(lambda h, x=x_ap, st=st, o=o: h.tensor_reduce(
            st[:, o:o + 1].unsqueeze(-1), x.rearrange("p (a b) -> p a b", b=1024),
            AX.X, ALU.add), list(x_bufs), [stb])
        self.v(lambda h, st=st, o=o: h.memset(st[:, o + 1:o + 2], 0.0),
               [], [stb])
        self.sc(lambda h, x=x_ap, sa=sq_ap, st=st, o=o: h.activation(
            sa, x, AF.Square, accum_out=st[:, o + 1:o + 2]),
            list(x_bufs), list(sq_bufs) + [stb])
        self.v(lambda h, st=st, o=o: h.tensor_scalar(
            st[:, o + 2:o + 3], st[:, o:o + 1], 1.0 / 1024.0, None, ALU.mult),
            [stb], [stb])
        self.v(lambda h, st=st, o=o: h.tensor_tensor(
            st[:, o + 5:o + 6], st[:, o + 2:o + 3], st[:, o + 2:o + 3], ALU.mult),
            [stb], [stb])
        self.v(lambda h, st=st, o=o: h.scalar_tensor_tensor(
            st[:, o + 4:o + 5], st[:, o + 1:o + 2], 1.0 / 1024.0,
            st[:, o + 5:o + 6], ALU.mult, ALU.subtract), [stb], [stb])
        self.v(lambda h, st=st, o=o: h.tensor_scalar(
            st[:, o + 4:o + 5], st[:, o + 4:o + 5], 1e-5, None, ALU.add),
            [stb], [stb])
        self.sc(lambda h, st=st, o=o: h.activation(
            st[:, o + 5:o + 6], st[:, o + 4:o + 5], AF.Sqrt),
            [stb], [stb])
        self.v(lambda h, st=st, o=o: h.reciprocal(
            st[:, o + 3:o + 4], st[:, o + 5:o + 6]), [stb], [stb])
        self.v(lambda h, x=x_ap, d=dst_ap, st=st, o=o: h.tensor_scalar(
            d, x, st[:, o + 2:o + 3], st[:, o + 3:o + 4], ALU.subtract, ALU.mult),
            list(x_bufs) + [stb], list(dst_bufs))

    def fp4(self, src_ap, src_bufs, dst_ap, dst_bufs, W, fslot,
            t1_ap, t1_bufs, t2_ap, t2_bufs, eng="vector"):
        nb = W // 16
        fs = self.fsc_h
        fb = self.fsc_b[fslot % 2]
        fo = 768 * (fslot % 2)
        assert 2 * nb <= 768
        src3 = src_ap.rearrange("p (a b) -> p a b", b=16)
        self.v(lambda h, s3=src3, d=fs[:, fo:fo + nb]: h.tensor_reduce(
            d.unsqueeze(-1), s3, AX.X, ALU.max, apply_absolute_value=True),
            list(src_bufs), [fb])
        self.v(lambda h, d=fs[:, fo:fo + nb]: h.tensor_scalar(
            d, d, 1.0 / 6.0, 1e-12, ALU.mult, ALU.max), [fb], [fb])
        self.v(lambda h, d=fs[:, fo + nb:fo + 2 * nb], s_=fs[:, fo:fo + nb]:
               h.reciprocal(d, s_), [fb], [fb])
        self.v(lambda h, d=fs[:, fo:fo + nb]: h.tensor_scalar(
            d, d, 0.5, None, ALU.mult), [fb], [fb])
        s_sc = fs[:, fo:fo + nb]
        s_rc = fs[:, fo + nb:fo + 2 * nb]
        scl_b = bass.AP(s_sc.tensor, s_sc.offset, [list(s_sc.ap[0]), [1, nb], [0, 16]])
        rec_b = bass.AP(s_rc.tensor, s_rc.offset, [list(s_rc.ap[0]), [1, nb], [0, 16]])
        t13 = t1_ap.rearrange("p (a b) -> p a b", b=16)
        self._e(eng, lambda h, d=t13, s3=src3, r=rec_b: h.tensor_tensor(d, s3, r, ALU.mult),
               list(src_bufs) + [fb], list(t1_bufs))
        self.sc(lambda h, d=t2_ap, s_=t1_ap: h.activation(d, s_, AF.Abs),
                list(t1_bufs), list(t2_bufs))
        self._e(eng, lambda h, d=dst_ap, a=t2_ap: h.tensor_scalar(
            d, a, 5.0, 2.0, ALU.is_gt, ALU.mult), list(t2_bufs), list(dst_bufs))
        for m in (2.5, 3.5):
            self._e(eng, lambda h, d=dst_ap, a=t2_ap, m=m: h.scalar_tensor_tensor(
                d, a, m, dst_ap, ALU.is_gt, ALU.add),
                list(t2_bufs) + list(dst_bufs), list(dst_bufs))
        self._e(eng, lambda h, d=dst_ap: h.tensor_scalar(
            d, d, 2.0, None, ALU.mult), list(dst_bufs), list(dst_bufs))
        for m in (0.25, 0.75, 1.25, 1.75):
            self._e(eng, lambda h, d=dst_ap, a=t2_ap, m=m: h.scalar_tensor_tensor(
                d, a, m, dst_ap, ALU.is_gt, ALU.add),
                list(t2_bufs) + list(dst_bufs), list(dst_bufs))
        self.sc(lambda h, d=t1_ap: h.activation(d, d, AF.Sign),
                list(t1_bufs), list(t1_bufs))
        self._e(eng, lambda h, d=dst_ap, s_=t1_ap: h.tensor_tensor(d, d, s_, ALU.mult),
               list(dst_bufs) + list(t1_bufs), list(dst_bufs))
        d3 = dst_ap.rearrange("p (a b) -> p a b", b=16)
        self._e(eng, lambda h, d=d3, s_=scl_b: h.tensor_tensor(d, d, s_, ALU.mult),
               list(dst_bufs) + [fb], list(dst_bufs))

    def ln_group(self, tis, dsts, sq_ap, sq_bufs):
        g0 = tis[0]
        n = len(tis)
        assert list(tis) == list(range(g0, g0 + n)) and n <= 4
        st = self.stat_h
        stb = self.stat_b[0]
        x4 = self.xh[:, g0 * 1024:(g0 + n) * 1024]
        xbufs = [self.xb[t] for t in tis]
        self.v(lambda h, x=x4, st=st, n=n: h.tensor_reduce(
            st[:, 0:n].unsqueeze(-1), x.rearrange("p (a b) -> p a b", b=1024),
            AX.X, ALU.add), xbufs, [stb])
        self.v(lambda h, st=st, n=n: h.memset(st[:, 4:4 + n], 0.0), [], [stb])
        for i, t in enumerate(tis):
            self.sc(lambda h, x=self.xh[:, t * 1024:(t + 1) * 1024], sa=sq_ap,
                    st=st, i=i: h.activation(sa, x, AF.Square,
                                             accum_out=st[:, 4 + i:5 + i]),
                    [self.xb[t]], list(sq_bufs) + [stb])
        self.v(lambda h, st=st, n=n: h.tensor_scalar(
            st[:, 8:8 + n], st[:, 0:n], 1.0 / 1024.0, None, ALU.mult), [stb], [stb])
        self.v(lambda h, st=st, n=n: h.tensor_tensor(
            st[:, 20:20 + n], st[:, 8:8 + n], st[:, 8:8 + n], ALU.mult), [stb], [stb])
        self.v(lambda h, st=st, n=n: h.scalar_tensor_tensor(
            st[:, 16:16 + n], st[:, 4:4 + n], 1.0 / 1024.0, st[:, 20:20 + n],
            ALU.mult, ALU.subtract), [stb], [stb])
        self.v(lambda h, st=st, n=n: h.tensor_scalar(
            st[:, 16:16 + n], st[:, 16:16 + n], 1e-5, None, ALU.add), [stb], [stb])
        self.sc(lambda h, st=st, n=n: h.activation(
            st[:, 20:20 + n], st[:, 16:16 + n], AF.Sqrt), [stb], [stb])
        self.v(lambda h, st=st, n=n: h.reciprocal(
            st[:, 12:12 + n], st[:, 20:20 + n]), [stb], [stb])
        for i, t in enumerate(tis):
            d_ap, d_bufs = dsts[i]
            self.v(lambda h, x=self.xh[:, t * 1024:(t + 1) * 1024], d=d_ap,
                   st=st, i=i: h.tensor_scalar(
                d, x, st[:, 8 + i:9 + i], st[:, 12 + i:13 + i],
                ALU.subtract, ALU.mult), [self.xb[t], stb], list(d_bufs))

    def ffn0_weights(self):
        A = self.aview
        SRCR = [A(0, 4096), A(4096, 4096)]
        T1, T1b = A(8192, 4096)
        T2, T2b = A(12288, 4096)
        STG, STGb = A(16384, 2048)
        fs_i = 0
        sslot = 0
        for wname, wdst, wdb, nt_rows, wcols, bat in (
                ("wg0_nat", self.wgqT_d, self.wgqT_db, 22, 1024, 4),
                ("wu0_nat", self.wuqT_d, self.wuqT_db, 22, 1024, 4),
                ("wd0_nat", self.wdqT_d, self.wdqT_db, 8, 2816, 1)):
            src_t = self.ins[wname]
            for nt0 in range(0, nt_rows, bat):
                nb_ = min(bat, nt_rows - nt0)
                W = wcols * nb_
                SRC, SRCb = SRCR[sslot]
                sslot ^= 1
                if nb_ == 1:
                    self.dma("sync", SRC[:, 0:W],
                             src_t.ap()[nt0 * 128:(nt0 + 1) * 128, :],
                             [self.in_bufs[wname]], SRCb)
                else:
                    sap = src_t.ap()[nt0 * 128:(nt0 + nb_) * 128, :].rearrange(
                        "(a b) c -> b a c", a=nb_)
                    dap = SRC[:, 0:W].rearrange("p (a c) -> p a c", c=wcols)
                    self.dma("sync", dap, sap, [self.in_bufs[wname]], SRCb)
                self.fp4(SRC[:, 0:W], SRCb, SRC[:, 0:W], SRCb, W, fs_i,
                         T1[:, 0:W], T1b, T2[:, 0:W], T2b,
                         eng="vector")
                fs_i += 1
                for bi in range(nb_):
                    nt = nt0 + bi
                    nck = wcols // 128
                    for ck in range(nck):
                        bank = 2 + (ck % 2)
                        self.tr(bank, self.pap(bank, 128, 128),
                                SRC[:, bi * wcols + ck * 128:
                                    bi * wcols + ck * 128 + 128], SRCb)
                        if wcols == 1024:
                            sg = STG[:, ck * 256 + (nt % 2) * 128:
                                     ck * 256 + (nt % 2) * 128 + 128]
                        else:
                            sg = STG[:, (ck % 4) * 128:(ck % 4) * 128 + 128]
                        self.sc(lambda h, d=sg, p=self.pap(bank, 128, 128):
                               h.copy(d, p), [self.pb(bank)], STGb)
                        if wcols != 1024:
                            row = (nt // 4) * 22 + ck
                            col = (nt % 4) * 128
                            self.dma("gpsimd",
                                     wdst.ap()[row:row + 1, :, :].squeeze(0)
                                     [:, col:col + 128],
                                     sg, STGb, [wdb])
                    if wcols == 1024 and nt % 2 == 1:
                        self.dma("gpsimd",
                                 wdst.ap()[nt // 2:nt // 2 + 1, :, :].squeeze(0),
                                 STG[:, 0:2048], STGb, [wdb])

    def ffn0(self):
        if not self.l0w_done:
            self.ffn0_weights()
            self.l0w_done = True
        A = self.aview
        XQT, XQTb = A(0, 4096)
        HQT, HQTb = A(4096, 11264)
        SG = [A(15360 + i * 512, 512) for i in range(4)]
        SGall, SGallb = A(15360, 2048)
        HS1, HS1b = A(17408, 1024)
        HS2, HS2b = A(18432, 1024)
        T1, T1b = A(0, 4096)
        T2, T2b = A(4096, 4096)
        fs_i = 0

        for tc in range(4):
            tis = list(range(tc * 4, tc * 4 + 4))
            dsts = [(self.xhat_h[:, i * 1024:(i + 1) * 1024], [self.xhat_b[i]])
                    for i in range(4)]
            self.ln_group(tis, dsts, T2[:, 0:1024], T2b)
            xall = self.xhat_h[:, 0:4096]
            xallb = list(self.xhat_b)
            self.fp4(xall, xallb, xall, xallb, 4096, fs_i, T1, T1b, T2, T2b)
            fs_i += 1
            for tt in range(4):
                for hc in range(8):
                    bank = 2 + (hc % 2)
                    self.tr(bank, self.pap(bank, 128, 128),
                            self.xhat_h[:, tt * 1024 + hc * 128:
                                        tt * 1024 + hc * 128 + 128],
                            [self.xhat_b[tt]])
                    dcol = hc * 512 + tt * 128
                    self.sc(lambda h, d=XQT[:, dcol:dcol + 128],
                           p=self.pap(bank, 128, 128): h.copy(d, p),
                           [self.pb(bank)], [XQTb[dcol // 512]])
            lidx = 0
            for nt2 in range(11):
                off = (nt2 % 2) * 256
                for proj, wsrc, wsb in (("g", self.wgqT_d, self.wgqT_db),
                                        ("u", self.wuqT_d, self.wuqT_db)):
                    par = lidx % 2
                    lidx += 1
                    base = par * 2048
                    for s in range(4):
                        sl = par * 4 + s
                        self.dma("sync", self.wp_h[:, sl * 512:(sl + 1) * 512],
                                 wsrc.ap()[nt2:nt2 + 1, :,
                                           s * 512:(s + 1) * 512].squeeze(0),
                                 [wsb], [self.wp_b[sl]])
                    for tt in range(4):
                        pairs = [(XQT[:, hc * 512 + tt * 128:
                                      hc * 512 + tt * 128 + 128],
                                  self.wp_h[:, base + hc * 256:
                                            base + hc * 256 + 256])
                                 for hc in range(8)]
                        self.mm(tt % 2, self.pap(tt % 2, 128, 256), pairs,
                                [XQTb[hc] for hc in range(8)] +
                                [self.wp_b[par * 4 + s] for s in range(4)])
                        sg_ap, sg_b = SG[tt]
                        if proj == "g":
                            self.sc(lambda h, d=sg_ap[:, off:off + 256],
                                    p=self.pap(tt % 2, 128, 256):
                                    h.activation(d, p, AF.Silu),
                                    [self.pb(tt % 2)], sg_b)
                        else:
                            self.v(lambda h, d=sg_ap[:, off:off + 256],
                                   p=self.pap(tt % 2, 128, 256):
                                   h.tensor_tensor(d, d, p, ALU.mult),
                                   list(sg_b) + [self.pb(tt % 2)], sg_b)
                if nt2 % 2 == 1:
                    nb = nt2 // 2
                    for pr in range(2):
                        pap_ = SGall[:, pr * 1024:(pr + 1) * 1024]
                        pbufs = list(SG[pr * 2][1]) + list(SG[pr * 2 + 1][1])
                        self.fp4(pap_, pbufs, pap_, pbufs, 1024, fs_i,
                                 HS1, HS1b, HS2, HS2b,
                                 eng="vector")
                        fs_i += 1
                    for tt in range(4):
                        sg_ap, sg_b = SG[tt]
                        for q in range(4):
                            bank = 2 + (q % 2)
                            self.tr(bank, self.pap(bank, 128, 128),
                                    sg_ap[:, q * 128:(q + 1) * 128], sg_b)
                            dcol = (nb * 4 + q) * 512 + tt * 128
                            self.sc(lambda h, d=HQT[:, dcol:dcol + 128],
                                   p=self.pap(bank, 128, 128): h.copy(d, p),
                                   [self.pb(bank)], [HQTb[dcol // 512]])
                elif nt2 == 10:
                    for tt in range(4):
                        sg_ap, sg_b = SG[tt]
                        self.fp4(sg_ap[:, 0:256], sg_b, sg_ap[:, 0:256], sg_b,
                                 256, fs_i, HS1[:, 0:256], HS1b,
                                 HS2[:, 0:256], HS2b)
                        fs_i += 1
                    for tt in range(4):
                        sg_ap, sg_b = SG[tt]
                        for q in range(2):
                            bank = 2 + (q % 2)
                            self.tr(bank, self.pap(bank, 128, 128),
                                    sg_ap[:, q * 128:(q + 1) * 128], sg_b)
                            dcol = (5 * 4 + q) * 512 + tt * 128
                            self.sc(lambda h, d=HQT[:, dcol:dcol + 128],
                                   p=self.pap(bank, 128, 128): h.copy(d, p),
                                   [self.pb(bank)], [HQTb[dcol // 512]])
            for fh in range(2):
                for ic in range(22):
                    slot = ic % 4
                    self.dma("sync", self.wp_h[:, slot * 512:(slot + 1) * 512],
                             self.wdqT_d.ap()[fh * 22 + ic:fh * 22 + ic + 1,
                                              :, :].squeeze(0),
                             [self.wdqT_db], [self.wp_b[slot]])
                    for tt in range(4):
                        lcol = ic * 512 + tt * 128
                        self.mm(4 + tt, self.pap(4 + tt),
                                [(HQT[:, lcol:lcol + 128],
                                  self.wp_h[:, slot * 512:(slot + 1) * 512])],
                                [HQTb[lcol // 512], self.wp_b[slot]],
                                start=(ic == 0), stop=(ic == 21))
                for tt in range(4):
                    ti = tc * 4 + tt
                    x_ap = self.xh[:, ti * 1024 + fh * 512: ti * 1024 + fh * 512 + 512]
                    self.v(lambda h, x=x_ap, p=self.pap(4 + tt):
                           h.scalar_tensor_tensor(x, p, 0.5, x, ALU.mult, ALU.add),
                           [self.pb(4 + tt), self.xb[ti]], [self.xb[ti]])

    def att0(self):
        A = self.aview
        XHT, XHTb = A(0, 2048)
        QT, QTb = A(2048, 2048)
        KT, KTb = A(4096, 2048)
        Y, Yb = A(6144, 1024)
        YT, YTb = A(7168, 1024)
        BMG, BMGb = A(8192, 512)
        LL, LLb = A(8704, 512)
        BOUT, BOUTb = A(9216, 1024)
        LNG, LNGb = A(10240, 1024)
        LNB, LNBb = A(11264, 1024)
        FS, FSb = A(12288, 1024)
        VO, VOb = A(13312, 1152)
        DN, DNb = A(14464, 1152)
        ST, STb = A(15616, 2064)
        self._att0_regs = (XHT, XHTb, QT, QTb, KT, KTb, Y, Yb, YT, YTb, BMG,
                           BMGb, LL, LLb, BOUT, BOUTb, LNG, LNGb, LNB, LNBb,
                           FS, FSb, VO, VOb, DN, DNb, ST, STb)

        for name, ap_, b_ in (("bout_rep", BOUT, BOUTb), ("lng_rep", LNG, LNGb),
                              ("lnb_rep", LNB, LNBb)):
            self.dma("sync", ap_, self.ins[name].ap(), [self.in_bufs[name]], b_)
        self.dma("sync", self.sm_h[:, 0:40], self.ins["bmg_bias"].ap(),
                 [self.in_bufs["bmg_bias"]], [self.sm_b[0], self.sm_b[1]])
        self.v(lambda h, a=ST: h.memset(a, 0.0), [], STb)
        vco = VO[:, 0:8 * 129].rearrange("p (a b) -> p a b", b=129)[:, :, 128:129]
        self.v(lambda h, a=vco: h.memset(a, 1.0), [], VOb)

        for g in range(8):
            for hc in range(8):
                self.dma("sync", self.wp_h[:, 3584 + hc * 64: 3584 + hc * 64 + 40],
                         self.ins["wbmgT"].ap()[hc * 128:(hc + 1) * 128, :],
                         [self.in_bufs["wbmgT"]], [self.wp_b[7]])
            for cc in range(2):
                ti = g * 2 + cc
                xh_ap = self.xhat_h[:, cc * 1024:(cc + 1) * 1024]
                xh_buf = self.xhat_b[cc]
                self.ln_tile(ti, xh_ap, [xh_buf], cc, FS, FSb)
                for hc in range(8):
                    self.tr(1, self.pap(1, 128, 128),
                            xh_ap[:, hc * 128:(hc + 1) * 128], [xh_buf])
                    dcol = hc * 256 + cc * 128
                    self.sc(lambda h, d=XHT[:, dcol:dcol + 128],
                           p=self.pap(1, 128, 128): h.copy(d, p),
                           [self.pb(1)], [XHTb[dcol // 512]])
            lq = 0
            for wname, REG, REGb in (("wqT0", QT, QTb), ("wkT0", KT, KTb)):
                wt = self.ins[wname]
                wdb_ = self.in_bufs[wname]
                for ft in range(8):
                    par = lq % 2
                    lq += 1
                    base = par * 1024
                    pbufs = [self.wp_b[par * 2], self.wp_b[par * 2 + 1]]
                    for s2 in range(2):
                        sl = par * 2 + s2
                        self.dma("sync", self.wp_h[:, sl * 512:(sl + 1) * 512],
                                 wt.ap()[ft:ft + 1, :,
                                         s2 * 512:(s2 + 1) * 512].squeeze(0),
                                 [wdb_], [self.wp_b[sl]])
                    for hc in range(8):
                        self.mm(0, self.pap(0, 128, 256),
                                [(self.wp_h[:, base + hc * 128:
                                            base + hc * 128 + 128],
                                  XHT[:, hc * 256:hc * 256 + 256])],
                                pbufs + list(XHTb),
                                start=(hc == 0), stop=(hc == 7))
                    self.v(lambda h, d=FS[:, 0:256], p=self.pap(0, 128, 256):
                           h.tensor_scalar(d, p, 0.0, None, ALU.min),
                           [self.pb(0)], FSb)
                    self.sc(lambda h, d=FS[:, 0:256]: h.activation(d, d, AF.Exp),
                            FSb, FSb)
                    self.v(lambda h, d=FS[:, 256:512], p=self.pap(0, 128, 256):
                           h.tensor_scalar(d, p, 0.0, None, ALU.max),
                           [self.pb(0)], FSb)
                    self.v(lambda h, d=REG[:, ft * 256:(ft + 1) * 256],
                           a=FS[:, 0:256], b2=FS[:, 256:512]:
                           h.tensor_tensor(d, a, b2, ALU.add),
                           FSb, [REGb[(ft * 256) // 512]])
            for cc in range(2):
                pairs = [(XHT[:, hc * 256 + cc * 128: hc * 256 + cc * 128 + 128],
                          self.wp_h[:, 3584 + hc * 64: 3584 + hc * 64 + 40])
                         for hc in range(8)]
                self.mm(5, self.pap(5, 128, 40), pairs,
                        list(XHTb) + [self.wp_b[7]])
                bo = cc * 64
                lo = cc * 64
                self.v(lambda h, d=BMG[:, bo:bo + 40], p=self.pap(5, 128, 40),
                       bb=self.sm_h[:, 0:40]: h.tensor_tensor(d, p, bb, ALU.add),
                       [self.pb(5), self.sm_b[0], self.sm_b[1]], BMGb)
                self.sc(lambda h, d=LL[:, lo:lo + 16], s_=BMG[:, bo:bo + 16]:
                        h.activation(d, s_, AF.Sigmoid), BMGb, LLb)
                self.v(lambda h, d=LL[:, lo:lo + 16], r=self.c_rrep[:, 0:16]:
                       h.tensor_tensor(d, d, r, ALU.mult),
                       LLb + [self.c_rrep_b], LLb)
                self.v(lambda h, d=LL[:, lo:lo + 16]:
                       h.tensor_scalar(d, d, 0.9999, None, ALU.min), LLb, LLb)
                self.sc(lambda h, d=LL[:, lo:lo + 16]: h.activation(d, d, AF.Ln),
                        LLb, LLb)
                self.sc(lambda h, d=LL[:, lo + 32:lo + 40],
                        s_=BMG[:, bo + 32:bo + 40]:
                        h.activation(d, s_, AF.Sigmoid), BMGb, LLb)
                me = BMG[:, bo + 16:bo + 32:2]
                mo = BMG[:, bo + 17:bo + 32:2]
                self.v(lambda h, d=LL[:, lo + 56:lo + 64], a=me, b2=mo:
                       h.tensor_tensor(d, a, b2, ALU.subtract), BMGb, LLb)
                self.sc(lambda h, d=LL[:, lo + 40:lo + 55:2],
                        s_=LL[:, lo + 56:lo + 64]:
                        h.activation(d, s_, AF.Sigmoid), LLb, LLb)
                self.sc(lambda h, d=LL[:, lo + 41:lo + 56:2],
                        s_=LL[:, lo + 56:lo + 64]:
                        h.activation(d, s_, AF.Sigmoid, scale=-1.0), LLb, LLb)
            for cc in range(2):
                self.att0_chunk(g * 2 + cc, cc)

    def att0_chunk(self, c, cc):
        (XHT, XHTb, QT, QTb, KT, KTb, Y, Yb, YT, YTb, BMG, BMGb, LL, LLb,
         BOUT, BOUTb, LNG, LNGb, LNB, LNBb, FS, FSb, VO, VOb, DN, DNb,
         ST, STb) = self._att0_regs
        lo = cc * 64
        for fh in range(2):
            for sl in range(8):
                self.dma("gpsimd", self.wp_h[:, sl * 512:(sl + 1) * 512],
                         self.ins["wvT0"].ap()[fh:fh + 1, :,
                                               sl * 512:(sl + 1) * 512].squeeze(0),
                         [self.in_bufs["wvT0"]], [self.wp_b[sl]])
            for hc in range(8):
                self.mm(4, self.pap(4),
                        [(XHT[:, hc * 256 + cc * 128: hc * 256 + cc * 128 + 128],
                          self.wp_h[:, hc * 512:(hc + 1) * 512])],
                        list(XHTb) + [self.wp_b[hc]],
                        start=(hc == 0), stop=(hc == 7))
            for hd in range(4):
                hh = fh * 4 + hd
                self.sc(lambda h, d=VO[:, hh * 129:hh * 129 + 128],
                       p=self.psum_h[4][0:128, hd * 128:(hd + 1) * 128]:
                       h.copy(d, p), [self.pb(4)], VOb)
        llc = LL[:, lo:lo + 16]
        self.mm(5, self.pap(5, 128, 16), [(self.c_tri[:, :], llc)],
                [self.c_tri_b] + LLb)
        self.sc(lambda h, d=DN[:, 1024 + 16 * cc:1040 + 16 * cc],
                p=self.pap(5, 128, 16):
                h.copy(d, p), [self.pb(5)], DNb)
        self.att0_sc_ktr(cc, 0)
        for hd in range(8):
            if hd + 1 < 8:
                self.att0_sc_ktr(cc, hd + 1)
            self.att0_head(c, cc, hd)
        for hc in range(8):
            self.tr(6, self.pap(6, 128, 128), Y[:, hc * 128:(hc + 1) * 128], Yb)
            self.sc(lambda h, d=YT[:, hc * 128:(hc + 1) * 128],
                   p=self.pap(6, 128, 128): h.copy(d, p),
                   [self.pb(6)], [YTb[hc // 4]])
        for fh in range(2):
            for sl in range(8):
                self.dma("gpsimd", self.wp_h[:, sl * 512:(sl + 1) * 512],
                         self.ins["woutT0"].ap()[fh:fh + 1, :,
                                                 sl * 512:(sl + 1) * 512].squeeze(0),
                         [self.in_bufs["woutT0"]], [self.wp_b[sl]])
            for hc in range(8):
                self.mm(7, self.pap(7),
                        [(YT[:, hc * 128:(hc + 1) * 128],
                          self.wp_h[:, hc * 512:(hc + 1) * 512])],
                        list(YTb) + [self.wp_b[hc]],
                        start=(hc == 0), stop=(hc == 7))
            xh_ap = self.xhat_h[:, cc * 1024 + fh * 512: cc * 1024 + fh * 512 + 512]
            self.v(lambda h, d=FS[:, fh * 512:(fh + 1) * 512], p=self.pap(7),
                   b2=BOUT[:, fh * 512:(fh + 1) * 512]:
                   h.tensor_tensor(d, p, b2, ALU.add),
                   [self.pb(7)] + BOUTb, FSb)
            self.v(lambda h, d=FS[:, fh * 512:(fh + 1) * 512], a=xh_ap:
                   h.tensor_tensor(d, d, a, ALU.add),
                   FSb + [self.xhat_b[cc]], FSb)
        self.ln_generic(FS, FSb, FS, FSb, cc, Y, Yb)
        self.v(lambda h, d=FS, g2=LNG: h.tensor_tensor(d, d, g2, ALU.mult),
               FSb + LNGb, FSb)
        self.v(lambda h, d=FS, b2=LNB: h.tensor_tensor(d, d, b2, ALU.add),
               FSb + LNBb, FSb)
        x_ap = self.xh[:, c * 1024:(c + 1) * 1024]
        self.v(lambda h, x=x_ap, r=FS: h.tensor_tensor(x, x, r, ALU.add),
               FSb + [self.xb[c]], [self.xb[c]])

    def att0_sc_ktr(self, cc, hd):
        (XHT, XHTb, QT, QTb, KT, KTb, Y, Yb, YT, YTb, BMG, BMGb, LL, LLb,
         BOUT, BOUTb, LNG, LNGb, LNB, LNBb, FS, FSb, VO, VOb, DN, DNb,
         ST, STb) = self._att0_regs
        kc = KT[:, hd * 256 + cc * 128: hd * 256 + cc * 128 + 128]
        qc = QT[:, hd * 256 + cc * 128: hd * 256 + cc * 128 + 128]
        kb = [KTb[(hd * 256) // 512]]
        qb = [QTb[(hd * 256) // 512]]
        self.mm(hd % 2, self.pap(hd % 2, 128, 128), [(kc, qc)], kb + qb)
        self.tr(2 + hd % 2, self.pap(2 + hd % 2, 128, 128), kc, kb)

    def att0_head(self, c, cc, hd):
        (XHT, XHTb, QT, QTb, KT, KTb, Y, Yb, YT, YTb, BMG, BMGb, LL, LLb,
         BOUT, BOUTb, LNG, LNGb, LNB, LNBb, FS, FSb, VO, VOb, DN, DNb,
         ST, STb) = self._att0_regs
        lo = cc * 64
        qc = QT[:, hd * 256 + cc * 128: hd * 256 + cc * 128 + 128]
        qb = [QTb[(hd * 256) // 512]]
        voh = VO[:, hd * 129:(hd + 1) * 129]
        gcol = LL[:, lo + 32 + hd:lo + 33 + hd]
        bSC = hd % 2
        bKT = 2 + hd % 2
        b4 = 4 + hd % 2
        b6 = 6 + hd % 2
        EA2 = DN[:, 0:256]
        MT2 = DN[:, 256:512]
        QP2 = DN[:, 512:768]
        KP2 = DN[:, 768:1024]
        MT = [DN[:, 256:384], DN[:, 384:512]]
        QP = [DN[:, 512:640], DN[:, 640:768]]
        KP = [DN[:, 768:896], DN[:, 896:1024]]
        acol2 = DN[:, 1024 + 16 * cc + 2 * hd:1026 + 16 * cc + 2 * hd]
        cs2 = DN[:, 1056:1058]
        dn2 = DN[:, 1058:1060]
        f2 = DN[:, 1060:1062]

        def j3(ap2, n=128):
            return ap2.rearrange("p (j c) -> p j c", c=n)

        def jbc(ap_, n=128):
            return bass.AP(ap_.tensor, ap_.offset,
                           [list(ap_.ap[0]), [0, 2], [1, n]])

        def ibc(ap_, n=128):
            return bass.AP(ap_.tensor, ap_.offset,
                           [list(ap_.ap[0]), [1, 2], [0, n]])

        llpair = LL[:, lo + 2 * hd:lo + 2 * hd + 2]
        self.v(lambda h, d=j3(KP2), t=jbc(self.c_tri[:, :]), l=ibc(llpair):
               h.tensor_tensor(d, t, l, ALU.mult),
               [self.c_tri_b] + LLb, DNb)
        for j in range(2):
            self.mm(b4, self.psum_h[b4][0:128, j * 128:(j + 1) * 128],
                    [(self.c_allones[:, :], KP[j])],
                    [self.c_allones_b] + DNb)
        for j in range(2):
            self.mm(b6, self.psum_h[b6][0:128, j * 128:(j + 1) * 128],
                    [(self.c_allones[:, :], KP[j]),
                     (KP[j], self.c_negones[:, :])],
                    [self.c_allones_b, self.c_negones_b] + DNb)
        self.sc(lambda h, d=EA2, p=self.psum_h[b4][0:128, 0:256]:
                h.activation(d, p, AF.Exp), [self.pb(b4)], DNb)
        self.v(lambda h, d=j3(MT2), p=j3(self.psum_h[b6][0:128, 0:256]),
               m=jbc(self.c_maskst[:, :]):
               h.tensor_tensor(d, p, m, ALU.add),
               [self.pb(b6), self.c_maskst_b], DNb)
        self.sc(lambda h, d=MT2: h.activation(d, d, AF.Exp), DNb, DNb)
        self.v(lambda h, d=MT2, g2=gcol:
               h.tensor_scalar(d, d, g2, None, ALU.mult), DNb + LLb, DNb)
        self.v(lambda h, d=j3(MT2), p=jbc(self.pap(bSC, 128, 128)):
               h.tensor_tensor(d, d, p, ALU.mult), DNb + [self.pb(bSC)], DNb)
        self.v(lambda h, d=j3(QP2), q=jbc(qc), e=j3(EA2):
               h.tensor_tensor(d, q, e, ALU.mult), qb + DNb, DNb)
        for j in range(2):
            sthj = ST[:, (2 * hd + j) * 129:(2 * hd + j + 1) * 129]
            self.mm(b4, self.psum_h[b4][0:128, j * 129:j * 129 + 129],
                    [(MT[j], voh), (QP[j], sthj)], DNb + VOb + STb)
        self.sc(lambda h, d=cs2, a=acol2: h.activation(d, a, AF.Exp, scale=-1.0),
                DNb, DNb)
        EA0 = DN[:, 0:128]
        ea_last = bass.AP(EA0.tensor, EA0.offset + 127,
                          [list(EA0.ap[0]), [128, 2]])
        self.v(lambda h, d=cs2, e=ea_last: h.tensor_tensor(d, d, e, ALU.mult),
               DNb, DNb)
        self.v(lambda h, d=cs2, g2=gcol: h.tensor_scalar(d, d, g2, None, ALU.mult),
               DNb + LLb, DNb)
        self.v(lambda h, d=j3(KP2), p=jbc(self.pap(bKT, 128, 128)), cs=ibc(cs2):
               h.tensor_tensor(d, p, cs, ALU.mult),
               [self.pb(bKT)] + DNb, DNb)
        for j in range(2):
            self.mm(b6, self.psum_h[b6][0:128, j * 129:j * 129 + 129],
                    [(KP[j], voh)], DNb + VOb)
        for j in range(2):
            sthj = ST[:, (2 * hd + j) * 129:(2 * hd + j + 1) * 129]
            self.v(lambda h, s_=sthj, el=ea_last[:, j:j + 1],
                   p=self.psum_h[b6][0:128, j * 129:j * 129 + 129]:
                   h.scalar_tensor_tensor(s_, s_, el, p, ALU.mult, ALU.add),
                   STb + DNb + [self.pb(b6)], STb)
        pd = self.psum_h[b4][0:128, 128:129]
        pd3 = bass.AP(pd.tensor, pd.offset, [list(pd.ap[0]), [129, 2], [1, 1]])
        self.v(lambda h, d=dn2.rearrange("p (j c) -> p j c", c=1), p=pd3:
               h.tensor_scalar(d, p, 1e-6, None, ALU.add), [self.pb(b4)], DNb)
        self.v(lambda h, d=f2, s_=dn2: h.reciprocal(d, s_), DNb, DNb)
        self.v(lambda h, d=f2, w=LL[:, lo + 40 + 2 * hd:lo + 42 + 2 * hd]:
               h.tensor_tensor(d, d, w, ALU.mult), DNb + LLb, DNb)
        yh = Y[:, hd * 128:(hd + 1) * 128]
        self.v(lambda h, d=yh, p=self.psum_h[b4][0:128, 0:128], f=f2[:, 0:1]:
               h.tensor_scalar(d, p, f, None, ALU.mult),
               [self.pb(b4)] + DNb, Yb)
        self.v(lambda h, d=yh, p=self.psum_h[b4][0:128, 129:257], f=f2[:, 1:2]:
               h.scalar_tensor_tensor(d, p, f, d, ALU.mult, ALU.add),
               [self.pb(b4)] + DNb + Yb, Yb)

    def ffn1_weights(self):
        A = self.aview
        SRC, SRCb = A(0, 4096)
        T1, T1b = A(4096, 4096)
        DSTI, DSTIb = A(8192, 4096)
        STGS = [A(12288, 1408), A(13696, 1408)]
        for wname, wdst, wdb, nt_rows, wcols, smo, bat in (
                ("wg1_nat", self.iwgT_d, self.iwgT_db, 22, 1024, 64, 4),
                ("wu1_nat", self.iwuT_d, self.iwuT_db, 22, 1024, 96, 4),
                ("wd1_nat", self.iwdT_d, self.iwdT_db, 8, 2816, 128, 1)):
            src_t = self.ins[wname]
            smb = self.sm_b[smo // 32]
            for nt0 in range(0, nt_rows, bat):
                nb_ = min(bat, nt_rows - nt0)
                W = wcols * nb_
                if nb_ == 1:
                    self.dma("sync", SRC[:, 0:W],
                             src_t.ap()[nt0 * 128:(nt0 + 1) * 128, :],
                             [self.in_bufs[wname]], SRCb)
                else:
                    sap = src_t.ap()[nt0 * 128:(nt0 + nb_) * 128, :].rearrange(
                        "(a b) c -> b a c", a=nb_)
                    dap = SRC[:, 0:W].rearrange("p (a c) -> p a c", c=wcols)
                    self.dma("sync", dap, sap, [self.in_bufs[wname]], SRCb)
                sma = self.sm_h[:, smo + nt0:smo + nt0 + nb_]
                src3 = SRC[:, 0:W].rearrange("p (a c) -> p a c", c=wcols)
                self.v(lambda h, d=sma, s3=src3:
                       h.tensor_reduce(d.unsqueeze(-1), s3,
                                       AX.X, ALU.max, apply_absolute_value=True),
                       SRCb, [smb])
                self.v(lambda h, d=sma: h.tensor_scalar(
                    d, d, 1.0 / 127.0, None, ALU.mult), [smb], [smb])
                rc = self.sm_h[:, 160:160 + nb_]
                self.v(lambda h, d=rc, s_=sma: h.reciprocal(d, s_),
                       [smb], [self.sm_b[5]])
                rc_b = bass.AP(rc.tensor, rc.offset,
                               [list(rc.ap[0]), [1, nb_], [0, wcols]])
                t13 = T1[:, 0:W].rearrange("p (a c) -> p a c", c=wcols)
                self.v(lambda h, d=t13, s3=src3, r=rc_b:
                       h.tensor_tensor(d, s3, r, ALU.mult),
                       SRCb + [self.sm_b[5]], T1b)
                self.v(lambda h, d=T1[:, 0:W]:
                       h.tensor_scalar(d, d, -128.0, 127.0, ALU.max, ALU.min),
                       T1b, T1b)
                self.v(lambda h, d=SRC[:, 0:W].bitcast(I32), s_=T1[:, 0:W]:
                       h.tensor_copy(d, s_), T1b, SRCb)
                self.v(lambda h, d=DSTI[:, 0:W], s_=SRC[:, 0:W].bitcast(I32):
                       h.tensor_copy(d, s_), SRCb, DSTIb)
                nck = wcols // 128
                for bi in range(nb_):
                    nt = nt0 + bi
                    st_ap, st_b = STGS[nt % 2]
                    stB = st_ap.bitcast(BF16)
                    for ck in range(nck):
                        bank = 2 + (ck % 2)
                        self.tr(bank, self.pap(bank, 128, 128),
                                DSTI[:, bi * wcols + ck * 128:
                                     bi * wcols + ck * 128 + 128], DSTIb)
                        sg = stB[:, ck * 128:(ck + 1) * 128]
                        self.sc(lambda h, d=sg, p=self.pap(bank, 128, 128):
                               h.copy(d, p), [self.pb(bank)], st_b)
                    self.dma("gpsimd",
                             wdst.ap()[nt:nt + 1, :, :].squeeze(0),
                             stB[:, 0:nck * 128], st_b, [wdb])

    def global_absmax(self, am_ap, am_buf, gco):
        self.tr(5, self.pap(5, 1, 128), am_ap, [am_buf])
        g1 = self.sm_h[0:1, 176:177]
        self.v(lambda h, d=g1, p=self.pap(5, 1, 128):
               h.tensor_reduce(d, p, AX.X, ALU.max), [self.pb(5)], [self.sm_b[5]])
        cst = self.sm_h[0:1, 192:208]
        self.v(lambda h, d=cst: h.memset(d, 0.0), [], [self.sm_b[6]])
        self.v(lambda h, d=self.sm_h[0:1, 192:193], s_=g1:
               h.tensor_copy(d, s_), [self.sm_b[5]], [self.sm_b[6]])
        self.dma("gpsimd", self.cc_in_d.ap(), cst, [self.sm_b[6]], [self.cc_in_db])
        self.gp(lambda h: h.collective_compute(
            "AllReduce", mybir.AluOpType.max, replica_groups=[list(range(8))],
            ins=[self.cc_in_d.ap().opt()], outs=[self.cc_out_d.ap().opt()]),
            [self.cc_in_db], [self.cc_out_db], sem_only=True)
        self.dma("gpsimd", self.sm_h[0:1, 208:224], self.cc_out_d.ap(),
                 [self.cc_out_db], [self.sm_b[6]])
        self.mm(5, self.pap(5, 128, 1),
                [(self.c_ones1[:, :], self.sm_h[0:1, 208:209])],
                [self.c_ones1_b, self.sm_b[6]])
        gc = self.sm_h[:, gco:gco + 1]
        self.sc(lambda h, d=gc, p=self.pap(5, 128, 1): h.copy(d, p),
               [self.pb(5)], [self.sm_b[7]])
        return gc, self.sm_b[7]

    def scale_from_clip(self, clip_name, gx, gxb, sxo, rsxo):
        clip = self.sm_h[0:1, 226:227]
        self.dma("sync", clip, self.ins[clip_name].ap(),
                 [self.in_bufs[clip_name]], [self.sm_b[7]])
        sgc = self.sm_h[0:1, 227:228]
        self.sc(lambda h, d=sgc, s_=clip: h.activation(d, s_, AF.Sigmoid),
                [self.sm_b[7]], [self.sm_b[7]])
        self.mm(5, self.pap(5, 128, 1), [(self.c_ones1[:, :], sgc)],
                [self.c_ones1_b, self.sm_b[7]])
        sx = self.sm_h[:, sxo:sxo + 1]
        rsx = self.sm_h[:, rsxo:rsxo + 1]
        self.v(lambda h, d=sx, p=self.pap(5, 128, 1), g2=gx:
               h.tensor_tensor(d, p, g2, ALU.mult),
               [self.pb(5), gxb], [self.sm_b[7]])
        self.v(lambda h, d=sx: h.tensor_scalar(d, sx, 1.0 / 127.0, None, ALU.mult),
               [self.sm_b[7]], [self.sm_b[7]])
        self.v(lambda h, d=rsx, s_=sx: h.reciprocal(d, s_),
               [self.sm_b[7]], [self.sm_b[7]])
        return sx, rsx, self.sm_b[7]

    def quant_int(self, src_ap, src_bufs, dst_ap, dst_bufs, rs_ap, rs_bufs,
                  t_ap, t_bufs, u_ap, u_bufs):
        self.v(lambda h, d=t_ap, s_=src_ap, r=rs_ap:
               h.tensor_scalar(d, s_, r, None, ALU.mult),
               list(src_bufs) + list(rs_bufs), list(t_bufs))
        self.v(lambda h, d=t_ap:
               h.tensor_scalar(d, t_ap, -128.0, 127.0, ALU.max, ALU.min),
               list(t_bufs), list(t_bufs))
        self.v(lambda h, d=u_ap.bitcast(I32), s_=t_ap:
               h.tensor_copy(d, s_), list(t_bufs), list(u_bufs))
        self.v(lambda h, d=dst_ap, s_=u_ap.bitcast(I32):
               h.tensor_copy(d, s_), list(u_bufs), list(dst_bufs))

    def ffn1(self):
        if not self.l1w_done:
            self.ffn1_weights()
            self.l1w_done = True
        A = self.aview
        XHT32, XHT32b = A(0, 16384)
        IXQT, IXQTb = A(0, 8192)
        IHQT, IHQTb = A(8192, 5632)
        FS0, FS0b = A(16384, 512)
        FS1, FS1b = A(16896, 512)
        FS2, FS2b = A(17408, 512)
        FS3, FS3b = A(17920, 512)
        SQ, SQb = A(18432, 1024)
        AMX = self.sm_h[:, 384:390]
        AMXb = [self.sm_b[12]]
        IXQTB = IXQT.bitcast(BF16)
        IHQTB = IHQT.bitcast(BF16)
        wpB = self.wp_h[:, :].bitcast(BF16)

        self.v(lambda h, d=AMX[:, 0:4]: h.memset(d, 0.0), [], AMXb)
        for g in range(4):
            tis = list(range(g * 4, g * 4 + 4))
            dsts = [(self.xhat_h[:, i * 1024:(i + 1) * 1024], [self.xhat_b[i]])
                    for i in range(4)]
            self.ln_group(tis, dsts, SQ, SQb)
            self.v(lambda h, d=AMX[:, 2:3], s_=self.xhat_h[:, 0:4096]:
                   h.tensor_reduce(
                d.unsqueeze(-1), s_.rearrange("p (a b) -> p a b", b=4096),
                AX.X, ALU.max, apply_absolute_value=True),
                list(self.xhat_b), AMXb)
            self.v(lambda h, d=AMX[:, 0:1], s_=AMX[:, 2:3]:
                   h.tensor_tensor(d, d, s_, ALU.max), AMXb, AMXb)
            for i, ti in enumerate(tis):
                xh_ap = self.xhat_h[:, i * 1024:(i + 1) * 1024]
                xh_buf = self.xhat_b[i]
                for hc in range(8):
                    bank = 6 + (hc % 2)
                    self.tr(bank, self.pap(bank, 128, 128),
                            xh_ap[:, hc * 128:(hc + 1) * 128], [xh_buf])
                    dcol = hc * 2048 + ti * 128
                    self.sc(lambda h, d=XHT32[:, dcol:dcol + 128],
                           p=self.pap(bank, 128, 128): h.copy(d, p),
                           [self.pb(bank)], [XHT32b[dcol // 512]])
        gx, gxb = self.global_absmax(AMX[:, 0:1], AMXb[0], 224)
        sx, rsx, sxb = self.scale_from_clip("clip_x", gx, gxb, 228, 229)
        for smo, dsto in ((64, 256), (96, 288)):
            self.v(lambda h, d=self.sm_h[:, dsto:dsto + 22],
                   s_=self.sm_h[:, smo:smo + 22], c2=sx:
                   h.tensor_scalar(d, s_, c2, None, ALU.mult),
                   [self.sm_b[smo // 32], sxb], [self.sm_b[dsto // 32]])
        for hc in range(8):
            for tq in range(4):
                co = hc * 2048 + tq * 512
                srcap = XHT32[:, co:co + 512]
                srcb = [XHT32b[co // 512]]
                dst = IXQTB[:, co:co + 512]
                dbufs = [IXQTb[(hc * 1024 + tq * 256) // 512]]
                fa, fab = (FS0, FS0b) if (hc * 4 + tq) % 2 == 0 else (FS2, FS2b)
                fb_, fbb = (FS1, FS1b) if (hc * 4 + tq) % 2 == 0 else (FS3, FS3b)
                self.quant_int(srcap, srcb, dst, dbufs, rsx, [sxb],
                               fa[:, 0:512], fab, fb_[:, 0:512], fbb)
        self.v(lambda h, d=AMX[:, 1:2]: h.memset(d, 0.0), [], AMXb)
        for nt in range(22):
            par = nt % 2
            goff = par * 1024
            uoff = 2048 + par * 1024
            for wd, wdb, off, slot in ((self.iwgT_d, self.iwgT_db, goff, par),
                                       (self.iwuT_d, self.iwuT_db, uoff, 2 + par)):
                srca = wd.ap()[nt:nt + 1, :, :].squeeze(0)
                self.dma("sync", wpB[:, off:off + 1024], srca,
                         [wdb], [self.wp_b[slot]])
            sg = self.sm_h[:, 256 + nt:257 + nt]
            su = self.sm_h[:, 288 + nt:289 + nt]
            for tq in range(4):
                fpar = tq % 2
                Fa, Fab = (FS0, FS0b) if fpar == 0 else (FS2, FS2b)
                Fb, Fbb = (FS1, FS1b) if fpar == 0 else (FS3, FS3b)
                gpairs = [(wpB[:, goff + hc * 128:goff + hc * 128 + 128],
                           IXQTB[:, hc * 2048 + tq * 512:hc * 2048 + tq * 512 + 512])
                          for hc in range(8)]
                upairs = [(wpB[:, uoff + hc * 128:uoff + hc * 128 + 128],
                           IXQTB[:, hc * 2048 + tq * 512:hc * 2048 + tq * 512 + 512])
                          for hc in range(8)]
                self.mm(fpar, self.pap(fpar), gpairs,
                        [self.wp_b[par]] + list(IXQTb))
                self.v(lambda h, d=Fa[:, 0:512], p=self.pap(fpar), s_=sg:
                       h.tensor_scalar(d, p, s_, None, ALU.mult),
                       [self.pb(fpar), self.sm_b[8]], Fab)
                self.sc(lambda h, d=Fb[:, 0:512], s_=Fa[:, 0:512]:
                        h.activation(d, s_, AF.Silu), Fab, Fbb)
                self.mm(2 + fpar, self.pap(2 + fpar), upairs,
                        [self.wp_b[2 + par]] + list(IXQTb))
                self.v(lambda h, d=Fa[:, 0:512], p=self.pap(2 + fpar), s_=su:
                       h.tensor_scalar(d, p, s_, None, ALU.mult),
                       [self.pb(2 + fpar), self.sm_b[9]], Fab)
                self.v(lambda h, d=Fa[:, 0:512], s_=Fb[:, 0:512]:
                       h.tensor_tensor(d, d, s_, ALU.mult), Fab + Fbb, Fab)
                self.v(lambda h, d=AMX[:, 2 + fpar:3 + fpar], s_=Fa[:, 0:512]:
                       h.tensor_reduce(d.unsqueeze(-1),
                                       s_.rearrange("p (a b) -> p a b", b=512),
                                       AX.X, ALU.max, apply_absolute_value=True),
                       Fab, AMXb)
                self.v(lambda h, d=AMX[:, 1:2], s_=AMX[:, 2 + fpar:3 + fpar]:
                       h.tensor_tensor(d, d, s_, ALU.max), AMXb, AMXb)
                self.dma("gpsimd",
                         self.hsp_d.ap()[nt * 128:(nt + 1) * 128,
                                         tq * 512:(tq + 1) * 512],
                         Fa[:, 0:512], Fab, [self.hsp_db])
        gh, ghb = self.global_absmax(AMX[:, 1:2], AMXb[0], 225)
        sh, rsh, shb = self.scale_from_clip("clip_d", gh, ghb, 230, 231)
        self.v(lambda h, d=self.sm_h[:, 320:328], s_=self.sm_h[:, 128:136], c2=sh:
               h.tensor_scalar(d, s_, c2, None, ALU.mult),
               [self.sm_b[4], shb], [self.sm_b[10]])
        for tq in range(4):
            for ic in range(22):
                fpar = ic % 2
                Fa, Fab = (FS0, FS0b) if fpar == 0 else (FS2, FS2b)
                Fb, Fbb = (FS1, FS1b) if fpar == 0 else (FS3, FS3b)
                self.dma("sync", Fa[:, 0:512],
                         self.hsp_d.ap()[ic * 128:(ic + 1) * 128,
                                         tq * 512:(tq + 1) * 512],
                         [self.hsp_db], Fab)
                dst = IHQTB[:, ic * 512:(ic + 1) * 512]
                dbufs = [IHQTb[(ic * 256) // 512]]
                self.quant_int(Fa[:, 0:512], Fab, dst, dbufs, rsh, [shb],
                               Fb[:, 0:512], Fbb, SQ[:, 0:512], SQb)
            for ft in range(8):
                par = ft % 2
                doff = par * 4096
                wslots = [self.wp_b[par * 4], self.wp_b[par * 4 + 1],
                          self.wp_b[par * 4 + 2]]
                srca = self.iwdT_d.ap()[ft:ft + 1, :, :].squeeze(0)
                self.dma("sync", wpB[:, doff:doff + 2816], srca,
                         [self.iwdT_db], wslots)
                pairs = [(wpB[:, doff + ic * 128:doff + ic * 128 + 128],
                          IHQTB[:, ic * 512:(ic + 1) * 512])
                         for ic in range(22)]
                self.mm(4 + par, self.pap(4 + par), pairs, wslots + list(IHQTb))
                scc = self.sm_h[:, 320 + ft:321 + ft]
                Fa, Fab = (FS0, FS0b) if par == 0 else (FS2, FS2b)
                self.v(lambda h, d=Fa[:, 0:512], p=self.pap(4 + par), s_=scc:
                       h.tensor_scalar(d, p, s_, None, ALU.mult),
                       [self.pb(4 + par), self.sm_b[10]], Fab)
                for q4 in range(4):
                    bank = 6 + (q4 % 2)
                    self.tr(bank, self.pap(bank, 128, 128),
                            Fa[:, q4 * 128:(q4 + 1) * 128], Fab)
                    ti = tq * 4 + q4
                    x_ap = self.xh[:, ti * 1024 + ft * 128:
                                   ti * 1024 + ft * 128 + 128]
                    self.v(lambda h, x=x_ap, p=self.pap(bank, 128, 128):
                           h.scalar_tensor_tensor(x, p, 0.5, x, ALU.mult, ALU.add),
                           [self.pb(bank), self.xb[ti]], [self.xb[ti]])

    def att1_prep(self):
        A = self.aview
        OH, OHb = A(0, 4096)
        STAGE, STAGEb = A(4096, 4096)
        self.dma("sync", OH[0:32, :], self.ins["c_onehot"].ap(),
                 [self.in_bufs["c_onehot"]], OHb)
        tbl = self.sm_h[0:32, 336:352]
        self.dma("sync", tbl, self.ins["tbl"].ap(),
                 [self.in_bufs["tbl"]], [self.sm_b[10]])
        for jc in range(8):
            self.mm(4, self.pap(4, 16, 512),
                    [(tbl, OH[0:32, jc * 512:(jc + 1) * 512])],
                    [self.sm_b[10]] + OHb)
            self.sc(lambda h, d=STAGE[0:16, jc * 512:(jc + 1) * 512],
                   p=self.pap(4, 16, 512): h.copy(d, p),
                   [self.pb(4)], STAGEb)
        self.dma("gpsimd", self.wr_d.ap(), STAGE[0:16, 0:4096], STAGEb,
                 [self.wr_db])
        self.mm(5, self.pap(5, 128, 16), [(self.c_sel15[:, :], tbl)],
                [self.c_sel15_b, self.sm_b[10]])
        self.sc(lambda h, d=self.sm_h[:, 352:368], p=self.pap(5, 128, 16):
               h.copy(d, p), [self.pb(5)], [self.sm_b[11]])
        self.mm(5, self.pap(5, 128, 16), [(self.c_sel31[:, :], tbl)],
                [self.c_sel31_b, self.sm_b[10]])
        self.sc(lambda h, d=self.sm_h[:, 368:384], p=self.pap(5, 128, 16):
               h.copy(d, p), [self.pb(5)], [self.sm_b[11]])

    def att1(self):
        lim = _ATT1_LIMIT
        if not self.att1_prep_done:
            self.att1_prep()
            self.att1_prep_done = True
        if lim < 1:
            return
        A = self.aview
        XHT32, XHT32b = A(0, 16384)
        SQ, SQb = A(16384, 1024)
        FS0, FS0b = A(17408, 512)
        FS1, FS1b = A(17920, 512)
        FS2, FS2b = A(18432, 512)
        KTPs = [A(0, 2048), A(14208, 2048)]
        VPs = [A(2048, 2176), A(12032, 2176)]
        QZ = [A(4224, 512), A(5760, 512)]
        PT = [A(4736, 512), A(5248, 512)]
        XTo = [A(8320, 512), A(8832, 512)]
        WWC, WWCb = A(9728, 2304)

        for g in range(4):
            tis = list(range(g * 4, g * 4 + 4))
            dsts = [(self.xhat_h[:, i * 1024:(i + 1) * 1024], [self.xhat_b[i]])
                    for i in range(4)]
            self.ln_group(tis, dsts, SQ, SQb)
            for i, ti in enumerate(tis):
                xh_ap = self.xhat_h[:, i * 1024:(i + 1) * 1024]
                xh_buf = self.xhat_b[i]
                for hc in range(8):
                    bank = 6 + (hc % 2)
                    self.tr(bank, self.pap(bank, 128, 128),
                            xh_ap[:, hc * 128:(hc + 1) * 128], [xh_buf])
                    dcol = hc * 2048 + ti * 128
                    self.sc(lambda h, d=XHT32[:, dcol:dcol + 128],
                           p=self.pap(bank, 128, 128): h.copy(d, p),
                           [self.pb(bank)], [XHT32b[dcol // 512]])
        if lim < 2:
            return
        for wname, dst_d, dst_db in (("wqT1", self.qsp_d, self.qsp_db),
                                     ("wkT1", self.ksp_d, self.ksp_db)):
            wt = self.ins[wname]
            wdb_ = self.in_bufs[wname]
            for half in range(2):
                for sl in range(8):
                    self.dma("sync", self.wp_h[:, sl * 512:(sl + 1) * 512],
                             wt.ap()[half:half + 1, :,
                                     sl * 512:(sl + 1) * 512].squeeze(0),
                             [wdb_], [self.wp_b[sl]])
                for qc in range(4):
                    for f4 in range(4):
                        ft = half * 4 + f4
                        pairs = [(self.wp_h[:, (hc * 4 + f4) * 128:
                                            (hc * 4 + f4) * 128 + 128],
                                  XHT32[:, hc * 2048 + qc * 512:
                                        hc * 2048 + qc * 512 + 512])
                                 for hc in range(8)]
                        self.mm(f4, self.pap(f4), pairs,
                                list(self.wp_b) + list(XHT32b))
                        Fa, Fab = (FS0, FS0b) if f4 % 2 == 0 else (FS1, FS1b)
                        self.sc(lambda h, d=Fa[:, 0:512], p=self.pap(f4):
                               h.copy(d, p), [self.pb(f4)], Fab)
                        self.dma("sync",
                                 dst_d.ap()[ft * 128:(ft + 1) * 128,
                                            qc * 512:(qc + 1) * 512],
                                 Fa[:, 0:512], Fab, [dst_db])
        for fh in range(2):
            for sl in range(8):
                self.dma("sync", self.wp_h[:, sl * 512:(sl + 1) * 512],
                         self.ins["wvT1"].ap()[fh:fh + 1, :,
                                               sl * 512:(sl + 1) * 512].squeeze(0),
                         [self.in_bufs["wvT1"]], [self.wp_b[sl]])
            for ti in range(16):
                pairs = [(XHT32[:, hc * 2048 + ti * 128:
                                hc * 2048 + ti * 128 + 128],
                          self.wp_h[:, hc * 512:(hc + 1) * 512])
                         for hc in range(8)]
                bank = fh * 4 + (ti % 4)
                self.mm(bank, self.pap(bank), pairs,
                        list(self.wp_b) + list(XHT32b))
                Fa, Fab = (FS0, FS0b) if ti % 2 == 0 else (FS1, FS1b)
                self.sc(lambda h, d=Fa[:, 0:512], p=self.pap(bank):
                       h.copy(d, p), [self.pb(bank)], Fab)
                self.dma("sync",
                         self.vsp_d.ap()[ti * 128:(ti + 1) * 128,
                                         fh * 512:(fh + 1) * 512],
                         Fa[:, 0:512], Fab, [self.vsp_db])
        if lim < 3:
            return
        self.v(lambda h, a=QZ[0][0][64:128, :]: h.memset(a, 0.0), [], QZ[0][1])
        self.v(lambda h, a=QZ[1][0][0:64, :]: h.memset(a, 0.0), [], QZ[1][1])
        for p in range(8):
            KTP, KTPb = KTPs[p % 2]
            VP, VPb = VPs[p % 2]
            self.dma("gpsimd", KTP, self.ksp_d.ap()[p * 128:(p + 1) * 128, :],
                     [self.ksp_db], KTPb)
            for hh in range(2):
                hglob = 2 * p + hh
                src = bass.AP(self.wr_d.ap().tensor, hglob * 4096 + 1408,
                              [[1, 128], [1, 1152]])
                self.dma("gpsimd", WWC[:, hh * 1152:(hh + 1) * 1152], src,
                         [self.wr_db], WWCb)
            for tk in range(16):
                src = self.vsp_d.ap()[tk * 128:(tk + 1) * 128,
                                      p * 128:(p + 1) * 128]
                src2 = src.rearrange("p (a b) -> p a b", b=64)
                dst = VP[:, tk * 130:tk * 130 + 130]
                dst2 = bass.AP(dst.tensor, dst.offset,
                               [list(dst.ap[0]), [65, 2], [1, 64]])
                self.dma("gpsimd", dst2, src2, [self.vsp_db], VPb)
            vones = bass.AP(VP.tensor, VP.offset + 64,
                            [list(VP.ap[0]), [130, 16], [65, 2]])
            self.v(lambda h, a=vones: h.memset(a, 1.0), VPb, VPb)
            for qc in range(4):
                self.dma("sync", QZ[0][0][0:64, :],
                         self.qsp_d.ap()[p * 128:p * 128 + 64,
                                         qc * 512:(qc + 1) * 512],
                         [self.qsp_db], QZ[0][1])
                self.dma("sync", QZ[1][0][64:128, :],
                         self.qsp_d.ap()[p * 128 + 64:p * 128 + 128,
                                         qc * 512:(qc + 1) * 512],
                         [self.qsp_db], QZ[1][1])
                for hh in range(2):
                    hglob = 2 * p + hh
                    for kt in range(16):
                        band = (4 * qc - 1 <= kt <= 4 * qc + 4)
                        sb_ = kt % 2
                        pairs = [(KTP[:, kt * 128:(kt + 1) * 128],
                                  QZ[hh][0][:, :])]
                        rds = KTPb + QZ[hh][1]
                        if band:
                            boff = (1920 - 128 * kt + 512 * qc) - 1408
                            ww_ap = WWC[:, hh * 1152 + boff:
                                        hh * 1152 + boff + 512]
                            pairs = pairs + [(self.c_rev[:, :], ww_ap)]
                            rds = rds + [self.c_rev_b] + WWCb
                        self.mm(sb_, self.pap(sb_), pairs, rds)
                        pt_ap, pt_b = PT[sb_]
                        if band:
                            self.sc(lambda h, d=pt_ap, ps=self.pap(sb_):
                                    h.activation(d, ps, AF.Exp, scale=0.125),
                                    [self.pb(sb_)], pt_b)
                        else:
                            bco = 352 if kt <= 4 * qc - 2 else 368
                            bias = self.sm_h[:, bco + hglob:bco + hglob + 1]
                            self.sc(lambda h, d=pt_ap, ps=self.pap(sb_), b2=bias:
                                    h.activation(d, ps, AF.Exp, scale=0.125,
                                                 bias=b2),
                                    [self.pb(sb_), self.sm_b[11]], pt_b)
                        self.mm(2, self.pap(2, 65, 512),
                                [(VP[:, kt * 130 + hh * 65:
                                     kt * 130 + hh * 65 + 65], pt_ap)],
                                VPb + pt_b, start=(kt == 0), stop=(kt == 15))
                    self.v(lambda h, d=FS0[0:1, 0:512],
                           p=self.psum_h[2][64:65, 0:512]:
                           h.reciprocal(d, p), [self.pb(2)], FS0b)
                    self.mm(3, self.pap(3, 64, 512),
                            [(self.c_ones1[0:1, 0:64], FS0[0:1, 0:512])],
                            [self.c_ones1_b] + FS0b)
                    self.sc(lambda h, d=FS1[0:64, 0:512], p=self.pap(3, 64, 512):
                           h.copy(d, p), [self.pb(3)], FS1b)
                    self.v(lambda h, d=FS2[0:64, 0:512],
                           p=self.psum_h[2][0:64, 0:512], m=FS1[0:64, 0:512]:
                           h.tensor_tensor(d, p, m, ALU.mult),
                           [self.pb(2)] + FS1b, FS2b)
                    self.dma("sync",
                             self.ysp_d.ap()[hglob * 64:(hglob + 1) * 64,
                                             qc * 512:(qc + 1) * 512],
                             FS2[0:64, 0:512], FS2b, [self.ysp_db])
        if lim < 4:
            return
        for fh in range(2):
            for sl in range(8):
                self.dma("gpsimd", self.wp_h[:, sl * 512:(sl + 1) * 512],
                         self.ins["woT1"].ap()[fh:fh + 1, :,
                                               sl * 512:(sl + 1) * 512].squeeze(0),
                         [self.in_bufs["woT1"]], [self.wp_b[sl]])
            for tg in range(4):
                for hc in range(8):
                    xt_ap, xt_b = XTo[hc % 2]
                    self.dma("sync", xt_ap[:, 0:512],
                             self.ysp_d.ap()[hc * 128:(hc + 1) * 128,
                                             tg * 512:(tg + 1) * 512],
                             [self.ysp_db], xt_b)
                    for q4 in range(4):
                        self.mm(fh * 4 + q4, self.pap(fh * 4 + q4),
                                [(xt_ap[:, q4 * 128:(q4 + 1) * 128],
                                  self.wp_h[:, hc * 512:(hc + 1) * 512])],
                                xt_b + [self.wp_b[hc]],
                                start=(hc == 0), stop=(hc == 7))
                for q4 in range(4):
                    ti = tg * 4 + q4
                    x_ap = self.xh[:, ti * 1024 + fh * 512:
                                   ti * 1024 + fh * 512 + 512]
                    self.v(lambda h, x=x_ap, p=self.pap(fh * 4 + q4):
                           h.tensor_tensor(x, x, p, ALU.add),
                           [self.pb(fh * 4 + q4), self.xb[ti]], [self.xb[ti]])


def kernel(**inputs):
    cons = host_constants()
    f32 = np.float32

    def tr(a):
        return np.ascontiguousarray(np.asarray(a, f32).T)

    def rep(a):
        return np.ascontiguousarray(np.tile(np.asarray(a, f32)[None, :], (128, 1)))

    shared = dict(cons)
    shared["tbl"] = np.asarray(inputs["L1_attn_rel_bias_table"], f32)
    shared["wg0_nat"] = np.asarray(inputs["L0_ffn_w_gate"], f32)
    shared["wu0_nat"] = np.asarray(inputs["L0_ffn_w_up"], f32)
    shared["wd0_nat"] = np.asarray(inputs["L0_ffn_w_down"], f32)
    def ftile(a):
        return np.ascontiguousarray(
            a.reshape(8, 128, 8, 128).transpose(2, 1, 0, 3).reshape(8, 128, 1024))

    def pool2a(a):
        return np.ascontiguousarray(
            a.reshape(8, 128, 2, 512).transpose(2, 1, 0, 3).reshape(2, 128, 4096))

    shared["wqT0"] = ftile(tr(inputs["L0_attn_Wq"]))
    shared["wkT0"] = ftile(tr(inputs["L0_attn_Wk"]))
    shared["wvT0"] = pool2a(tr(inputs["L0_attn_Wv"]))
    shared["woutT0"] = pool2a(tr(inputs["L0_attn_Wout"]))
    shared["wbmgT"] = np.ascontiguousarray(np.concatenate(
        [np.asarray(inputs["L0_attn_Wbeta"], f32).T,
         np.asarray(inputs["L0_attn_Wmix"], f32).T,
         np.asarray(inputs["L0_attn_Wgate"], f32).T], axis=1))
    shared["bmg_bias"] = np.ascontiguousarray(np.tile(np.concatenate(
        [np.asarray(inputs["L0_attn_bbeta"], f32),
         np.asarray(inputs["L0_attn_bmix"], f32),
         np.asarray(inputs["L0_attn_bgate"], f32)])[None, :], (128, 1)))
    shared["bout_rep"] = rep(inputs["L0_attn_bout"])
    shared["lng_rep"] = rep(inputs["L0_attn_ln_g"])
    shared["lnb_rep"] = rep(inputs["L0_attn_ln_b"])
    def pool2(a):
        return np.ascontiguousarray(
            a.reshape(8, 128, 2, 512).transpose(2, 1, 0, 3).reshape(2, 128, 4096))

    shared["wqT1"] = pool2(tr(inputs["L1_attn_w_q"]))
    shared["wkT1"] = pool2(tr(inputs["L1_attn_w_k"]))
    shared["wvT1"] = pool2(tr(inputs["L1_attn_w_v"]))
    shared["woT1"] = pool2(tr(inputs["L1_attn_w_o"]))
    shared["wg1_nat"] = np.asarray(inputs["L1_ffn_w_gate"], f32)
    shared["wu1_nat"] = np.asarray(inputs["L1_ffn_w_up"], f32)
    shared["wd1_nat"] = np.asarray(inputs["L1_ffn_w_down"], f32)
    shared["clip_x"] = np.asarray(inputs["L1_ffn_clip_x"], f32).reshape(1, 1)
    shared["clip_d"] = np.asarray(inputs["L1_ffn_clip_down"], f32).reshape(1, 1)

    x = np.asarray(inputs["L0_attn_x"], f32)
    in_desc = [("x_in", (T, D), F32)]
    for name, arr in shared.items():
        in_desc.append((name, arr.shape, F32))

    nc = bass.Bass("TRN2", target_bir_lowering=False, debug=False)
    k = K(nc, _STAGE)
    k.build(in_desc)

    in_maps = []
    for c in range(8):
        m = dict(shared)
        m["x_in"] = np.ascontiguousarray(x[c])
        in_maps.append(m)
    res = run_bass_kernel_spmd(nc, in_maps, core_ids=list(range(8)))
    out = np.stack([np.asarray(res.results[c]["out"]) for c in range(8)], axis=0)
    return out.astype(np.float32)


# version 17
# speedup vs baseline: 1.0024x; 1.0000x over previous
import math
import numpy as np
import concourse.bass as bass
import concourse.mybir as mybir
from concourse.bass_utils import run_bass_kernel_spmd
from concourse.alu_op_type import AluOpType as ALU

F32 = mybir.dt.float32
BF16 = mybir.dt.bfloat16
I32 = mybir.dt.int32
AF = mybir.ActivationFunctionType
AX = mybir.AxisListType

T = 2048
D = 1024
NI = 2816
NEG = -1.0e30

_STAGE = 6
_ATT1_LIMIT = 99


ENGINES = ("sync", "tensor", "vector", "scalar", "gpsimd")


class Buf:
    __slots__ = ("name", "writer", "readers")

    def __init__(self, name):
        self.name = name
        self.writer = None
        self.readers = []


class Sched:

    def __init__(self):
        self.ops = {e: [] for e in ENGINES}
        self.edges = []
        self.gseq = []

    def op(self, eng, fn, reads=(), writes=(), dma=False, sem_only=False):
        idx = len(self.ops[eng])
        for b in reads:
            w = b.writer
            if w is not None and w != (eng, idx):
                self.edges.append((w[0], w[1], eng, idx))
            b.readers.append((eng, idx))
        for b in writes:
            w = b.writer
            if w is not None and w != (eng, idx):
                self.edges.append((w[0], w[1], eng, idx))
            for (re, ri) in b.readers:
                if (re, ri) != (eng, idx):
                    self.edges.append((re, ri, eng, idx))
            b.writer = (eng, idx)
            b.readers = []
        self.ops[eng].append([fn, dma, False, sem_only])
        self.gseq.append((eng, idx))
        return (eng, idx)

    def finalize_and_emit(self, nc):
        ops = self.ops
        incoming = {}
        for (se, si, de, di) in self.edges:
            incoming.setdefault((de, di), set()).add((se, si))
        for (de, di), srcs in incoming.items():
            for (se, si) in srcs:
                ops[se][si][2] = True
        for e in ENGINES:
            for o in ops[e]:
                if o[1]:
                    o[2] = True
        val_at = {}
        for e in ENGINES:
            c = 0
            va = []
            for o in ops[e]:
                if o[2]:
                    c += 16 if o[1] else 1
                va.append(c)
            val_at[e] = va
        K = {e: {e2: 0 for e2 in ENGINES} for e in ENGINES}
        val_issued = {e: 0 for e in ENGINES}
        self_done = {e: -1 for e in ENGINES}
        done_know = {}
        waits = {}
        drains = set()
        for (e, i) in self.gseq:
            srcs = incoming.get((e, i), ())
            sem_w = {}
            drain_srcs = []
            for (se, si) in srcs:
                v = val_at[se][si]
                if sem_w.get(se, 0) < v:
                    sem_w[se] = v
            ke = K[e]
            for se in sorted(sem_w):
                v = sem_w[se]
                if ke[se] >= v:
                    continue
                waits.setdefault((e, i), {})[se] = v
                dk = done_know.get((se, v))
                if dk:
                    for e2, v2 in dk.items():
                        if ke[e2] < v2:
                            ke[e2] = v2
                if ke[se] < v:
                    ke[se] = v
            if drain_srcs:
                need = False
                for j in drain_srcs:
                    if self_done[e] >= j:
                        continue
                    oj = ops[e][j]
                    if oj[2] and ke[e] >= val_at[e][j]:
                        continue
                    need = True
                    break
                if need:
                    drains.add((e, i))
                    self_done[e] = i - 1
                    if ke[e] < val_issued[e]:
                        ke[e] = val_issued[e]
            o = ops[e][i]
            if o[2]:
                val_issued[e] += 16 if o[1] else 1
                snap = dict(ke)
                snap[e] = val_issued[e]
                done_know[(e, val_issued[e])] = snap
        sems = {e: nc.alloc_semaphore(f"sem_{e}") for e in ENGINES}
        nops = sum(len(v) for v in ops.values())
        nincs = sum(1 for e in ENGINES for o in ops[e] if o[2])
        nwaits = sum(len(d) for d in waits.values())
        print(f"[kernel] ops={nops} incs={nincs} waits={nwaits} "
              f"drains={len(drains)} edges={len(self.edges)} "
              + " ".join(f"{e}:{len(ops[e])}" for e in ENGINES))
        sched = self

        def emit(eng_name, handle):
            for i, o in enumerate(sched.ops[eng_name]):
                w = waits.get((eng_name, i))
                if w:
                    for se in sorted(w):
                        handle.wait_ge(sems[se], w[se])
                if (eng_name, i) in drains:
                    handle.drain()
                inst = o[0](handle)
                if o[2]:
                    assert inst is not None, f"op {eng_name}:{i} needs inc"
                    inst.then_inc(sems[eng_name], 16 if o[1] else 1)

        with nc.Block() as block:
            @block.sync
            def _(h):
                emit("sync", h)

            @block.tensor
            def _(h):
                emit("tensor", h)

            @block.vector
            def _(h):
                emit("vector", h)

            @block.scalar
            def _(h):
                emit("scalar", h)

            @block.gpsimd
            def _(h):
                emit("gpsimd", h)


def _alibi_slopes(n):
    def pow2(m):
        start = 2 ** (-(2 ** (-(math.log2(m) - 3))))
        return [start * start ** i for i in range(m)]
    if math.log2(n).is_integer():
        return np.array(pow2(n), dtype=np.float32)
    c = 2 ** int(math.floor(math.log2(n)))
    s = pow2(c) + _alibi_slopes(2 * c).tolist()[: n - c]
    return np.array(s, dtype=np.float32)


def _rel_bucket(relpos):
    num_buckets = 16
    n = -relpos
    ret = (n < 0).astype(np.int64) * num_buckets
    n = np.abs(n)
    max_exact = num_buckets // 2
    is_small = n < max_exact
    val_if_large = max_exact + (
        np.log(np.maximum(n, 1).astype(np.float32) / max_exact)
        / math.log(128 / max_exact) * (num_buckets - max_exact)
    ).astype(np.int64)
    val_if_large = np.minimum(val_if_large, num_buckets - 1)
    return ret + np.where(is_small, n, val_if_large)


def host_constants():
    c = {}
    c["c_ident"] = np.eye(128, dtype=np.float32)
    c["c_ones1"] = np.ones((1, 128), dtype=np.float32)
    c["c_allones"] = np.ones((128, 128), dtype=np.float32)
    c["c_negones"] = -np.ones((128, 128), dtype=np.float32)
    ii = np.arange(128)
    c["c_tri"] = (ii[:, None] <= ii[None, :]).astype(np.float32)
    c["c_maskst"] = np.where(ii[:, None] <= ii[None, :], 0.0, NEG).astype(np.float32)
    c["c_rev"] = np.eye(128, dtype=np.float32)[::-1].copy()
    j = np.arange(4096)
    rel = np.minimum(j, 4094) - 2047
    buck = _rel_bucket(-rel)
    oh = np.zeros((32, 4096), np.float32)
    oh[buck, j] = 8.0
    c["c_onehot"] = oh
    sel15 = np.zeros((32, 128), np.float32)
    sel15[15, :] = 1.0
    sel31 = np.zeros((32, 128), np.float32)
    sel31[31, :] = 1.0
    c["c_sel15"] = sel15
    c["c_sel31"] = sel31
    r = np.exp(-_alibi_slopes(8))
    c["c_rrep"] = np.tile(np.repeat(r, 2)[None, :], (128, 1)).astype(np.float32)
    return c


CONST_SHAPES = {"c_ident": [128, 128], "c_ones1": [1, 128],
                "c_allones": [128, 128], "c_negones": [128, 128],
                "c_tri": [128, 128], "c_maskst": [128, 128], "c_rev": [128, 128],
                "c_sel15": [32, 128], "c_sel31": [32, 128], "c_rrep": [128, 16]}


class K:
    def __init__(self, nc, stage):
        self.nc = nc
        self.s = Sched()
        self.stage = stage
        self.debug_fn = None
        self._dumps = []

    def dump(self, name, src_ap, bufs, shape):
        t = self.nc.dram_tensor("dbg_" + name, list(shape), src_ap.dtype,
                                kind="ExternalOutput")
        b = Buf("dbg_" + name)
        self.dma("sync", t.ap(), src_ap, list(bufs), [b])
        self._dumps.append(b)

    def sbuf(self, name, shape, dtype=F32, nbufs=1):
        h = self.nc.alloc_sbuf_tensor(name, shape, dtype)
        bufs = [Buf(f"{name}.{i}") for i in range(nbufs)]
        return h, (bufs[0] if nbufs == 1 else bufs)

    def dram_scratch(self, name, shape, dtype=F32):
        h = self.nc.dram_tensor(name, shape, dtype)
        return h, Buf(name)

    def aview(self, off, n):
        ap = self.A_h[:, off:off + n]
        lo, hi = off // 512, (off + n - 1) // 512
        return ap, self.A_bufs[lo:hi + 1]

    def dma(self, eng, dst_ap, src_ap, reads, writes):
        self.s.op(eng, lambda h, d=dst_ap, s_=src_ap: h.dma_start(out=d, in_=s_),
                  reads=list(reads), writes=list(writes), dma=True)

    def mm(self, bank, out_ap, pairs, reads, start=True, stop=True):
        pb = self.psum_b[bank]

        def fn(h, pairs=pairs, out=out_ap, start=start, stop=stop):
            last = None
            n = len(pairs)
            for i, (l, r) in enumerate(pairs):
                last = h.matmul(out, l, r, start=(start and i == 0),
                                stop=(stop and i == n - 1),
                                skip_group_check=True)
            return last
        rd = list(reads)
        if not start:
            rd = rd + [pb]
        self.s.op("tensor", fn, reads=rd, writes=[pb])

    def tr(self, bank, out_ap, in_ap, reads):
        pb = self.psum_b[bank]
        self.s.op("tensor",
                  lambda h, o=out_ap, i=in_ap, d=self.ident_ap: h.transpose(o, i, d),
                  reads=list(reads) + [self.ident_buf], writes=[pb])

    def v(self, fn, reads=(), writes=()):
        self.s.op("vector", fn, reads=list(reads), writes=list(writes))

    def _e(self, eng, fn, reads=(), writes=()):
        self.s.op(eng, fn, reads=list(reads), writes=list(writes))

    def heat(self, n, reads, bank=0):
        for _ in range(n):
            self.mm(bank, self.pap(bank, 128, 512),
                    [(self.c_ident[:, :], self.heat_h[:, :])],
                    [self.c_ident_b, self.heat_b] + list(reads))

    def sc(self, fn, reads=(), writes=()):
        self.s.op("scalar", fn, reads=list(reads), writes=list(writes))

    def gp(self, fn, reads=(), writes=(), sem_only=False):
        self.s.op("gpsimd", fn, reads=list(reads), writes=list(writes),
                  sem_only=sem_only)

    def pb(self, bank):
        return self.psum_b[bank]

    def pap(self, bank, p=128, w=512):
        return self.psum_h[bank][0:p, 0:w]

    def build(self, in_desc):
        nc = self.nc
        ins = {}
        self.in_bufs = {}
        for name, shape, dtype in in_desc:
            ins[name] = nc.dram_tensor(name, list(shape), dtype, kind="ExternalInput")
            self.in_bufs[name] = Buf("in_" + name)
        self.ins = ins
        self.out_t = nc.dram_tensor("out", [T, D], F32, kind="ExternalOutput")
        self.out_buf = Buf("out")

        self.psum_h = []
        self.psum_b = []
        for i in range(8):
            h = nc.alloc_psum_tensor(f"pbank{i}", [128, 512], F32)
            self.psum_h.append(h)
            self.psum_b.append(Buf(f"pbank{i}"))

        AW = 19456
        self.A_h, _ = self.sbuf("arena", [128, AW], F32)
        self.A_words = AW
        self.A_bufs = [Buf(f"A.{i}") for i in range(AW // 512)]

        for name, shp in CONST_SHAPES.items():
            h, b = self.sbuf(name + "_s", shp, F32)
            setattr(self, name, h)
            setattr(self, name + "_b", b)
            self.dma("sync", h[:, :], ins[name].ap(), [self.in_bufs[name]], [b])
        self.ident_ap = self.c_ident[:, :]
        self.ident_buf = self.c_ident_b

        self.heat_h, hb = self.sbuf("heat", [128, 512], F32)
        self.heat_b = hb
        self.v(lambda h, a=self.heat_h[:, :]: h.memset(a, 1.0), [], [hb])

        self.xh, self.xb = self.sbuf("x", [128, 16 * 1024], F32, nbufs=16)
        for i in range(16):
            self.dma("sync" if i % 2 == 0 else "gpsimd",
                     self.xh[:, i * 1024:(i + 1) * 1024],
                     ins["x_in"].ap()[i * 128:(i + 1) * 128, :],
                     [self.in_bufs["x_in"]], [self.xb[i]])

        self.xhat_h, self.xhat_b = self.sbuf("xhat", [128, 4 * 1024], F32, nbufs=4)
        self.stat_h, self.stat_b = self.sbuf("stat", [128, 64], F32, nbufs=8)
        self.fsc_h, self.fsc_b = self.sbuf("fsc", [128, 1536], F32, nbufs=2)
        self.wp_h, self.wp_b = self.sbuf("wpool", [128, 8 * 512], F32, nbufs=8)
        self.sm_h, self.sm_b = self.sbuf("small", [128, 16 * 32], F32, nbufs=16)

        self.wgqT_d, self.wgqT_db = self.dram_scratch("wgqT_d", [11, 128, 2048])
        self.wuqT_d, self.wuqT_db = self.dram_scratch("wuqT_d", [11, 128, 2048])
        self.wdqT_d, self.wdqT_db = self.dram_scratch("wdqT_d", [44, 128, 512])
        self.iwgT_d, self.iwgT_db = self.dram_scratch("iwgT_d", [22, 128, 1024], BF16)
        self.iwuT_d, self.iwuT_db = self.dram_scratch("iwuT_d", [22, 128, 1024], BF16)
        self.iwdT_d, self.iwdT_db = self.dram_scratch("iwdT_d", [8, 128, 2816], BF16)
        self.hsp_d, self.hsp_db = self.dram_scratch("hsp_d", [NI, T])
        self.ysp_d, self.ysp_db = self.dram_scratch("ysp_d", [D, T])
        self.xTsp_d, self.xTsp_db = self.dram_scratch("xTsp_d", [D, T])
        self.qsp_d, self.qsp_db = self.dram_scratch("qsp_d", [D, T])
        self.ksp_d, self.ksp_db = self.dram_scratch("ksp_d", [D, T])
        self.vsp_d, self.vsp_db = self.dram_scratch("vsp_d", [T, D])
        self.wr_d, self.wr_db = self.dram_scratch("wr_d", [16, 4096])
        self.cc_in_d, self.cc_in_db = self.dram_scratch("cc_in", [1, 16])
        self.cc_out_d, self.cc_out_db = self.dram_scratch("cc_out", [1, 16])

        self.l0w_done = False
        self.l1w_done = False
        self.att1_prep_done = False

        if self.debug_fn is not None:
            self.debug_fn(self)
        if self.stage >= 1:
            self.ffn0()
        if self.stage >= 2:
            self.att0()
        if self.stage >= 3:
            self.ffn0()
        if self.stage >= 4:
            self.ffn1()
        if self.stage >= 5:
            self.att1()
        if self.stage >= 6:
            self.ffn1()

        for i in range(16):
            self.s.op("sync", (lambda h, i=i: h.dma_start(
                out=self.out_t.ap()[i * 128:(i + 1) * 128, :],
                in_=self.xh[:, i * 1024:(i + 1) * 1024])),
                reads=[self.xb[i]], writes=[self.out_buf], dma=True)
        self.s.op("sync", lambda h: None, reads=[self.out_buf] + self._dumps)
        self.s.finalize_and_emit(nc)

    def ln_tile(self, ti, dst_ap, dst_bufs, slot, sq_ap, sq_bufs):
        x_ap = self.xh[:, ti * 1024:(ti + 1) * 1024]
        xbuf = self.xb[ti]
        self.ln_generic(x_ap, [xbuf], dst_ap, dst_bufs, slot, sq_ap, sq_bufs)

    def ln_generic(self, x_ap, x_bufs, dst_ap, dst_bufs, slot, sq_ap, sq_bufs):
        st = self.stat_h
        stb = self.stat_b[slot % 8]
        o = 8 * (slot % 8)
        self.v(lambda h, x=x_ap, st=st, o=o: h.tensor_reduce(
            st[:, o:o + 1].unsqueeze(-1), x.rearrange("p (a b) -> p a b", b=1024),
            AX.X, ALU.add), list(x_bufs), [stb])
        self.v(lambda h, st=st, o=o: h.memset(st[:, o + 1:o + 2], 0.0),
               [], [stb])
        self.sc(lambda h, x=x_ap, sa=sq_ap, st=st, o=o: h.activation(
            sa, x, AF.Square, accum_out=st[:, o + 1:o + 2]),
            list(x_bufs), list(sq_bufs) + [stb])
        self.v(lambda h, st=st, o=o: h.tensor_scalar(
            st[:, o + 2:o + 3], st[:, o:o + 1], 1.0 / 1024.0, None, ALU.mult),
            [stb], [stb])
        self.v(lambda h, st=st, o=o: h.tensor_tensor(
            st[:, o + 5:o + 6], st[:, o + 2:o + 3], st[:, o + 2:o + 3], ALU.mult),
            [stb], [stb])
        self.v(lambda h, st=st, o=o: h.scalar_tensor_tensor(
            st[:, o + 4:o + 5], st[:, o + 1:o + 2], 1.0 / 1024.0,
            st[:, o + 5:o + 6], ALU.mult, ALU.subtract), [stb], [stb])
        self.v(lambda h, st=st, o=o: h.tensor_scalar(
            st[:, o + 4:o + 5], st[:, o + 4:o + 5], 1e-5, None, ALU.add),
            [stb], [stb])
        self.sc(lambda h, st=st, o=o: h.activation(
            st[:, o + 5:o + 6], st[:, o + 4:o + 5], AF.Sqrt),
            [stb], [stb])
        self.v(lambda h, st=st, o=o: h.reciprocal(
            st[:, o + 3:o + 4], st[:, o + 5:o + 6]), [stb], [stb])
        self.v(lambda h, x=x_ap, d=dst_ap, st=st, o=o: h.tensor_scalar(
            d, x, st[:, o + 2:o + 3], st[:, o + 3:o + 4], ALU.subtract, ALU.mult),
            list(x_bufs) + [stb], list(dst_bufs))

    def fp4(self, src_ap, src_bufs, dst_ap, dst_bufs, W, fslot,
            t1_ap, t1_bufs, t2_ap, t2_bufs, eng="vector"):
        nb = W // 16
        fs = self.fsc_h
        fb = self.fsc_b[fslot % 2]
        fo = 768 * (fslot % 2)
        assert 2 * nb <= 768
        src3 = src_ap.rearrange("p (a b) -> p a b", b=16)
        self.v(lambda h, s3=src3, d=fs[:, fo:fo + nb]: h.tensor_reduce(
            d.unsqueeze(-1), s3, AX.X, ALU.max, apply_absolute_value=True),
            list(src_bufs), [fb])
        self.v(lambda h, d=fs[:, fo:fo + nb]: h.tensor_scalar(
            d, d, 1.0 / 6.0, 1e-12, ALU.mult, ALU.max), [fb], [fb])
        self.v(lambda h, d=fs[:, fo + nb:fo + 2 * nb], s_=fs[:, fo:fo + nb]:
               h.reciprocal(d, s_), [fb], [fb])
        self.v(lambda h, d=fs[:, fo:fo + nb]: h.tensor_scalar(
            d, d, 0.5, None, ALU.mult), [fb], [fb])
        s_sc = fs[:, fo:fo + nb]
        s_rc = fs[:, fo + nb:fo + 2 * nb]
        scl_b = bass.AP(s_sc.tensor, s_sc.offset, [list(s_sc.ap[0]), [1, nb], [0, 16]])
        rec_b = bass.AP(s_rc.tensor, s_rc.offset, [list(s_rc.ap[0]), [1, nb], [0, 16]])
        t13 = t1_ap.rearrange("p (a b) -> p a b", b=16)
        self._e(eng, lambda h, d=t13, s3=src3, r=rec_b: h.tensor_tensor(d, s3, r, ALU.mult),
               list(src_bufs) + [fb], list(t1_bufs))
        self.sc(lambda h, d=t2_ap, s_=t1_ap: h.activation(d, s_, AF.Abs),
                list(t1_bufs), list(t2_bufs))
        self._e(eng, lambda h, d=dst_ap, a=t2_ap: h.tensor_scalar(
            d, a, 5.0, 2.0, ALU.is_gt, ALU.mult), list(t2_bufs), list(dst_bufs))
        for m in (2.5, 3.5):
            self._e(eng, lambda h, d=dst_ap, a=t2_ap, m=m: h.scalar_tensor_tensor(
                d, a, m, dst_ap, ALU.is_gt, ALU.add),
                list(t2_bufs) + list(dst_bufs), list(dst_bufs))
        self._e(eng, lambda h, d=dst_ap: h.tensor_scalar(
            d, d, 2.0, None, ALU.mult), list(dst_bufs), list(dst_bufs))
        for m in (0.25, 0.75, 1.25, 1.75):
            self._e(eng, lambda h, d=dst_ap, a=t2_ap, m=m: h.scalar_tensor_tensor(
                d, a, m, dst_ap, ALU.is_gt, ALU.add),
                list(t2_bufs) + list(dst_bufs), list(dst_bufs))
        self.sc(lambda h, d=t1_ap: h.activation(d, d, AF.Sign),
                list(t1_bufs), list(t1_bufs))
        self._e(eng, lambda h, d=dst_ap, s_=t1_ap: h.tensor_tensor(d, d, s_, ALU.mult),
               list(dst_bufs) + list(t1_bufs), list(dst_bufs))
        d3 = dst_ap.rearrange("p (a b) -> p a b", b=16)
        self._e(eng, lambda h, d=d3, s_=scl_b: h.tensor_tensor(d, d, s_, ALU.mult),
               list(dst_bufs) + [fb], list(dst_bufs))

    def ln_group(self, tis, dsts, sq_ap, sq_bufs):
        g0 = tis[0]
        n = len(tis)
        assert list(tis) == list(range(g0, g0 + n)) and n <= 4
        st = self.stat_h
        stb = self.stat_b[0]
        x4 = self.xh[:, g0 * 1024:(g0 + n) * 1024]
        xbufs = [self.xb[t] for t in tis]
        self.v(lambda h, x=x4, st=st, n=n: h.tensor_reduce(
            st[:, 0:n].unsqueeze(-1), x.rearrange("p (a b) -> p a b", b=1024),
            AX.X, ALU.add), xbufs, [stb])
        self.v(lambda h, st=st, n=n: h.memset(st[:, 4:4 + n], 0.0), [], [stb])
        for i, t in enumerate(tis):
            self.sc(lambda h, x=self.xh[:, t * 1024:(t + 1) * 1024], sa=sq_ap,
                    st=st, i=i: h.activation(sa, x, AF.Square,
                                             accum_out=st[:, 4 + i:5 + i]),
                    [self.xb[t]], list(sq_bufs) + [stb])
        self.v(lambda h, st=st, n=n: h.tensor_scalar(
            st[:, 8:8 + n], st[:, 0:n], 1.0 / 1024.0, None, ALU.mult), [stb], [stb])
        self.v(lambda h, st=st, n=n: h.tensor_tensor(
            st[:, 20:20 + n], st[:, 8:8 + n], st[:, 8:8 + n], ALU.mult), [stb], [stb])
        self.v(lambda h, st=st, n=n: h.scalar_tensor_tensor(
            st[:, 16:16 + n], st[:, 4:4 + n], 1.0 / 1024.0, st[:, 20:20 + n],
            ALU.mult, ALU.subtract), [stb], [stb])
        self.v(lambda h, st=st, n=n: h.tensor_scalar(
            st[:, 16:16 + n], st[:, 16:16 + n], 1e-5, None, ALU.add), [stb], [stb])
        self.sc(lambda h, st=st, n=n: h.activation(
            st[:, 20:20 + n], st[:, 16:16 + n], AF.Sqrt), [stb], [stb])
        self.v(lambda h, st=st, n=n: h.reciprocal(
            st[:, 12:12 + n], st[:, 20:20 + n]), [stb], [stb])
        for i, t in enumerate(tis):
            d_ap, d_bufs = dsts[i]
            self.v(lambda h, x=self.xh[:, t * 1024:(t + 1) * 1024], d=d_ap,
                   st=st, i=i: h.tensor_scalar(
                d, x, st[:, 8 + i:9 + i], st[:, 12 + i:13 + i],
                ALU.subtract, ALU.mult), [self.xb[t], stb], list(d_bufs))

    def ffn0_weights(self):
        A = self.aview
        SRCR = [A(0, 4096), A(4096, 4096)]
        T1, T1b = A(8192, 4096)
        T2, T2b = A(12288, 4096)
        STG, STGb = A(16384, 2048)
        fs_i = 0
        sslot = 0
        for wname, wdst, wdb, nt_rows, wcols, bat in (
                ("wg0_nat", self.wgqT_d, self.wgqT_db, 22, 1024, 4),
                ("wu0_nat", self.wuqT_d, self.wuqT_db, 22, 1024, 4),
                ("wd0_nat", self.wdqT_d, self.wdqT_db, 8, 2816, 1)):
            src_t = self.ins[wname]
            for nt0 in range(0, nt_rows, bat):
                nb_ = min(bat, nt_rows - nt0)
                W = wcols * nb_
                SRC, SRCb = SRCR[sslot]
                sslot ^= 1
                if nb_ == 1:
                    self.dma("sync", SRC[:, 0:W],
                             src_t.ap()[nt0 * 128:(nt0 + 1) * 128, :],
                             [self.in_bufs[wname]], SRCb)
                else:
                    sap = src_t.ap()[nt0 * 128:(nt0 + nb_) * 128, :].rearrange(
                        "(a b) c -> b a c", a=nb_)
                    dap = SRC[:, 0:W].rearrange("p (a c) -> p a c", c=wcols)
                    self.dma("sync", dap, sap, [self.in_bufs[wname]], SRCb)
                self.fp4(SRC[:, 0:W], SRCb, SRC[:, 0:W], SRCb, W, fs_i,
                         T1[:, 0:W], T1b, T2[:, 0:W], T2b,
                         eng="vector")
                fs_i += 1
                for bi in range(nb_):
                    nt = nt0 + bi
                    nck = wcols // 128
                    for ck in range(nck):
                        bank = 2 + (ck % 2)
                        self.tr(bank, self.pap(bank, 128, 128),
                                SRC[:, bi * wcols + ck * 128:
                                    bi * wcols + ck * 128 + 128], SRCb)
                        if wcols == 1024:
                            sg = STG[:, ck * 256 + (nt % 2) * 128:
                                     ck * 256 + (nt % 2) * 128 + 128]
                        else:
                            sg = STG[:, (ck % 4) * 128:(ck % 4) * 128 + 128]
                        self.sc(lambda h, d=sg, p=self.pap(bank, 128, 128):
                               h.copy(d, p), [self.pb(bank)], STGb)
                        if wcols != 1024:
                            row = (nt // 4) * 22 + ck
                            col = (nt % 4) * 128
                            self.dma("gpsimd",
                                     wdst.ap()[row:row + 1, :, :].squeeze(0)
                                     [:, col:col + 128],
                                     sg, STGb, [wdb])
                    if wcols == 1024 and nt % 2 == 1:
                        self.dma("gpsimd",
                                 wdst.ap()[nt // 2:nt // 2 + 1, :, :].squeeze(0),
                                 STG[:, 0:2048], STGb, [wdb])

    def ffn0(self):
        if not self.l0w_done:
            self.ffn0_weights()
            self.l0w_done = True
        A = self.aview
        XQT, XQTb = A(0, 4096)
        HQT, HQTb = A(4096, 11264)
        SG = [A(15360 + i * 512, 512) for i in range(4)]
        SGall, SGallb = A(15360, 2048)
        HS1, HS1b = A(17408, 1024)
        HS2, HS2b = A(18432, 1024)
        T1, T1b = A(0, 4096)
        T2, T2b = A(4096, 4096)
        fs_i = 0

        for tc in range(4):
            tis = list(range(tc * 4, tc * 4 + 4))
            dsts = [(self.xhat_h[:, i * 1024:(i + 1) * 1024], [self.xhat_b[i]])
                    for i in range(4)]
            self.ln_group(tis, dsts, T2[:, 0:1024], T2b)
            xall = self.xhat_h[:, 0:4096]
            xallb = list(self.xhat_b)
            self.fp4(xall, xallb, xall, xallb, 4096, fs_i, T1, T1b, T2, T2b)
            fs_i += 1
            for tt in range(4):
                for hc in range(8):
                    bank = 2 + (hc % 2)
                    self.tr(bank, self.pap(bank, 128, 128),
                            self.xhat_h[:, tt * 1024 + hc * 128:
                                        tt * 1024 + hc * 128 + 128],
                            [self.xhat_b[tt]])
                    dcol = hc * 512 + tt * 128
                    self.sc(lambda h, d=XQT[:, dcol:dcol + 128],
                           p=self.pap(bank, 128, 128): h.copy(d, p),
                           [self.pb(bank)], [XQTb[dcol // 512]])
            lidx = 0
            for nt2 in range(11):
                off = (nt2 % 2) * 256
                for proj, wsrc, wsb in (("g", self.wgqT_d, self.wgqT_db),
                                        ("u", self.wuqT_d, self.wuqT_db)):
                    par = lidx % 2
                    lidx += 1
                    base = par * 2048
                    for s in range(4):
                        sl = par * 4 + s
                        self.dma("sync", self.wp_h[:, sl * 512:(sl + 1) * 512],
                                 wsrc.ap()[nt2:nt2 + 1, :,
                                           s * 512:(s + 1) * 512].squeeze(0),
                                 [wsb], [self.wp_b[sl]])
                    for tt in range(4):
                        pairs = [(XQT[:, hc * 512 + tt * 128:
                                      hc * 512 + tt * 128 + 128],
                                  self.wp_h[:, base + hc * 256:
                                            base + hc * 256 + 256])
                                 for hc in range(8)]
                        self.mm(tt % 2, self.pap(tt % 2, 128, 256), pairs,
                                [XQTb[hc] for hc in range(8)] +
                                [self.wp_b[par * 4 + s] for s in range(4)])
                        sg_ap, sg_b = SG[tt]
                        if proj == "g":
                            self.sc(lambda h, d=sg_ap[:, off:off + 256],
                                    p=self.pap(tt % 2, 128, 256):
                                    h.activation(d, p, AF.Silu),
                                    [self.pb(tt % 2)], sg_b)
                        else:
                            self.v(lambda h, d=sg_ap[:, off:off + 256],
                                   p=self.pap(tt % 2, 128, 256):
                                   h.tensor_tensor(d, d, p, ALU.mult),
                                   list(sg_b) + [self.pb(tt % 2)], sg_b)
                if nt2 % 2 == 1:
                    nb = nt2 // 2
                    for pr in range(2):
                        pap_ = SGall[:, pr * 1024:(pr + 1) * 1024]
                        pbufs = list(SG[pr * 2][1]) + list(SG[pr * 2 + 1][1])
                        self.fp4(pap_, pbufs, pap_, pbufs, 1024, fs_i,
                                 HS1, HS1b, HS2, HS2b,
                                 eng="vector")
                        fs_i += 1
                    for tt in range(4):
                        sg_ap, sg_b = SG[tt]
                        for q in range(4):
                            bank = 2 + (q % 2)
                            self.tr(bank, self.pap(bank, 128, 128),
                                    sg_ap[:, q * 128:(q + 1) * 128], sg_b)
                            dcol = (nb * 4 + q) * 512 + tt * 128
                            self.sc(lambda h, d=HQT[:, dcol:dcol + 128],
                                   p=self.pap(bank, 128, 128): h.copy(d, p),
                                   [self.pb(bank)], [HQTb[dcol // 512]])
                elif nt2 == 10:
                    for tt in range(4):
                        sg_ap, sg_b = SG[tt]
                        self.fp4(sg_ap[:, 0:256], sg_b, sg_ap[:, 0:256], sg_b,
                                 256, fs_i, HS1[:, 0:256], HS1b,
                                 HS2[:, 0:256], HS2b)
                        fs_i += 1
                    for tt in range(4):
                        sg_ap, sg_b = SG[tt]
                        for q in range(2):
                            bank = 2 + (q % 2)
                            self.tr(bank, self.pap(bank, 128, 128),
                                    sg_ap[:, q * 128:(q + 1) * 128], sg_b)
                            dcol = (5 * 4 + q) * 512 + tt * 128
                            self.sc(lambda h, d=HQT[:, dcol:dcol + 128],
                                   p=self.pap(bank, 128, 128): h.copy(d, p),
                                   [self.pb(bank)], [HQTb[dcol // 512]])
            for fh in range(2):
                for ic in range(22):
                    slot = ic % 4
                    self.dma("sync", self.wp_h[:, slot * 512:(slot + 1) * 512],
                             self.wdqT_d.ap()[fh * 22 + ic:fh * 22 + ic + 1,
                                              :, :].squeeze(0),
                             [self.wdqT_db], [self.wp_b[slot]])
                    for tt in range(4):
                        lcol = ic * 512 + tt * 128
                        self.mm(4 + tt, self.pap(4 + tt),
                                [(HQT[:, lcol:lcol + 128],
                                  self.wp_h[:, slot * 512:(slot + 1) * 512])],
                                [HQTb[lcol // 512], self.wp_b[slot]],
                                start=(ic == 0), stop=(ic == 21))
                for tt in range(4):
                    ti = tc * 4 + tt
                    x_ap = self.xh[:, ti * 1024 + fh * 512: ti * 1024 + fh * 512 + 512]
                    self.v(lambda h, x=x_ap, p=self.pap(4 + tt):
                           h.scalar_tensor_tensor(x, p, 0.5, x, ALU.mult, ALU.add),
                           [self.pb(4 + tt), self.xb[ti]], [self.xb[ti]])

    def att0(self):
        A = self.aview
        XHT, XHTb = A(0, 2048)
        QT, QTb = A(2048, 2048)
        KT, KTb = A(4096, 2048)
        Y, Yb = A(6144, 1024)
        YT, YTb = A(7168, 1024)
        BMG, BMGb = A(8192, 512)
        LL, LLb = A(8704, 512)
        BOUT, BOUTb = A(9216, 1024)
        LNG, LNGb = A(10240, 1024)
        LNB, LNBb = A(11264, 1024)
        FS, FSb = A(12288, 1024)
        VO, VOb = A(13312, 1152)
        DN, DNb = A(14464, 1152)
        ST, STb = A(15616, 2064)
        self._att0_regs = (XHT, XHTb, QT, QTb, KT, KTb, Y, Yb, YT, YTb, BMG,
                           BMGb, LL, LLb, BOUT, BOUTb, LNG, LNGb, LNB, LNBb,
                           FS, FSb, VO, VOb, DN, DNb, ST, STb)

        for name, ap_, b_ in (("bout_rep", BOUT, BOUTb), ("lng_rep", LNG, LNGb),
                              ("lnb_rep", LNB, LNBb)):
            self.dma("sync", ap_, self.ins[name].ap(), [self.in_bufs[name]], b_)
        self.dma("sync", self.sm_h[:, 0:40], self.ins["bmg_bias"].ap(),
                 [self.in_bufs["bmg_bias"]], [self.sm_b[0], self.sm_b[1]])
        self.v(lambda h, a=ST: h.memset(a, 0.0), [], STb)
        vco = VO[:, 0:8 * 129].rearrange("p (a b) -> p a b", b=129)[:, :, 128:129]
        self.v(lambda h, a=vco: h.memset(a, 1.0), [], VOb)

        for g in range(8):
            for hc in range(8):
                self.dma("sync", self.wp_h[:, 3584 + hc * 64: 3584 + hc * 64 + 40],
                         self.ins["wbmgT"].ap()[hc * 128:(hc + 1) * 128, :],
                         [self.in_bufs["wbmgT"]], [self.wp_b[7]])
            for cc in range(2):
                ti = g * 2 + cc
                xh_ap = self.xhat_h[:, cc * 1024:(cc + 1) * 1024]
                xh_buf = self.xhat_b[cc]
                self.ln_tile(ti, xh_ap, [xh_buf], cc, FS, FSb)
                for hc in range(8):
                    self.tr(1, self.pap(1, 128, 128),
                            xh_ap[:, hc * 128:(hc + 1) * 128], [xh_buf])
                    dcol = hc * 256 + cc * 128
                    self.sc(lambda h, d=XHT[:, dcol:dcol + 128],
                           p=self.pap(1, 128, 128): h.copy(d, p),
                           [self.pb(1)], [XHTb[dcol // 512]])
            lq = 0
            for wname, REG, REGb in (("wqT0", QT, QTb), ("wkT0", KT, KTb)):
                wt = self.ins[wname]
                wdb_ = self.in_bufs[wname]
                for ft in range(8):
                    par = lq % 2
                    lq += 1
                    base = par * 1024
                    pbufs = [self.wp_b[par * 2], self.wp_b[par * 2 + 1]]
                    for s2 in range(2):
                        sl = par * 2 + s2
                        self.dma("sync", self.wp_h[:, sl * 512:(sl + 1) * 512],
                                 wt.ap()[ft:ft + 1, :,
                                         s2 * 512:(s2 + 1) * 512].squeeze(0),
                                 [wdb_], [self.wp_b[sl]])
                    for hc in range(8):
                        self.mm(0, self.pap(0, 128, 256),
                                [(self.wp_h[:, base + hc * 128:
                                            base + hc * 128 + 128],
                                  XHT[:, hc * 256:hc * 256 + 256])],
                                pbufs + list(XHTb),
                                start=(hc == 0), stop=(hc == 7))
                    self.v(lambda h, d=FS[:, 0:256], p=self.pap(0, 128, 256):
                           h.tensor_scalar(d, p, 0.0, None, ALU.min),
                           [self.pb(0)], FSb)
                    self.sc(lambda h, d=FS[:, 0:256]: h.activation(d, d, AF.Exp),
                            FSb, FSb)
                    self.v(lambda h, d=FS[:, 256:512], p=self.pap(0, 128, 256):
                           h.tensor_scalar(d, p, 0.0, None, ALU.max),
                           [self.pb(0)], FSb)
                    self.v(lambda h, d=REG[:, ft * 256:(ft + 1) * 256],
                           a=FS[:, 0:256], b2=FS[:, 256:512]:
                           h.tensor_tensor(d, a, b2, ALU.add),
                           FSb, [REGb[(ft * 256) // 512]])
            for cc in range(2):
                pairs = [(XHT[:, hc * 256 + cc * 128: hc * 256 + cc * 128 + 128],
                          self.wp_h[:, 3584 + hc * 64: 3584 + hc * 64 + 40])
                         for hc in range(8)]
                self.mm(5, self.pap(5, 128, 40), pairs,
                        list(XHTb) + [self.wp_b[7]])
                bo = cc * 64
                lo = cc * 64
                self.v(lambda h, d=BMG[:, bo:bo + 40], p=self.pap(5, 128, 40),
                       bb=self.sm_h[:, 0:40]: h.tensor_tensor(d, p, bb, ALU.add),
                       [self.pb(5), self.sm_b[0], self.sm_b[1]], BMGb)
                self.sc(lambda h, d=LL[:, lo:lo + 16], s_=BMG[:, bo:bo + 16]:
                        h.activation(d, s_, AF.Sigmoid), BMGb, LLb)
                self.v(lambda h, d=LL[:, lo:lo + 16], r=self.c_rrep[:, 0:16]:
                       h.tensor_tensor(d, d, r, ALU.mult),
                       LLb + [self.c_rrep_b], LLb)
                self.v(lambda h, d=LL[:, lo:lo + 16]:
                       h.tensor_scalar(d, d, 0.9999, None, ALU.min), LLb, LLb)
                self.sc(lambda h, d=LL[:, lo:lo + 16]: h.activation(d, d, AF.Ln),
                        LLb, LLb)
                self.sc(lambda h, d=LL[:, lo + 32:lo + 40],
                        s_=BMG[:, bo + 32:bo + 40]:
                        h.activation(d, s_, AF.Sigmoid), BMGb, LLb)
                me = BMG[:, bo + 16:bo + 32:2]
                mo = BMG[:, bo + 17:bo + 32:2]
                self.v(lambda h, d=LL[:, lo + 56:lo + 64], a=me, b2=mo:
                       h.tensor_tensor(d, a, b2, ALU.subtract), BMGb, LLb)
                self.sc(lambda h, d=LL[:, lo + 40:lo + 55:2],
                        s_=LL[:, lo + 56:lo + 64]:
                        h.activation(d, s_, AF.Sigmoid), LLb, LLb)
                self.sc(lambda h, d=LL[:, lo + 41:lo + 56:2],
                        s_=LL[:, lo + 56:lo + 64]:
                        h.activation(d, s_, AF.Sigmoid, scale=-1.0), LLb, LLb)
            for cc in range(2):
                self.att0_chunk(g * 2 + cc, cc)

    def att0_chunk(self, c, cc):
        (XHT, XHTb, QT, QTb, KT, KTb, Y, Yb, YT, YTb, BMG, BMGb, LL, LLb,
         BOUT, BOUTb, LNG, LNGb, LNB, LNBb, FS, FSb, VO, VOb, DN, DNb,
         ST, STb) = self._att0_regs
        lo = cc * 64
        for fh in range(2):
            for sl in range(8):
                self.dma("gpsimd", self.wp_h[:, sl * 512:(sl + 1) * 512],
                         self.ins["wvT0"].ap()[fh:fh + 1, :,
                                               sl * 512:(sl + 1) * 512].squeeze(0),
                         [self.in_bufs["wvT0"]], [self.wp_b[sl]])
            for hc in range(8):
                self.mm(4, self.pap(4),
                        [(XHT[:, hc * 256 + cc * 128: hc * 256 + cc * 128 + 128],
                          self.wp_h[:, hc * 512:(hc + 1) * 512])],
                        list(XHTb) + [self.wp_b[hc]],
                        start=(hc == 0), stop=(hc == 7))
            for hd in range(4):
                hh = fh * 4 + hd
                self.sc(lambda h, d=VO[:, hh * 129:hh * 129 + 128],
                       p=self.psum_h[4][0:128, hd * 128:(hd + 1) * 128]:
                       h.copy(d, p), [self.pb(4)], VOb)
        llc = LL[:, lo:lo + 16]
        self.mm(5, self.pap(5, 128, 16), [(self.c_tri[:, :], llc)],
                [self.c_tri_b] + LLb)
        self.sc(lambda h, d=DN[:, 1024 + 16 * cc:1040 + 16 * cc],
                p=self.pap(5, 128, 16):
                h.copy(d, p), [self.pb(5)], DNb)
        self.att0_sc_ktr(cc, 0)
        for hd in range(8):
            if hd + 1 < 8:
                self.att0_sc_ktr(cc, hd + 1)
            self.att0_head(c, cc, hd)
        for hc in range(8):
            self.tr(6, self.pap(6, 128, 128), Y[:, hc * 128:(hc + 1) * 128], Yb)
            self.sc(lambda h, d=YT[:, hc * 128:(hc + 1) * 128],
                   p=self.pap(6, 128, 128): h.copy(d, p),
                   [self.pb(6)], [YTb[hc // 4]])
        for fh in range(2):
            for sl in range(8):
                self.dma("gpsimd", self.wp_h[:, sl * 512:(sl + 1) * 512],
                         self.ins["woutT0"].ap()[fh:fh + 1, :,
                                                 sl * 512:(sl + 1) * 512].squeeze(0),
                         [self.in_bufs["woutT0"]], [self.wp_b[sl]])
            for hc in range(8):
                self.mm(7, self.pap(7),
                        [(YT[:, hc * 128:(hc + 1) * 128],
                          self.wp_h[:, hc * 512:(hc + 1) * 512])],
                        list(YTb) + [self.wp_b[hc]],
                        start=(hc == 0), stop=(hc == 7))
            xh_ap = self.xhat_h[:, cc * 1024 + fh * 512: cc * 1024 + fh * 512 + 512]
            self.v(lambda h, d=FS[:, fh * 512:(fh + 1) * 512], p=self.pap(7),
                   b2=BOUT[:, fh * 512:(fh + 1) * 512]:
                   h.tensor_tensor(d, p, b2, ALU.add),
                   [self.pb(7)] + BOUTb, FSb)
            self.v(lambda h, d=FS[:, fh * 512:(fh + 1) * 512], a=xh_ap:
                   h.tensor_tensor(d, d, a, ALU.add),
                   FSb + [self.xhat_b[cc]], FSb)
        self.ln_generic(FS, FSb, FS, FSb, cc, Y, Yb)
        self.v(lambda h, d=FS, g2=LNG: h.tensor_tensor(d, d, g2, ALU.mult),
               FSb + LNGb, FSb)
        self.v(lambda h, d=FS, b2=LNB: h.tensor_tensor(d, d, b2, ALU.add),
               FSb + LNBb, FSb)
        x_ap = self.xh[:, c * 1024:(c + 1) * 1024]
        self.v(lambda h, x=x_ap, r=FS: h.tensor_tensor(x, x, r, ALU.add),
               FSb + [self.xb[c]], [self.xb[c]])

    def att0_sc_ktr(self, cc, hd):
        (XHT, XHTb, QT, QTb, KT, KTb, Y, Yb, YT, YTb, BMG, BMGb, LL, LLb,
         BOUT, BOUTb, LNG, LNGb, LNB, LNBb, FS, FSb, VO, VOb, DN, DNb,
         ST, STb) = self._att0_regs
        kc = KT[:, hd * 256 + cc * 128: hd * 256 + cc * 128 + 128]
        qc = QT[:, hd * 256 + cc * 128: hd * 256 + cc * 128 + 128]
        kb = [KTb[(hd * 256) // 512]]
        qb = [QTb[(hd * 256) // 512]]
        self.mm(hd % 2, self.pap(hd % 2, 128, 128), [(kc, qc)], kb + qb)
        self.tr(2 + hd % 2, self.pap(2 + hd % 2, 128, 128), kc, kb)

    def att0_head(self, c, cc, hd):
        (XHT, XHTb, QT, QTb, KT, KTb, Y, Yb, YT, YTb, BMG, BMGb, LL, LLb,
         BOUT, BOUTb, LNG, LNGb, LNB, LNBb, FS, FSb, VO, VOb, DN, DNb,
         ST, STb) = self._att0_regs
        lo = cc * 64
        qc = QT[:, hd * 256 + cc * 128: hd * 256 + cc * 128 + 128]
        qb = [QTb[(hd * 256) // 512]]
        voh = VO[:, hd * 129:(hd + 1) * 129]
        gcol = LL[:, lo + 32 + hd:lo + 33 + hd]
        bSC = hd % 2
        bKT = 2 + hd % 2
        b4 = 4 + hd % 2
        b6 = 6 + hd % 2
        EA2 = DN[:, 0:256]
        MT2 = DN[:, 256:512]
        QP2 = DN[:, 512:768]
        KP2 = DN[:, 768:1024]
        MT = [DN[:, 256:384], DN[:, 384:512]]
        QP = [DN[:, 512:640], DN[:, 640:768]]
        KP = [DN[:, 768:896], DN[:, 896:1024]]
        acol2 = DN[:, 1024 + 16 * cc + 2 * hd:1026 + 16 * cc + 2 * hd]
        cs2 = DN[:, 1056:1058]
        dn2 = DN[:, 1058:1060]
        f2 = DN[:, 1060:1062]

        def j3(ap2, n=128):
            return ap2.rearrange("p (j c) -> p j c", c=n)

        def jbc(ap_, n=128):
            return bass.AP(ap_.tensor, ap_.offset,
                           [list(ap_.ap[0]), [0, 2], [1, n]])

        def ibc(ap_, n=128):
            return bass.AP(ap_.tensor, ap_.offset,
                           [list(ap_.ap[0]), [1, 2], [0, n]])

        llpair = LL[:, lo + 2 * hd:lo + 2 * hd + 2]
        self.v(lambda h, d=j3(KP2), t=jbc(self.c_tri[:, :]), l=ibc(llpair):
               h.tensor_tensor(d, t, l, ALU.mult),
               [self.c_tri_b] + LLb, DNb)
        for j in range(2):
            self.mm(b4, self.psum_h[b4][0:128, j * 128:(j + 1) * 128],
                    [(self.c_allones[:, :], KP[j])],
                    [self.c_allones_b] + DNb)
        for j in range(2):
            self.mm(b6, self.psum_h[b6][0:128, j * 128:(j + 1) * 128],
                    [(self.c_allones[:, :], KP[j]),
                     (KP[j], self.c_negones[:, :])],
                    [self.c_allones_b, self.c_negones_b] + DNb)
        self.sc(lambda h, d=EA2, p=self.psum_h[b4][0:128, 0:256]:
                h.activation(d, p, AF.Exp), [self.pb(b4)], DNb)
        self.v(lambda h, d=j3(MT2), p=j3(self.psum_h[b6][0:128, 0:256]),
               m=jbc(self.c_maskst[:, :]):
               h.tensor_tensor(d, p, m, ALU.add),
               [self.pb(b6), self.c_maskst_b], DNb)
        self.sc(lambda h, d=MT2: h.activation(d, d, AF.Exp), DNb, DNb)
        self.v(lambda h, d=MT2, g2=gcol:
               h.tensor_scalar(d, d, g2, None, ALU.mult), DNb + LLb, DNb)
        self.v(lambda h, d=j3(MT2), p=jbc(self.pap(bSC, 128, 128)):
               h.tensor_tensor(d, d, p, ALU.mult), DNb + [self.pb(bSC)], DNb)
        self.v(lambda h, d=j3(QP2), q=jbc(qc), e=j3(EA2):
               h.tensor_tensor(d, q, e, ALU.mult), qb + DNb, DNb)
        for j in range(2):
            sthj = ST[:, (2 * hd + j) * 129:(2 * hd + j + 1) * 129]
            self.mm(b4, self.psum_h[b4][0:128, j * 129:j * 129 + 129],
                    [(MT[j], voh), (QP[j], sthj)], DNb + VOb + STb)
        self.sc(lambda h, d=cs2, a=acol2: h.activation(d, a, AF.Exp, scale=-1.0),
                DNb, DNb)
        EA0 = DN[:, 0:128]
        ea_last = bass.AP(EA0.tensor, EA0.offset + 127,
                          [list(EA0.ap[0]), [128, 2]])
        self.v(lambda h, d=cs2, e=ea_last: h.tensor_tensor(d, d, e, ALU.mult),
               DNb, DNb)
        self.v(lambda h, d=cs2, g2=gcol: h.tensor_scalar(d, d, g2, None, ALU.mult),
               DNb + LLb, DNb)
        self.v(lambda h, d=j3(KP2), p=jbc(self.pap(bKT, 128, 128)), cs=ibc(cs2):
               h.tensor_tensor(d, p, cs, ALU.mult),
               [self.pb(bKT)] + DNb, DNb)
        for j in range(2):
            self.mm(b6, self.psum_h[b6][0:128, j * 129:j * 129 + 129],
                    [(KP[j], voh)], DNb + VOb)
        for j in range(2):
            sthj = ST[:, (2 * hd + j) * 129:(2 * hd + j + 1) * 129]
            self.v(lambda h, s_=sthj, el=ea_last[:, j:j + 1],
                   p=self.psum_h[b6][0:128, j * 129:j * 129 + 129]:
                   h.scalar_tensor_tensor(s_, s_, el, p, ALU.mult, ALU.add),
                   STb + DNb + [self.pb(b6)], STb)
        pd = self.psum_h[b4][0:128, 128:129]
        pd3 = bass.AP(pd.tensor, pd.offset, [list(pd.ap[0]), [129, 2], [1, 1]])
        self.v(lambda h, d=dn2.rearrange("p (j c) -> p j c", c=1), p=pd3:
               h.tensor_scalar(d, p, 1e-6, None, ALU.add), [self.pb(b4)], DNb)
        self.v(lambda h, d=f2, s_=dn2: h.reciprocal(d, s_), DNb, DNb)
        self.v(lambda h, d=f2, w=LL[:, lo + 40 + 2 * hd:lo + 42 + 2 * hd]:
               h.tensor_tensor(d, d, w, ALU.mult), DNb + LLb, DNb)
        yh = Y[:, hd * 128:(hd + 1) * 128]
        self.v(lambda h, d=yh, p=self.psum_h[b4][0:128, 0:128], f=f2[:, 0:1]:
               h.tensor_scalar(d, p, f, None, ALU.mult),
               [self.pb(b4)] + DNb, Yb)
        self.v(lambda h, d=yh, p=self.psum_h[b4][0:128, 129:257], f=f2[:, 1:2]:
               h.scalar_tensor_tensor(d, p, f, d, ALU.mult, ALU.add),
               [self.pb(b4)] + DNb + Yb, Yb)

    def ffn1_weights(self):
        A = self.aview
        SRC, SRCb = A(0, 4096)
        T1, T1b = A(4096, 4096)
        DSTI, DSTIb = A(8192, 4096)
        STGS = [A(12288, 1408), A(13696, 1408)]
        for wname, wdst, wdb, nt_rows, wcols, smo, bat in (
                ("wg1_nat", self.iwgT_d, self.iwgT_db, 22, 1024, 64, 4),
                ("wu1_nat", self.iwuT_d, self.iwuT_db, 22, 1024, 96, 4),
                ("wd1_nat", self.iwdT_d, self.iwdT_db, 8, 2816, 128, 1)):
            src_t = self.ins[wname]
            smb = self.sm_b[smo // 32]
            for nt0 in range(0, nt_rows, bat):
                nb_ = min(bat, nt_rows - nt0)
                W = wcols * nb_
                if nb_ == 1:
                    self.dma("sync", SRC[:, 0:W],
                             src_t.ap()[nt0 * 128:(nt0 + 1) * 128, :],
                             [self.in_bufs[wname]], SRCb)
                else:
                    sap = src_t.ap()[nt0 * 128:(nt0 + nb_) * 128, :].rearrange(
                        "(a b) c -> b a c", a=nb_)
                    dap = SRC[:, 0:W].rearrange("p (a c) -> p a c", c=wcols)
                    self.dma("sync", dap, sap, [self.in_bufs[wname]], SRCb)
                sma = self.sm_h[:, smo + nt0:smo + nt0 + nb_]
                src3 = SRC[:, 0:W].rearrange("p (a c) -> p a c", c=wcols)
                self.v(lambda h, d=sma, s3=src3:
                       h.tensor_reduce(d.unsqueeze(-1), s3,
                                       AX.X, ALU.max, apply_absolute_value=True),
                       SRCb, [smb])
                self.v(lambda h, d=sma: h.tensor_scalar(
                    d, d, 1.0 / 127.0, None, ALU.mult), [smb], [smb])
                rc = self.sm_h[:, 160:160 + nb_]
                self.v(lambda h, d=rc, s_=sma: h.reciprocal(d, s_),
                       [smb], [self.sm_b[5]])
                rc_b = bass.AP(rc.tensor, rc.offset,
                               [list(rc.ap[0]), [1, nb_], [0, wcols]])
                t13 = T1[:, 0:W].rearrange("p (a c) -> p a c", c=wcols)
                self.v(lambda h, d=t13, s3=src3, r=rc_b:
                       h.tensor_tensor(d, s3, r, ALU.mult),
                       SRCb + [self.sm_b[5]], T1b)
                self.v(lambda h, d=T1[:, 0:W]:
                       h.tensor_scalar(d, d, -128.0, 127.0, ALU.max, ALU.min),
                       T1b, T1b)
                self.v(lambda h, d=SRC[:, 0:W].bitcast(I32), s_=T1[:, 0:W]:
                       h.tensor_copy(d, s_), T1b, SRCb)
                self.v(lambda h, d=DSTI[:, 0:W], s_=SRC[:, 0:W].bitcast(I32):
                       h.tensor_copy(d, s_), SRCb, DSTIb)
                nck = wcols // 128
                for bi in range(nb_):
                    nt = nt0 + bi
                    st_ap, st_b = STGS[nt % 2]
                    stB = st_ap.bitcast(BF16)
                    for ck in range(nck):
                        bank = 2 + (ck % 2)
                        self.tr(bank, self.pap(bank, 128, 128),
                                DSTI[:, bi * wcols + ck * 128:
                                     bi * wcols + ck * 128 + 128], DSTIb)
                        sg = stB[:, ck * 128:(ck + 1) * 128]
                        self.sc(lambda h, d=sg, p=self.pap(bank, 128, 128):
                               h.copy(d, p), [self.pb(bank)], st_b)
                    self.dma("gpsimd",
                             wdst.ap()[nt:nt + 1, :, :].squeeze(0),
                             stB[:, 0:nck * 128], st_b, [wdb])

    def global_absmax(self, am_ap, am_buf, gco):
        self.tr(5, self.pap(5, 1, 128), am_ap, [am_buf])
        g1 = self.sm_h[0:1, 176:177]
        self.v(lambda h, d=g1, p=self.pap(5, 1, 128):
               h.tensor_reduce(d, p, AX.X, ALU.max), [self.pb(5)], [self.sm_b[5]])
        cst = self.sm_h[0:1, 192:208]
        self.v(lambda h, d=cst: h.memset(d, 0.0), [], [self.sm_b[6]])
        self.v(lambda h, d=self.sm_h[0:1, 192:193], s_=g1:
               h.tensor_copy(d, s_), [self.sm_b[5]], [self.sm_b[6]])
        self.dma("gpsimd", self.cc_in_d.ap(), cst, [self.sm_b[6]], [self.cc_in_db])
        self.gp(lambda h: h.collective_compute(
            "AllReduce", mybir.AluOpType.max, replica_groups=[list(range(8))],
            ins=[self.cc_in_d.ap().opt()], outs=[self.cc_out_d.ap().opt()]),
            [self.cc_in_db], [self.cc_out_db], sem_only=True)
        self.dma("gpsimd", self.sm_h[0:1, 208:224], self.cc_out_d.ap(),
                 [self.cc_out_db], [self.sm_b[6]])
        self.mm(5, self.pap(5, 128, 1),
                [(self.c_ones1[:, :], self.sm_h[0:1, 208:209])],
                [self.c_ones1_b, self.sm_b[6]])
        gc = self.sm_h[:, gco:gco + 1]
        self.sc(lambda h, d=gc, p=self.pap(5, 128, 1): h.copy(d, p),
               [self.pb(5)], [self.sm_b[7]])
        return gc, self.sm_b[7]

    def scale_from_clip(self, clip_name, gx, gxb, sxo, rsxo):
        clip = self.sm_h[0:1, 226:227]
        self.dma("sync", clip, self.ins[clip_name].ap(),
                 [self.in_bufs[clip_name]], [self.sm_b[7]])
        sgc = self.sm_h[0:1, 227:228]
        self.sc(lambda h, d=sgc, s_=clip: h.activation(d, s_, AF.Sigmoid),
                [self.sm_b[7]], [self.sm_b[7]])
        self.mm(5, self.pap(5, 128, 1), [(self.c_ones1[:, :], sgc)],
                [self.c_ones1_b, self.sm_b[7]])
        sx = self.sm_h[:, sxo:sxo + 1]
        rsx = self.sm_h[:, rsxo:rsxo + 1]
        self.v(lambda h, d=sx, p=self.pap(5, 128, 1), g2=gx:
               h.tensor_tensor(d, p, g2, ALU.mult),
               [self.pb(5), gxb], [self.sm_b[7]])
        self.v(lambda h, d=sx: h.tensor_scalar(d, sx, 1.0 / 127.0, None, ALU.mult),
               [self.sm_b[7]], [self.sm_b[7]])
        self.v(lambda h, d=rsx, s_=sx: h.reciprocal(d, s_),
               [self.sm_b[7]], [self.sm_b[7]])
        return sx, rsx, self.sm_b[7]

    def quant_int(self, src_ap, src_bufs, dst_ap, dst_bufs, rs_ap, rs_bufs,
                  t_ap, t_bufs, u_ap, u_bufs):
        self.v(lambda h, d=t_ap, s_=src_ap, r=rs_ap:
               h.tensor_scalar(d, s_, r, None, ALU.mult),
               list(src_bufs) + list(rs_bufs), list(t_bufs))
        self.v(lambda h, d=t_ap:
               h.tensor_scalar(d, t_ap, -128.0, 127.0, ALU.max, ALU.min),
               list(t_bufs), list(t_bufs))
        self.v(lambda h, d=u_ap.bitcast(I32), s_=t_ap:
               h.tensor_copy(d, s_), list(t_bufs), list(u_bufs))
        self.v(lambda h, d=dst_ap, s_=u_ap.bitcast(I32):
               h.tensor_copy(d, s_), list(u_bufs), list(dst_bufs))

    def ffn1(self):
        if not self.l1w_done:
            self.ffn1_weights()
            self.l1w_done = True
        A = self.aview
        XHT32, XHT32b = A(0, 16384)
        IXQT, IXQTb = A(0, 8192)
        IHQT, IHQTb = A(8192, 5632)
        FS0, FS0b = A(16384, 512)
        FS1, FS1b = A(16896, 512)
        FS2, FS2b = A(17408, 512)
        FS3, FS3b = A(17920, 512)
        SQ, SQb = A(18432, 1024)
        AMX = self.sm_h[:, 384:390]
        AMXb = [self.sm_b[12]]
        IXQTB = IXQT.bitcast(BF16)
        IHQTB = IHQT.bitcast(BF16)
        wpB = self.wp_h[:, :].bitcast(BF16)

        self.v(lambda h, d=AMX[:, 0:4]: h.memset(d, 0.0), [], AMXb)
        for g in range(4):
            tis = list(range(g * 4, g * 4 + 4))
            dsts = [(self.xhat_h[:, i * 1024:(i + 1) * 1024], [self.xhat_b[i]])
                    for i in range(4)]
            self.ln_group(tis, dsts, SQ, SQb)
            self.v(lambda h, d=AMX[:, 2:3], s_=self.xhat_h[:, 0:4096]:
                   h.tensor_reduce(
                d.unsqueeze(-1), s_.rearrange("p (a b) -> p a b", b=4096),
                AX.X, ALU.max, apply_absolute_value=True),
                list(self.xhat_b), AMXb)
            self.v(lambda h, d=AMX[:, 0:1], s_=AMX[:, 2:3]:
                   h.tensor_tensor(d, d, s_, ALU.max), AMXb, AMXb)
            for i, ti in enumerate(tis):
                xh_ap = self.xhat_h[:, i * 1024:(i + 1) * 1024]
                xh_buf = self.xhat_b[i]
                for hc in range(8):
                    bank = 6 + (hc % 2)
                    self.tr(bank, self.pap(bank, 128, 128),
                            xh_ap[:, hc * 128:(hc + 1) * 128], [xh_buf])
                    dcol = hc * 2048 + ti * 128
                    self.sc(lambda h, d=XHT32[:, dcol:dcol + 128],
                           p=self.pap(bank, 128, 128): h.copy(d, p),
                           [self.pb(bank)], [XHT32b[dcol // 512]])
        gx, gxb = self.global_absmax(AMX[:, 0:1], AMXb[0], 224)
        sx, rsx, sxb = self.scale_from_clip("clip_x", gx, gxb, 228, 229)
        for smo, dsto in ((64, 256), (96, 288)):
            self.v(lambda h, d=self.sm_h[:, dsto:dsto + 22],
                   s_=self.sm_h[:, smo:smo + 22], c2=sx:
                   h.tensor_scalar(d, s_, c2, None, ALU.mult),
                   [self.sm_b[smo // 32], sxb], [self.sm_b[dsto // 32]])
        for hc in range(8):
            for tq in range(4):
                co = hc * 2048 + tq * 512
                srcap = XHT32[:, co:co + 512]
                srcb = [XHT32b[co // 512]]
                dst = IXQTB[:, co:co + 512]
                dbufs = [IXQTb[(hc * 1024 + tq * 256) // 512]]
                fa, fab = (FS0, FS0b) if (hc * 4 + tq) % 2 == 0 else (FS2, FS2b)
                fb_, fbb = (FS1, FS1b) if (hc * 4 + tq) % 2 == 0 else (FS3, FS3b)
                self.quant_int(srcap, srcb, dst, dbufs, rsx, [sxb],
                               fa[:, 0:512], fab, fb_[:, 0:512], fbb)
        self.v(lambda h, d=AMX[:, 1:2]: h.memset(d, 0.0), [], AMXb)
        for nt in range(22):
            par = nt % 2
            goff = par * 1024
            uoff = 2048 + par * 1024
            for wd, wdb, off, slot in ((self.iwgT_d, self.iwgT_db, goff, par),
                                       (self.iwuT_d, self.iwuT_db, uoff, 2 + par)):
                srca = wd.ap()[nt:nt + 1, :, :].squeeze(0)
                self.dma("sync", wpB[:, off:off + 1024], srca,
                         [wdb], [self.wp_b[slot]])
            sg = self.sm_h[:, 256 + nt:257 + nt]
            su = self.sm_h[:, 288 + nt:289 + nt]
            for tq in range(4):
                fpar = tq % 2
                Fa, Fab = (FS0, FS0b) if fpar == 0 else (FS2, FS2b)
                Fb, Fbb = (FS1, FS1b) if fpar == 0 else (FS3, FS3b)
                gpairs = [(wpB[:, goff + hc * 128:goff + hc * 128 + 128],
                           IXQTB[:, hc * 2048 + tq * 512:hc * 2048 + tq * 512 + 512])
                          for hc in range(8)]
                upairs = [(wpB[:, uoff + hc * 128:uoff + hc * 128 + 128],
                           IXQTB[:, hc * 2048 + tq * 512:hc * 2048 + tq * 512 + 512])
                          for hc in range(8)]
                self.mm(fpar, self.pap(fpar), gpairs,
                        [self.wp_b[par]] + list(IXQTb))
                self.v(lambda h, d=Fa[:, 0:512], p=self.pap(fpar), s_=sg:
                       h.tensor_scalar(d, p, s_, None, ALU.mult),
                       [self.pb(fpar), self.sm_b[8]], Fab)
                self.sc(lambda h, d=Fb[:, 0:512], s_=Fa[:, 0:512]:
                        h.activation(d, s_, AF.Silu), Fab, Fbb)
                self.mm(2 + fpar, self.pap(2 + fpar), upairs,
                        [self.wp_b[2 + par]] + list(IXQTb))
                self.v(lambda h, d=Fa[:, 0:512], p=self.pap(2 + fpar), s_=su:
                       h.tensor_scalar(d, p, s_, None, ALU.mult),
                       [self.pb(2 + fpar), self.sm_b[9]], Fab)
                self.v(lambda h, d=Fa[:, 0:512], s_=Fb[:, 0:512]:
                       h.tensor_tensor(d, d, s_, ALU.mult), Fab + Fbb, Fab)
                self.v(lambda h, d=AMX[:, 2 + fpar:3 + fpar], s_=Fa[:, 0:512]:
                       h.tensor_reduce(d.unsqueeze(-1),
                                       s_.rearrange("p (a b) -> p a b", b=512),
                                       AX.X, ALU.max, apply_absolute_value=True),
                       Fab, AMXb)
                self.v(lambda h, d=AMX[:, 1:2], s_=AMX[:, 2 + fpar:3 + fpar]:
                       h.tensor_tensor(d, d, s_, ALU.max), AMXb, AMXb)
                self.dma("gpsimd",
                         self.hsp_d.ap()[nt * 128:(nt + 1) * 128,
                                         tq * 512:(tq + 1) * 512],
                         Fa[:, 0:512], Fab, [self.hsp_db])
        gh, ghb = self.global_absmax(AMX[:, 1:2], AMXb[0], 225)
        sh, rsh, shb = self.scale_from_clip("clip_d", gh, ghb, 230, 231)
        self.v(lambda h, d=self.sm_h[:, 320:328], s_=self.sm_h[:, 128:136], c2=sh:
               h.tensor_scalar(d, s_, c2, None, ALU.mult),
               [self.sm_b[4], shb], [self.sm_b[10]])
        for tq in range(4):
            for ic in range(22):
                fpar = ic % 2
                Fa, Fab = (FS0, FS0b) if fpar == 0 else (FS2, FS2b)
                Fb, Fbb = (FS1, FS1b) if fpar == 0 else (FS3, FS3b)
                self.dma("sync", Fa[:, 0:512],
                         self.hsp_d.ap()[ic * 128:(ic + 1) * 128,
                                         tq * 512:(tq + 1) * 512],
                         [self.hsp_db], Fab)
                dst = IHQTB[:, ic * 512:(ic + 1) * 512]
                dbufs = [IHQTb[(ic * 256) // 512]]
                self.quant_int(Fa[:, 0:512], Fab, dst, dbufs, rsh, [shb],
                               Fb[:, 0:512], Fbb, SQ[:, 0:512], SQb)
            for ft in range(8):
                par = ft % 2
                doff = par * 4096
                wslots = [self.wp_b[par * 4], self.wp_b[par * 4 + 1],
                          self.wp_b[par * 4 + 2]]
                srca = self.iwdT_d.ap()[ft:ft + 1, :, :].squeeze(0)
                self.dma("sync", wpB[:, doff:doff + 2816], srca,
                         [self.iwdT_db], wslots)
                pairs = [(wpB[:, doff + ic * 128:doff + ic * 128 + 128],
                          IHQTB[:, ic * 512:(ic + 1) * 512])
                         for ic in range(22)]
                self.mm(4 + par, self.pap(4 + par), pairs, wslots + list(IHQTb))
                scc = self.sm_h[:, 320 + ft:321 + ft]
                Fa, Fab = (FS0, FS0b) if par == 0 else (FS2, FS2b)
                self.v(lambda h, d=Fa[:, 0:512], p=self.pap(4 + par), s_=scc:
                       h.tensor_scalar(d, p, s_, None, ALU.mult),
                       [self.pb(4 + par), self.sm_b[10]], Fab)
                for q4 in range(4):
                    bank = 6 + (q4 % 2)
                    self.tr(bank, self.pap(bank, 128, 128),
                            Fa[:, q4 * 128:(q4 + 1) * 128], Fab)
                    ti = tq * 4 + q4
                    x_ap = self.xh[:, ti * 1024 + ft * 128:
                                   ti * 1024 + ft * 128 + 128]
                    self.v(lambda h, x=x_ap, p=self.pap(bank, 128, 128):
                           h.scalar_tensor_tensor(x, p, 0.5, x, ALU.mult, ALU.add),
                           [self.pb(bank), self.xb[ti]], [self.xb[ti]])

    def att1_prep(self):
        A = self.aview
        OH, OHb = A(0, 4096)
        STAGE, STAGEb = A(4096, 4096)
        self.dma("sync", OH[0:32, :], self.ins["c_onehot"].ap(),
                 [self.in_bufs["c_onehot"]], OHb)
        tbl = self.sm_h[0:32, 336:352]
        self.dma("sync", tbl, self.ins["tbl"].ap(),
                 [self.in_bufs["tbl"]], [self.sm_b[10]])
        for jc in range(8):
            self.mm(4, self.pap(4, 16, 512),
                    [(tbl, OH[0:32, jc * 512:(jc + 1) * 512])],
                    [self.sm_b[10]] + OHb)
            self.sc(lambda h, d=STAGE[0:16, jc * 512:(jc + 1) * 512],
                   p=self.pap(4, 16, 512): h.copy(d, p),
                   [self.pb(4)], STAGEb)
        self.dma("gpsimd", self.wr_d.ap(), STAGE[0:16, 0:4096], STAGEb,
                 [self.wr_db])
        self.mm(5, self.pap(5, 128, 16), [(self.c_sel15[:, :], tbl)],
                [self.c_sel15_b, self.sm_b[10]])
        self.sc(lambda h, d=self.sm_h[:, 352:368], p=self.pap(5, 128, 16):
               h.copy(d, p), [self.pb(5)], [self.sm_b[11]])
        self.mm(5, self.pap(5, 128, 16), [(self.c_sel31[:, :], tbl)],
                [self.c_sel31_b, self.sm_b[10]])
        self.sc(lambda h, d=self.sm_h[:, 368:384], p=self.pap(5, 128, 16):
               h.copy(d, p), [self.pb(5)], [self.sm_b[11]])

    def att1(self):
        lim = _ATT1_LIMIT
        if not self.att1_prep_done:
            self.att1_prep()
            self.att1_prep_done = True
        if lim < 1:
            return
        A = self.aview
        XHT32, XHT32b = A(0, 16384)
        SQ, SQb = A(16384, 1024)
        FS0, FS0b = A(17408, 512)
        FS1, FS1b = A(17920, 512)
        FS2, FS2b = A(18432, 512)
        KTPs = [A(0, 2048), A(14208, 2048)]
        VPs = [A(2048, 2176), A(12032, 2176)]
        QZs = [[A(4224, 512), A(5760, 512)],
               [A(16384, 512), A(16896, 512)]]
        PT = [A(4736, 512), A(5248, 512)]
        XTo = [A(8320, 512), A(8832, 512)]
        WWC, WWCb = A(9728, 2304)

        for g in range(4):
            tis = list(range(g * 4, g * 4 + 4))
            dsts = [(self.xhat_h[:, i * 1024:(i + 1) * 1024], [self.xhat_b[i]])
                    for i in range(4)]
            self.ln_group(tis, dsts, SQ, SQb)
            for i, ti in enumerate(tis):
                xh_ap = self.xhat_h[:, i * 1024:(i + 1) * 1024]
                xh_buf = self.xhat_b[i]
                for hc in range(8):
                    bank = 6 + (hc % 2)
                    self.tr(bank, self.pap(bank, 128, 128),
                            xh_ap[:, hc * 128:(hc + 1) * 128], [xh_buf])
                    dcol = hc * 2048 + ti * 128
                    self.sc(lambda h, d=XHT32[:, dcol:dcol + 128],
                           p=self.pap(bank, 128, 128): h.copy(d, p),
                           [self.pb(bank)], [XHT32b[dcol // 512]])
        if lim < 2:
            return
        for wname, dst_d, dst_db in (("wqT1", self.qsp_d, self.qsp_db),
                                     ("wkT1", self.ksp_d, self.ksp_db)):
            wt = self.ins[wname]
            wdb_ = self.in_bufs[wname]
            for half in range(2):
                for sl in range(8):
                    self.dma("sync", self.wp_h[:, sl * 512:(sl + 1) * 512],
                             wt.ap()[half:half + 1, :,
                                     sl * 512:(sl + 1) * 512].squeeze(0),
                             [wdb_], [self.wp_b[sl]])
                for qc in range(4):
                    for f4 in range(4):
                        ft = half * 4 + f4
                        pairs = [(self.wp_h[:, (hc * 4 + f4) * 128:
                                            (hc * 4 + f4) * 128 + 128],
                                  XHT32[:, hc * 2048 + qc * 512:
                                        hc * 2048 + qc * 512 + 512])
                                 for hc in range(8)]
                        self.mm(f4, self.pap(f4), pairs,
                                list(self.wp_b) + list(XHT32b))
                        Fa, Fab = (FS0, FS0b) if f4 % 2 == 0 else (FS1, FS1b)
                        self.sc(lambda h, d=Fa[:, 0:512], p=self.pap(f4):
                               h.copy(d, p), [self.pb(f4)], Fab)
                        self.dma("sync",
                                 dst_d.ap()[ft * 128:(ft + 1) * 128,
                                            qc * 512:(qc + 1) * 512],
                                 Fa[:, 0:512], Fab, [dst_db])
        for fh in range(2):
            for sl in range(8):
                self.dma("sync", self.wp_h[:, sl * 512:(sl + 1) * 512],
                         self.ins["wvT1"].ap()[fh:fh + 1, :,
                                               sl * 512:(sl + 1) * 512].squeeze(0),
                         [self.in_bufs["wvT1"]], [self.wp_b[sl]])
            for ti in range(16):
                pairs = [(XHT32[:, hc * 2048 + ti * 128:
                                hc * 2048 + ti * 128 + 128],
                          self.wp_h[:, hc * 512:(hc + 1) * 512])
                         for hc in range(8)]
                bank = fh * 4 + (ti % 4)
                self.mm(bank, self.pap(bank), pairs,
                        list(self.wp_b) + list(XHT32b))
                Fa, Fab = (FS0, FS0b) if ti % 2 == 0 else (FS1, FS1b)
                self.sc(lambda h, d=Fa[:, 0:512], p=self.pap(bank):
                       h.copy(d, p), [self.pb(bank)], Fab)
                self.dma("sync",
                         self.vsp_d.ap()[ti * 128:(ti + 1) * 128,
                                         fh * 512:(fh + 1) * 512],
                         Fa[:, 0:512], Fab, [self.vsp_db])
        if lim < 3:
            return
        for qzp in QZs:
            self.v(lambda h, a=qzp[0][0][64:128, :]: h.memset(a, 0.0), [], qzp[0][1])
            self.v(lambda h, a=qzp[1][0][0:64, :]: h.memset(a, 0.0), [], qzp[1][1])
        for p in range(8):
            KTP, KTPb = KTPs[p % 2]
            VP, VPb = VPs[p % 2]
            self.dma("gpsimd", KTP, self.ksp_d.ap()[p * 128:(p + 1) * 128, :],
                     [self.ksp_db], KTPb)
            for hh in range(2):
                hglob = 2 * p + hh
                src = bass.AP(self.wr_d.ap().tensor, hglob * 4096 + 1408,
                              [[1, 128], [1, 1152]])
                self.dma("gpsimd", WWC[:, hh * 1152:(hh + 1) * 1152], src,
                         [self.wr_db], WWCb)
            for tk in range(16):
                src = self.vsp_d.ap()[tk * 128:(tk + 1) * 128,
                                      p * 128:(p + 1) * 128]
                src2 = src.rearrange("p (a b) -> p a b", b=64)
                dst = VP[:, tk * 130:tk * 130 + 130]
                dst2 = bass.AP(dst.tensor, dst.offset,
                               [list(dst.ap[0]), [65, 2], [1, 64]])
                self.dma("gpsimd", dst2, src2, [self.vsp_db], VPb)
            vones = bass.AP(VP.tensor, VP.offset + 64,
                            [list(VP.ap[0]), [130, 16], [65, 2]])
            self.v(lambda h, a=vones: h.memset(a, 1.0), VPb, VPb)
            for qc in range(4):
                QZ = QZs[qc % 2]
                self.dma("sync", QZ[0][0][0:64, :],
                         self.qsp_d.ap()[p * 128:p * 128 + 64,
                                         qc * 512:(qc + 1) * 512],
                         [self.qsp_db], QZ[0][1])
                self.dma("sync", QZ[1][0][64:128, :],
                         self.qsp_d.ap()[p * 128 + 64:p * 128 + 128,
                                         qc * 512:(qc + 1) * 512],
                         [self.qsp_db], QZ[1][1])
                for hh in range(2):
                    hglob = 2 * p + hh
                    for kt in range(16):
                        band = (4 * qc - 1 <= kt <= 4 * qc + 4)
                        sb_ = kt % 2
                        pairs = [(KTP[:, kt * 128:(kt + 1) * 128],
                                  QZ[hh][0][:, :])]
                        rds = KTPb + QZ[hh][1]
                        if band:
                            boff = (1920 - 128 * kt + 512 * qc) - 1408
                            ww_ap = WWC[:, hh * 1152 + boff:
                                        hh * 1152 + boff + 512]
                            pairs = pairs + [(self.c_rev[:, :], ww_ap)]
                            rds = rds + [self.c_rev_b] + WWCb
                        self.mm(sb_, self.pap(sb_), pairs, rds)
                        pt_ap, pt_b = PT[sb_]
                        if band:
                            self.sc(lambda h, d=pt_ap, ps=self.pap(sb_):
                                    h.activation(d, ps, AF.Exp, scale=0.125),
                                    [self.pb(sb_)], pt_b)
                        else:
                            bco = 352 if kt <= 4 * qc - 2 else 368
                            bias = self.sm_h[:, bco + hglob:bco + hglob + 1]
                            self.sc(lambda h, d=pt_ap, ps=self.pap(sb_), b2=bias:
                                    h.activation(d, ps, AF.Exp, scale=0.125,
                                                 bias=b2),
                                    [self.pb(sb_), self.sm_b[11]], pt_b)
                        self.mm(2, self.pap(2, 65, 512),
                                [(VP[:, kt * 130 + hh * 65:
                                     kt * 130 + hh * 65 + 65], pt_ap)],
                                VPb + pt_b, start=(kt == 0), stop=(kt == 15))
                    self.v(lambda h, d=FS0[0:1, 0:512],
                           p=self.psum_h[2][64:65, 0:512]:
                           h.reciprocal(d, p), [self.pb(2)], FS0b)
                    self.mm(3, self.pap(3, 64, 512),
                            [(self.c_ones1[0:1, 0:64], FS0[0:1, 0:512])],
                            [self.c_ones1_b] + FS0b)
                    self.sc(lambda h, d=FS1[0:64, 0:512], p=self.pap(3, 64, 512):
                           h.copy(d, p), [self.pb(3)], FS1b)
                    self.v(lambda h, d=FS2[0:64, 0:512],
                           p=self.psum_h[2][0:64, 0:512], m=FS1[0:64, 0:512]:
                           h.tensor_tensor(d, p, m, ALU.mult),
                           [self.pb(2)] + FS1b, FS2b)
                    self.dma("sync",
                             self.ysp_d.ap()[hglob * 64:(hglob + 1) * 64,
                                             qc * 512:(qc + 1) * 512],
                             FS2[0:64, 0:512], FS2b, [self.ysp_db])
        if lim < 4:
            return
        for fh in range(2):
            for sl in range(8):
                self.dma("gpsimd", self.wp_h[:, sl * 512:(sl + 1) * 512],
                         self.ins["woT1"].ap()[fh:fh + 1, :,
                                               sl * 512:(sl + 1) * 512].squeeze(0),
                         [self.in_bufs["woT1"]], [self.wp_b[sl]])
            for tg in range(4):
                for hc in range(8):
                    xt_ap, xt_b = XTo[hc % 2]
                    self.dma("sync", xt_ap[:, 0:512],
                             self.ysp_d.ap()[hc * 128:(hc + 1) * 128,
                                             tg * 512:(tg + 1) * 512],
                             [self.ysp_db], xt_b)
                    for q4 in range(4):
                        self.mm(fh * 4 + q4, self.pap(fh * 4 + q4),
                                [(xt_ap[:, q4 * 128:(q4 + 1) * 128],
                                  self.wp_h[:, hc * 512:(hc + 1) * 512])],
                                xt_b + [self.wp_b[hc]],
                                start=(hc == 0), stop=(hc == 7))
                for q4 in range(4):
                    ti = tg * 4 + q4
                    x_ap = self.xh[:, ti * 1024 + fh * 512:
                                   ti * 1024 + fh * 512 + 512]
                    self.v(lambda h, x=x_ap, p=self.pap(fh * 4 + q4):
                           h.tensor_tensor(x, x, p, ALU.add),
                           [self.pb(fh * 4 + q4), self.xb[ti]], [self.xb[ti]])


def kernel(**inputs):
    cons = host_constants()
    f32 = np.float32

    def tr(a):
        return np.ascontiguousarray(np.asarray(a, f32).T)

    def rep(a):
        return np.ascontiguousarray(np.tile(np.asarray(a, f32)[None, :], (128, 1)))

    shared = dict(cons)
    shared["tbl"] = np.asarray(inputs["L1_attn_rel_bias_table"], f32)
    shared["wg0_nat"] = np.asarray(inputs["L0_ffn_w_gate"], f32)
    shared["wu0_nat"] = np.asarray(inputs["L0_ffn_w_up"], f32)
    shared["wd0_nat"] = np.asarray(inputs["L0_ffn_w_down"], f32)
    def ftile(a):
        return np.ascontiguousarray(
            a.reshape(8, 128, 8, 128).transpose(2, 1, 0, 3).reshape(8, 128, 1024))

    def pool2a(a):
        return np.ascontiguousarray(
            a.reshape(8, 128, 2, 512).transpose(2, 1, 0, 3).reshape(2, 128, 4096))

    shared["wqT0"] = ftile(tr(inputs["L0_attn_Wq"]))
    shared["wkT0"] = ftile(tr(inputs["L0_attn_Wk"]))
    shared["wvT0"] = pool2a(tr(inputs["L0_attn_Wv"]))
    shared["woutT0"] = pool2a(tr(inputs["L0_attn_Wout"]))
    shared["wbmgT"] = np.ascontiguousarray(np.concatenate(
        [np.asarray(inputs["L0_attn_Wbeta"], f32).T,
         np.asarray(inputs["L0_attn_Wmix"], f32).T,
         np.asarray(inputs["L0_attn_Wgate"], f32).T], axis=1))
    shared["bmg_bias"] = np.ascontiguousarray(np.tile(np.concatenate(
        [np.asarray(inputs["L0_attn_bbeta"], f32),
         np.asarray(inputs["L0_attn_bmix"], f32),
         np.asarray(inputs["L0_attn_bgate"], f32)])[None, :], (128, 1)))
    shared["bout_rep"] = rep(inputs["L0_attn_bout"])
    shared["lng_rep"] = rep(inputs["L0_attn_ln_g"])
    shared["lnb_rep"] = rep(inputs["L0_attn_ln_b"])
    def pool2(a):
        return np.ascontiguousarray(
            a.reshape(8, 128, 2, 512).transpose(2, 1, 0, 3).reshape(2, 128, 4096))

    shared["wqT1"] = pool2(tr(inputs["L1_attn_w_q"]))
    shared["wkT1"] = pool2(tr(inputs["L1_attn_w_k"]))
    shared["wvT1"] = pool2(tr(inputs["L1_attn_w_v"]))
    shared["woT1"] = pool2(tr(inputs["L1_attn_w_o"]))
    shared["wg1_nat"] = np.asarray(inputs["L1_ffn_w_gate"], f32)
    shared["wu1_nat"] = np.asarray(inputs["L1_ffn_w_up"], f32)
    shared["wd1_nat"] = np.asarray(inputs["L1_ffn_w_down"], f32)
    shared["clip_x"] = np.asarray(inputs["L1_ffn_clip_x"], f32).reshape(1, 1)
    shared["clip_d"] = np.asarray(inputs["L1_ffn_clip_down"], f32).reshape(1, 1)

    x = np.asarray(inputs["L0_attn_x"], f32)
    in_desc = [("x_in", (T, D), F32)]
    for name, arr in shared.items():
        in_desc.append((name, arr.shape, F32))

    nc = bass.Bass("TRN2", target_bir_lowering=False, debug=False)
    k = K(nc, _STAGE)
    k.build(in_desc)

    in_maps = []
    for c in range(8):
        m = dict(shared)
        m["x_in"] = np.ascontiguousarray(x[c])
        in_maps.append(m)
    res = run_bass_kernel_spmd(nc, in_maps, core_ids=list(range(8)))
    out = np.stack([np.asarray(res.results[c]["out"]) for c in range(8)], axis=0)
    return out.astype(np.float32)
